# version 07
# baseline (speedup 1.0000x reference)
import sys
sys.path.insert(0, '/opt/trn_rl_repo')
import math
import numpy as np
import ml_dtypes
import concourse.bass as bass
import concourse.mybir as mybir
from concourse.bass_utils import run_bass_kernel_spmd



F32 = mybir.dt.float32
BF16 = mybir.dt.bfloat16
AF = mybir.ActivationFunctionType
ALU = mybir.AluOpType
AX = mybir.AxisListType

D = 2048
T = 4096
NT = T // 128
NC = D // 128


class Ctx:

    def __init__(self, nc, block, sem_names):
        self.nc = nc
        self.block = block
        self.h = {}
        self.n = {}
        self.marks = {}
        self.scope = "" 
        self._cm = []
        for nm in sem_names:
            cm = nc.semaphore(nm)
            self.h[nm] = cm.__enter__()
            self._cm.append(cm)
            self.n[nm] = 0

    def inc(self, instr, sem, amt=1):
        self.n[sem] += amt
        instr.then_inc(self.h[sem], amt)
        return self.n[sem]

    def inc_cc(self, instr, sem):
        self.n[sem] += 1
        instr.then_inc(self.h[sem])
        return self.n[sem]

    def mark(self, key, sem=None, val=None):
        key = (self.scope,) + tuple(key)
        assert key not in self.marks, key
        self.marks[key] = (sem, val)

    def wait(self, eng, key):
        key = (self.scope,) + tuple(key)
        sem, val = self.marks[key]
        eng.wait_ge(self.h[sem], val)

    def ckpt(self, eng, dma_sem, ck_sem, key):
        eng.wait_ge(self.h[dma_sem], self.n[dma_sem])
        eng.sem_inc(self.h[ck_sem], 1)
        self.n[ck_sem] += 1
        self.mark(key, ck_sem, self.n[ck_sem])

    def wait_maybe(self, eng, key):
        key = (self.scope,) + tuple(key)
        if key in self.marks:
            sem, val = self.marks[key]
            eng.wait_ge(self.h[sem], val)


def alloc_sbuf(nc, stack, name, shape, dtype):
    cm = nc.sbuf_tensor(name, shape, dtype)
    t = cm.__enter__()
    stack.append(cm)
    return t


def alloc_psum(nc, stack, name, shape, dtype):
    cm = nc.psum_tensor(name, shape, dtype)
    t = cm.__enter__()
    stack.append(cm)
    return t



NB = 6


def emit_stage1(ctx, ap, stack, tag="s1"):
    ctx.scope = tag
    nc, block = ctx.nc, ctx.block
    sb = lambda n, s, d: alloc_sbuf(nc, stack, f"{tag}_{n}", s, d)
    ps = lambda n, s, d: alloc_psum(nc, stack, f"{tag}_{n}", s, d)

    x_tm, xn_tm = ap["x_tm"], ap["xn_tm"]
    out_fm = {"xn": ap["xn_fm"], "h1": ap["h1_fm"], "h2": ap["h2_fm"]}

    xt = [sb(f"xt{j}", [128, D], F32) for j in range(2)]
    xn = [sb(f"xn{j}", [128, D], F32) for j in range(4)]
    xnb = [sb(f"xnb{j}", [128, D], BF16) for j in range(NB)]
    yb = [sb(f"yb{j}", [128, D], BF16) for j in range(NB)]
    sq = sb("sq", [128, D], F32)
    lnw = sb("lnw", [128, 32], F32)
    idb = sb("idb", [128, 128], BF16)
    va = [sb(f"va{j}", [128, 1], F32) for j in range(2)]
    vm = [sb(f"vm{j}", [128, 1], F32) for j in range(2)]
    vs = [sb(f"vs{j}", [128, 1], F32) for j in range(2)]
    vb = [sb(f"vb{j}", [128, 1], F32) for j in range(2)]
    vs2 = [sb(f"vs2{j}", [128, 1], F32) for j in range(2)]
    vb2 = [sb(f"vb2{j}", [128, 1], F32) for j in range(2)]
    vc = sb("vc", [128, 1], F32)
    vc2 = sb("vc2", [128, 1], F32)
    veps5 = sb("veps5", [128, 1], F32)
    veps6 = sb("veps6", [128, 1], F32)
    dmy = [sb(f"dmy{j}", [128, 32], F32) for j in range(2)]
    ex = {s: sb(f"ex_{s}", [128, NC * 512], BF16) for s in ("xn", "h1", "h2")}
    psx = [ps(f"psx{j}", [128, 512], BF16) for j in range(2)]
    psy = [ps(f"psy{j}", [128, 512], BF16) for j in range(2)]

    sepv = lambda eng: eng.tensor_copy(dmy[1][:], dmy[0][:])
    sepa = lambda eng: eng.activation(out=dmy[1][:], in_=dmy[0][:], func=AF.Copy)

    ld, c, a, trn, evx, evy, st = ("ld", "c", "a", "trn", "evx", "evy", "st")
    ldc, stc = "ldc", "stc"
    M = ctx.mark
    NG = NT // 4

    @block.sync
    def _(sync):
        ctx.inc(sync.dma_start(out=lnw[:], in_=ap["lnw"]), ld, 16)
        ctx.inc(sync.dma_start(out=idb[:], in_=ap["identb"]), ld, 16)
        ctx.inc(sync.dma_start(out=xt[0][:], in_=x_tm[0:128, :]), ld, 16)
        ctx.ckpt(sync, ld, ldc, ("ld_tile", 0))

    @block.vector
    def _(vector):
        vector.memset(veps5[:], 1e-5)
        vector.memset(veps6[:], 1e-6)
        vector.memset(dmy[0][:], 0.0)

    for t in range(NT + 8):
        g = (t - 6) // 4 if (t - 6) % 4 == 0 and 0 <= (t - 6) // 4 < NG else None

        if g is not None:
            if g > 0:
                @block.gpsimd
                def _(gpsimd, g=g):
                    ctx.ckpt(gpsimd, st, stc, ("st_xn", g - 1, 0))
                    for ch2 in range(NC):
                        if ch2 > 0:
                            M(("st_xn", g - 1, ch2), stc, ctx.n[stc])
                        M(("st_h2", g - 1, ch2), stc, ctx.n[stc])
            for ch in range(NC):
                @block.tensor
                def _(tensor, g=g, ch=ch):
                    if ch == 0:
                        ctx.wait(tensor, ("s2", 4 * g + 3))
                        ctx.wait(tensor, ("d3", 4 * g + 3))
                    for strm, src, bank in (("xn", xnb, psx), ("y", yb, psy)):
                        key = ("ev_" + strm, g, ch - 2) if ch >= 2 else (("ev_" + strm, g - 1, NC - 2 + ch) if g > 0 else None)
                        if key is not None:
                            ctx.wait(tensor, key)
                        p = bank[ch % 2]
                        i = None
                        for j in range(4):
                            i = nc.tensor.transpose(p[:, 128 * j:128 * (j + 1)],
                                                    src[(4 * g + j) % NB][:, 128 * ch:128 * (ch + 1)],
                                                    idb[:])
                        v = ctx.inc(i, trn, 1)
                        M(("trn", g, ch, strm), trn, v)

                @block.vector
                def _(vector, g=g, ch=ch):
                    ctx.wait(vector, ("trn", g, ch, "xn"))
                    if g > 0:
                        ctx.wait(vector, ("st_xn", g - 1, ch))
                    i = vector.tensor_copy(ex["xn"][:, 512 * ch:512 * (ch + 1)], psx[ch % 2][:])
                    v = ctx.inc(i, evx, 1)
                    M(("ev_xn", g, ch), evx, v)

                @block.scalar
                def _(scalar, g=g, ch=ch):
                    ctx.wait(scalar, ("trn", g, ch, "y"))
                    if g > 0:
                        ctx.wait(scalar, ("st_h2", g - 1, ch))
                    scalar.activation(out=ex["h1"][:, 512 * ch:512 * (ch + 1)], in_=psy[ch % 2][:],
                                      func=AF.Copy, scale=lnw[:, ch:ch + 1])
                    i = scalar.activation(out=ex["h2"][:, 512 * ch:512 * (ch + 1)], in_=psy[ch % 2][:],
                                          func=AF.Copy, scale=lnw[:, 16 + ch:16 + ch + 1])
                    v = ctx.inc(i, evy, 1)
                    M(("ev_y", g, ch), evy, v)

                @block.gpsimd
                def _(gpsimd, g=g, ch=ch):
                    ctx.wait(gpsimd, ("ev_xn", g, ch))
                    ctx.inc(gpsimd.dma_start(out=out_fm["xn"][128 * ch:128 * (ch + 1), 512 * g:512 * (g + 1)],
                                             in_=ex["xn"][:, 512 * ch:512 * (ch + 1)]), st, 16)
                    ctx.wait(gpsimd, ("ev_y", g, ch))
                    ctx.inc(gpsimd.dma_start(out=out_fm["h1"][128 * ch:128 * (ch + 1), 512 * g:512 * (g + 1)],
                                             in_=ex["h1"][:, 512 * ch:512 * (ch + 1)]), st, 16)
                    ctx.inc(gpsimd.dma_start(out=out_fm["h2"][128 * ch:128 * (ch + 1), 512 * g:512 * (g + 1)],
                                             in_=ex["h2"][:, 512 * ch:512 * (ch + 1)]), st, 16)

        if t + 1 < NT or t == NT - 1:
            @block.sync
            def _(sync, t=t):
                tt = t + 1
                if tt < NT:
                    if tt >= 2:
                        ctx.wait(sync, ("s1", tt - 2))
                    ctx.inc(sync.dma_start(out=xt[tt % 2][:], in_=x_tm[128 * tt:128 * (tt + 1), :]), ld, 16)
                if 1 <= t < NT:
                    ctx.ckpt(sync, ld, ldc, ("ld_tile", t))

        @block.vector
        def _(vector, t=t):
            if t < NT:
                ctx.wait(vector, ("ld_tile", t))
                if t >= 2:
                    ctx.wait(vector, ("s1", t - 2))
                if t >= 4:
                    ctx.wait(vector, ("st_tm", t - 4))
                    ctx.wait(vector, ("s2", t - 4))
                if t >= NB:
                    ctx.wait(vector, ("trn", (t - NB) // 4, NC - 1, "y"))
                vector.tensor_reduce(va[t % 2][:], xt[t % 2][:], axis=AX.X, op=ALU.add)
                sepv(vector)
                i = vector.tensor_scalar(out=vm[t % 2][:], in0=va[t % 2][:], scalar1=-1.0 / D, scalar2=None, op0=ALU.mult)
                v = ctx.inc(i, c, 1)
                M(("d1", t), c, v)
            if 0 <= t - 1 < NT:
                u = t - 1
                ctx.wait(vector, ("s1", u))
                vector.reciprocal(out=vc[:], in_=vb[u % 2][:])
                sepv(vector)
                i = vector.tensor_scalar(out=xn[u % 4][:], in0=xn[u % 4][:], scalar1=vc[:], scalar2=None, op0=ALU.mult)
                v = ctx.inc(i, c, 1)
                M(("d2", u), c, v)
            if 0 <= t - 2 < NT:
                u = t - 2
                ctx.wait(vector, ("s2", u))
                vector.reciprocal(out=vc2[:], in_=vb2[u % 2][:])
                sepv(vector)
                i = vector.tensor_scalar(out=yb[u % NB][:], in0=xn[u % 4][:], scalar1=vc2[:], scalar2=None, op0=ALU.mult)
                v = ctx.inc(i, c, 1)
                M(("d3", u), c, v)

        @block.scalar
        def _(scalar, t=t):
            if t < NT:
                ctx.wait(scalar, ("d1", t))
                scalar.activation(out=xn[t % 4][:], in_=xt[t % 2][:], func=AF.Identity, bias=vm[t % 2][:])
                scalar.activation(out=sq[:], in_=xn[t % 4][:], func=AF.Square, accum_out=vs[t % 2][:])
                sepa(scalar)
                i = scalar.activation(out=vb[t % 2][:], in_=vs[t % 2][:], func=AF.Sqrt, scale=1.0 / D, bias=veps5[:])
                v = ctx.inc(i, a, 1)
                M(("s1", t), a, v)
            if 0 <= t - 1 < NT:
                u = t - 1
                ctx.wait(scalar, ("d2", u))
                scalar.activation(out=sq[:], in_=xn[u % 4][:], func=AF.Square, accum_out=vs2[u % 2][:])
                scalar.activation(out=xnb[u % NB][:], in_=xn[u % 4][:], func=AF.Copy)
                i = scalar.activation(out=vb2[u % 2][:], in_=vs2[u % 2][:], func=AF.Sqrt, scale=1.0 / D, bias=veps6[:])
                v = ctx.inc(i, a, 1)
                M(("s2", u), a, v)

        if 0 <= t - 1 < NT:
            @block.gpsimd
            def _(gpsimd, t=t):
                u = t - 1
                ctx.wait(gpsimd, ("d2", u))
                ctx.inc(gpsimd.dma_start(out=xn_tm[128 * u:128 * (u + 1), :], in_=xn[u % 4][:]), st, 16)
                if u >= 1:
                    ctx.ckpt(gpsimd, st, stc, ("st_tm", u - 1))


def stage1_sems(tag="s1"):
    return ["ld", "wld", "mm", "ev", "c", "a", "trn", "g", "evx", "evy", "st", "ldc", "wldc", "stc", "s10_cc"]




def emit_fm_matmul(ctx, ap, stack, tag, deps, x_name, out_name, segs,
                   x_rows=D, t_cols=T, out_row0=0, evac_view=None):
    ctx.scope = tag
    nc, block = ctx.nc, ctx.block
    sb = lambda n, s, d: alloc_sbuf(nc, stack, f"{tag}_{n}", s, d)
    psa = lambda n, s, d: alloc_psum(nc, stack, f"{tag}_{n}", s, d)

    NCX = x_rows // 128
    NH = t_cols // 512 // 2
    HALF = t_cols // 2

    X = [sb(f"x{c}", [128, t_cols], BF16) for c in range(NCX)]
    wsl = [sb(f"w{j}", [128, NCX * 128], BF16) for j in range(2)]
    ob = [sb(f"o{j}", [128, t_cols], BF16) for j in range(2)]
    ps = [psa(f"ps{j}", [128, 512], F32) for j in range(2 * NH)]

    ld, wld, mm, ev, st = ("ld", "wld", "mm", "ev", "st")
    ldc, wldc, stc = ("ldc", "wldc", "stc")
    M = ctx.mark
    out = ap[out_name]

    chunks = []
    rows_used = {}
    for seg in segs:
        wt_name, f_lo, f_hi, func, scale = seg[:5]
        oname = seg[5] if len(seg) > 5 else out_name
        view = seg[6] if len(seg) > 6 else evac_view
        base = out_row0 if oname == out_name else 0
        for fc in range(f_lo // 128, f_hi // 128):
            r = rows_used.get(oname, 0)
            rows_used[oname] = r + 1
            chunks.append((ap[wt_name], fc, func, scale, ap[oname], base + 128 * r, view))
    NF = len(chunks)

    @block.sync
    def _(sync):
        for s_, v_ in deps:
            sync.wait_ge(ctx.h[s_], v_)
        for c in range(NCX):
            ctx.inc(sync.dma_start(out=X[c][:], in_=ap[x_name][128 * c:128 * (c + 1), :]), ld, 16)
        ctx.ckpt(sync, ld, ldc, ("xload",))

    @block.gpsimd
    def _(gpsimd):
        for s_, v_ in deps:
            gpsimd.wait_ge(ctx.h[s_], v_)
        wt, fc = chunks[0][0], chunks[0][1]
        wr = wt.rearrange("(c p) f -> p c f", p=128)
        i = gpsimd.dma_start(out=wsl[0][:].rearrange("p (c f) -> p c f", f=128),
                             in_=wr[:, :, 128 * fc:128 * (fc + 1)])
        ctx.inc(i, wld, 16)
        ctx.ckpt(gpsimd, wld, wldc, ("wload", 0))

    @block.tensor
    def _(tensor):
        for s_, v_ in deps:
            tensor.wait_ge(ctx.h[s_], v_)

    @block.vector
    def _(vector):
        for s_, v_ in deps:
            vector.wait_ge(ctx.h[s_], v_)

    @block.scalar
    def _(scalar):
        for s_, v_ in deps:
            scalar.wait_ge(ctx.h[s_], v_)

    for f in range(NF):
        wt, fc, func, scale, oap, orow, view = chunks[f]

        if f + 1 < NF:
            @block.gpsimd
            def _(gpsimd, f=f):
                wtn, fcn = chunks[f + 1][0], chunks[f + 1][1]
                if f >= 1:
                    ctx.wait(gpsimd, ("mm_half", f - 1, 1))
                wr = wtn.rearrange("(c p) f -> p c f", p=128)
                i = gpsimd.dma_start(out=wsl[(f + 1) % 2][:].rearrange("p (c f) -> p c f", f=128),
                                     in_=wr[:, :, 128 * fcn:128 * (fcn + 1)])
                ctx.inc(i, wld, 16)
                ctx.ckpt(gpsimd, wld, wldc, ("wload", f + 1))

        @block.tensor
        def _(tensor, f=f):
            if f == 0:
                ctx.wait(tensor, ("xload",))
            ctx.wait(tensor, ("wload", f))
            for h in range(2):
                if f >= 1:
                    ctx.wait(tensor, ("ev", f - 1, h))
                i = None
                for c in range(NCX):
                    for b in range(NH):
                        i = nc.tensor.matmul(ps[NH * h + b][:],
                                             wsl[f % 2][:, 128 * c:128 * (c + 1)],
                                             X[c][:, HALF * h + 512 * b:HALF * h + 512 * (b + 1)],
                                             start=(c == 0), stop=(c == NCX - 1))
                v = ctx.inc(i, mm, 1)
                M(("mm_half", f, h), mm, v)

        @block.scalar
        def _(scalar, f=f):
            for h in range(2):
                ctx.wait(scalar, ("mm_half", f, h))
                if f >= 2:
                    ctx.wait(scalar, ("st", f - 2))
                i = None
                for b in range(NH):
                    kw = {}
                    if scale is not None:
                        kw["scale"] = scale
                    if view is None:
                        dst = ob[f % 2][:, HALF * h + 512 * b:HALF * h + 512 * (b + 1)]
                    else:
                        dst = view(ob[f % 2], NH * h + b)
                    i = scalar.activation(out=dst, in_=ps[NH * h + b][:], func=func, **kw)
                v = ctx.inc(i, ev, 1)
                M(("ev", f, h), ev, v)

        @block.gpsimd
        def _(gpsimd, f=f):
            ctx.wait(gpsimd, ("ev", f, 1))
            ctx.inc(gpsimd.dma_start(out=oap[orow:orow + 128, :],
                                     in_=ob[f % 2][:]), st, 16)
            if f >= 1:
                ctx.ckpt(gpsimd, st, stc, ("st", f - 1))

    return {"st_final": (st, ctx.n[st])}


def fm_matmul_sems(tag):
    return ["ld", "wld", "mm", "ev", "c", "a", "trn", "g", "evx", "evy", "st", "ldc", "wldc", "stc", "s10_cc"]




def emit_stage3(ctx, ap, stack, deps, tag="s3"):
    ctx.scope = tag
    nc, block = ctx.nc, ctx.block
    sb = lambda n, s, d: alloc_sbuf(nc, stack, f"{tag}_{n}", s, d)
    psa = lambda n, s, d: alloc_psum(nc, stack, f"{tag}_{n}", s, d)

    qkv = ap["qkv0_fm"]
    out_fm = ap["attn0_fm"]

    k4 = [sb(f"k4{j}", [128, 2048], BF16) for j in range(2)]
    v4 = [sb(f"v4{j}", [128, 2048], BF16) for j in range(2)]
    q4 = [sb(f"q4{j}", [128, 2048], BF16) for j in range(2)]
    kt = [sb(f"kt{j}", [128, 2048], BF16) for j in range(2)]
    vt = [sb(f"vt{j}", [128, 16 * 129], BF16) for j in range(2)]
    vk4m = [sb(f"vk4m{j}", [128, 128], BF16) for j in range(2)]
    ks4 = [sb(f"ks4{j}", [128, 4], BF16) for j in range(2)]
    rec = [sb(f"rec{j}", [4, 512], F32) for j in range(2)]
    rec2 = [sb(f"rec2{j}", [4, 512], F32) for j in range(2)]
    recb = [sb(f"recb{j}", [4, 512], BF16) for j in range(2)]
    bcs = [sb(f"bcs{j}", [128, 512], BF16) for j in range(2)]
    attn = [sb(f"attn{j}", [128, 2048], BF16) for j in range(2)]
    idb = sb("idb", [128, 128], BF16)
    m128 = sb("m128", [128, 128], BF16)
    m4 = sb("m4", [128, 4], F32)
    s4 = sb("s4", [4, 128], BF16)
    dmy = [sb(f"dmy{j}", [128, 32], F32) for j in range(2)]

    pst = [psa(f"pst{j}", [128, 512], BF16) for j in range(2)]
    pvk = psa("pvk", [128, 129], F32)
    pa = [psa(f"pa{j}", [128, 512], F32) for j in range(2)]
    pb = [psa(f"pb{j}", [4, 512], F32) for j in range(2)]
    pbc = psa("pbc", [128, 512], F32)

    ld, c, a, trn, g, st = ("ld", "c", "a", "trn", "g", "st")
    ldc, stc = "ldc", "stc"
    M = ctx.mark
    NIT = 32

    @block.sync
    def _(sync):
        for s_, v_ in deps:
            sync.wait_ge(ctx.h[s_], v_)
        for nm, tile in (("identb", idb), ("mask128", m128), ("mask4", m4), ("sel4", s4)):
            ctx.inc(sync.dma_start(out=tile[:], in_=ap[nm]), ld, 16)
        for j, (tile, r0) in enumerate(((k4[0], 2048), (v4[0], 4096), (q4[0], 0))):
            ctx.inc(sync.dma_start(out=tile[:], in_=qkv[r0:r0 + 128, 0:2048]), ld, 16)
        ctx.ckpt(sync, ld, ldc, ("ld", 0))

    for eng in ("tensor", "vector", "scalar", "gpsimd"):
        @getattr(block, eng)
        def _(e):
            for s_, v_ in deps:
                e.wait_ge(ctx.h[s_], v_)

    @block.vector
    def _(vector):
        vector.memset(vt[0][:], 1.0)
        vector.memset(vt[1][:], 1.0)
        vector.memset(dmy[0][:], 0.0)

    for it in range(NIT):
        b, cb = it // 16, it % 16
        bb = it % 2

        @block.sync
        def _(sync, it=it):
            if it >= 1:
                ctx.ckpt(sync, ld, ldc, ("ld", it))
            if it + 1 < NIT:
                nb, ncb = (it + 1) // 16, (it + 1) % 16
                if it >= 1:
                    ctx.wait(sync, ("trans", it - 1, "k", 3))
                    ctx.wait(sync, ("trans", it - 1, "v", 3))
                    ctx.wait(sync, ("out2", it - 1, 3))
                for tile, r0 in ((k4[(it + 1) % 2], 2048), (v4[(it + 1) % 2], 4096), (q4[(it + 1) % 2], 0)):
                    ctx.inc(sync.dma_start(out=tile[:], in_=qkv[r0 + 128 * ncb:r0 + 128 * (ncb + 1),
                                                                2048 * nb:2048 * (nb + 1)]), ld, 16)

        for strm, src, dst in (("k", k4, kt), ("v", v4, vt)):
            for j0 in range(4):
                @block.tensor
                def _(tensor, it=it, strm=strm, j0=j0, src=src):
                    if strm == "k" and j0 == 0:
                        ctx.wait(tensor, ("ld", it))
                        if it >= 2:
                            ctx.wait(tensor, ("vk", it - 2))
                            ctx.wait(tensor, ("vtevac", it - 2))
                    key = ("tevac", it, strm, j0 - 2) if j0 >= 2 else \
                          (("tevac", it, "k", 2 + j0) if strm == "v" else
                           (("tevac", it - 1, "v", 2 + j0) if it >= 1 else None))
                    if key is not None:
                        ctx.wait(tensor, key)
                    i = None
                    for jj in range(4):
                        j = 4 * j0 + jj
                        i = nc.tensor.transpose(pst[j0 % 2][:, 128 * jj:128 * (jj + 1)],
                                                src[it % 2][:, 128 * j:128 * (j + 1)], idb[:])
                    v = ctx.inc(i, trn, 1)
                    M(("trans", it, strm, j0), trn, v)

                if strm == "k":
                    @block.vector
                    def _(vector, it=it, j0=j0, dst=dst):
                        ctx.wait(vector, ("trans", it, "k", j0))
                        if j0 == 0 and it >= 2:
                            ctx.wait(vector, ("vk", it - 2))
                        i = vector.tensor_copy(dst[it % 2][:, 512 * j0:512 * (j0 + 1)], pst[j0 % 2][:])
                        v = ctx.inc(i, c, 1)
                        M(("tevac", it, "k", j0), c, v)
                else:
                    @block.scalar
                    def _(scalar, it=it, j0=j0, dst=dst):
                        ctx.wait(scalar, ("trans", it, "v", j0))
                        if j0 == 0 and it >= 2:
                            ctx.wait(scalar, ("vk", it - 2))
                        vv = dst[it % 2][:].rearrange("p (j e) -> p j e", e=129)
                        i = scalar.activation(out=vv[:, 4 * j0:4 * (j0 + 1), 0:128],
                                              in_=pst[j0 % 2][:].rearrange("p (j e) -> p j e", e=128),
                                              func=AF.Copy)
                        v = ctx.inc(i, a, 1)
                        M(("tevac", it, "v", j0), a, v)
                        if j0 == 3:
                            M(("vtevac", it), a, v)

        @block.tensor
        def _(tensor, it=it):
            ctx.wait(tensor, ("tevac", it, "k", 3))
            ctx.wait(tensor, ("tevac", it, "v", 3))
            if it >= 1:
                ctx.wait(tensor, ("vkevac", it - 1))
            i = None
            vv = vt[it % 2][:].rearrange("p (j e) -> p j e", e=129)
            for j in range(16):
                i = nc.tensor.matmul(pvk[:], kt[it % 2][:, 128 * j:128 * (j + 1)],
                                     vv[:, j, :], start=(j == 0), stop=(j == 15))
            v = ctx.inc(i, trn, 1)
            M(("vk", it), trn, v)

        @block.vector
        def _(vector, it=it):
            ctx.wait(vector, ("vk", it))
            if it >= 2:
                ctx.wait(vector, ("out2", it - 2, 3))
            vector.tensor_tensor(out=vk4m[bb][:], in0=pvk[:, 0:128], in1=m128[:], op=ALU.mult)
            i = vector.tensor_scalar(out=ks4[bb][:], in0=m4[:], scalar1=pvk[:, 128:129], scalar2=None, op0=ALU.mult)
            v = ctx.inc(i, c, 1)
            M(("vkevac", it), c, v)

        for ch in range(4):
            @block.tensor
            def _(tensor, it=it, ch=ch):
                if ch == 0:
                    ctx.wait(tensor, ("vkevac", it))
                key = ("attnmul", it, ch - 2) if ch >= 2 else (("attnmul", it - 1, 2 + ch) if it >= 1 else None)
                if key is not None:
                    ctx.wait(tensor, key)
                keyb = ("div1", it, ch - 2) if ch >= 2 else (("div1", it - 1, 2 + ch) if it >= 1 else None)
                if keyb is not None:
                    ctx.wait(tensor, keyb)
                i = nc.tensor.matmul(pa[ch % 2][:], vk4m[bb][:], q4[it % 2][:, 512 * ch:512 * (ch + 1)],
                                     start=True, stop=True)
                i = nc.tensor.matmul(pb[ch % 2][:], ks4[bb][:], q4[it % 2][:, 512 * ch:512 * (ch + 1)],
                                     start=True, stop=True)
                v = ctx.inc(i, trn, 1)
                M(("out2", it, ch), trn, v)

            @block.vector
            def _(vector, it=it, ch=ch):
                ctx.wait(vector, ("out2", it, ch))
                key = ("bc", it, ch - 2) if ch >= 2 else (("bc", it - 1, 2 + ch) if it >= 1 else None)
                if key is not None:
                    ctx.wait(vector, key)
                vector.tensor_scalar(out=rec[ch % 2][:], in0=pb[ch % 2][:], scalar1=1e-15, scalar2=None, op0=ALU.add)
                vector.tensor_copy(dmy[1][:], dmy[0][:])
                vector.reciprocal(out=rec2[ch % 2][:], in_=rec[ch % 2][:])
                vector.tensor_copy(dmy[1][:], dmy[0][:])
                i = vector.tensor_copy(recb[ch % 2][:], rec2[ch % 2][:])
                v = ctx.inc(i, c, 1)
                M(("div1", it, ch), c, v)

            @block.tensor
            def _(tensor, it=it, ch=ch):
                ctx.wait(tensor, ("div1", it, ch))
                if not (it == 0 and ch == 0):
                    pc, pch = (it, ch - 1) if ch >= 1 else (it - 1, 3)
                    ctx.wait(tensor, ("bcevac", pc, pch))
                i = nc.tensor.matmul(pbc[:], s4[:], recb[ch % 2][:], start=True, stop=True)
                v = ctx.inc(i, trn, 1)
                M(("bc", it, ch), trn, v)

            @block.scalar
            def _(scalar, it=it, ch=ch):
                ctx.wait(scalar, ("bc", it, ch))
                key = ("attnmul", it, ch - 2) if ch >= 2 else (("attnmul", it - 1, 2 + ch) if it >= 1 else None)
                if key is not None:
                    ctx.wait(scalar, key)
                i = scalar.activation(out=bcs[ch % 2][:], in_=pbc[:], func=AF.Copy)
                v = ctx.inc(i, a, 1)
                M(("bcevac", it, ch), a, v)

            @block.vector
            def _(vector, it=it, ch=ch):
                ctx.wait(vector, ("bcevac", it, ch))
                if ch == 0 and it >= 2:
                    ctx.wait(vector, ("attst", it - 2))
                i = vector.tensor_tensor(out=attn[it % 2][:, 512 * ch:512 * (ch + 1)],
                                         in0=pa[ch % 2][:], in1=bcs[ch % 2][:], op=ALU.mult)
                v = ctx.inc(i, c, 1)
                M(("attnmul", it, ch), c, v)

        @block.gpsimd
        def _(gpsimd, it=it):
            if it >= 1:
                ctx.ckpt(gpsimd, st, stc, ("attst", it - 1))
            ctx.wait(gpsimd, ("attnmul", it, 3))
            ctx.inc(gpsimd.dma_start(out=out_fm[128 * cb:128 * (cb + 1), 2048 * b:2048 * (b + 1)],
                                     in_=attn[it % 2][:]), st, 16)


def stage3_sems(tag="s3"):
    return ["ld", "wld", "mm", "ev", "c", "a", "trn", "g", "evx", "evy", "st", "ldc", "wldc", "stc", "s10_cc"]




def emit_tm_matmul(ctx, ap, stack, tag, deps, x_name, wt_name, out_name,
                   bias_name=None, addins=(), din=D, fout=D, out_dt=F32,
                   out_view=None):
    ctx.scope = tag
    nc, block = ctx.nc, ctx.block
    sb = lambda n, s, d: alloc_sbuf(nc, stack, f"{tag}_{n}", s, d)
    psa = lambda n, s, d: alloc_psum(nc, stack, f"{tag}_{n}", s, d)

    NCX = din // 128
    NFB = fout // 512
    assert NFB <= 4
    NSET = 2 if 2 * NFB + (1 if bias_name else 0) <= 8 else 1

    W = [sb(f"w{c}", [128, fout], BF16) for c in range(NCX)]
    X = [sb(f"x{j}", [128, 4 * 128], BF16) for j in range(NCX)]
    X2 = [sb(f"x2_{j}", [128, 4 * 128], BF16) for j in range(NCX)]
    ADD = [[sb(f"ad{k}_{j}", [128, fout], F32) for j in range(2)] for k in range(len(addins))]
    OB = [sb(f"ob{j}", [128, fout], F32 if out_dt == F32 else out_dt) for j in range(2)]
    ps = [psa(f"ps{j}", [128, 512], F32) for j in range(NSET * NFB)]
    bias_b = sb("biasb", [128, fout], F32) if bias_name else None
    ones1 = sb("ones1", [1, 128], BF16) if bias_name else None
    brow = sb("brow", [1, fout], BF16) if bias_name else None
    pbias = psa("pbias", [128, 512], F32) if bias_name else None

    ld, wld, mm, ev, st = ("ld", "wld", "mm", "ev", "st")
    ldc, wldc, stc = ("ldc", "wldc", "stc")
    M = ctx.mark
    out = ap[out_name]
    xap = ap[x_name]
    XB = [X, X2]

    @block.sync
    def _(sync):
        for s_, v_ in deps:
            sync.wait_ge(ctx.h[s_], v_)
        if bias_name:
            ctx.inc(sync.dma_start(out=ones1[:], in_=ap["ones1b"]), ld, 16)
        for c in range(NCX):
            ctx.inc(sync.dma_start(out=X[c][:], in_=xap[128 * c:128 * (c + 1), 0:512]), ld, 16)
        for k, anm in enumerate(addins):
            srcap = ap[anm][0:128, :] if out_view is None else out_view(ap[anm], 0)
            ctx.inc(sync.dma_start(out=ADD[k][0][:], in_=srcap), ld, 16)
        ctx.ckpt(sync, ld, ldc, ("xload", 0))
        if addins:
            M(("adld", 0), ldc, ctx.n[ldc])
        if bias_name:
            M(("auxld0",), ldc, ctx.n[ldc])

    @block.gpsimd
    def _(gpsimd):
        for s_, v_ in deps:
            gpsimd.wait_ge(ctx.h[s_], v_)
        if bias_name:
            ctx.inc(gpsimd.dma_start(out=brow[:], in_=ap[bias_name]), wld, 16)
        wr = ap[wt_name].rearrange("(c p) f -> p c f", p=128)
        for c in range(NCX):
            ctx.inc(gpsimd.dma_start(out=W[c][:], in_=wr[:, c, :]), wld, 16)
        ctx.ckpt(gpsimd, wld, wldc, ("wload",))
        if bias_name:
            M(("auxld1",), wldc, ctx.n[wldc])

    for eng in ("tensor", "vector", "scalar"):
        @getattr(block, eng)
        def _(e):
            for s_, v_ in deps:
                e.wait_ge(ctx.h[s_], v_)

    if bias_name:
        for fb in range(fout // 512):
            @block.tensor
            def _(tensor, fb=fb):
                if fb == 0:
                    ctx.wait(tensor, ("auxld0",))
                    ctx.wait(tensor, ("auxld1",))
                else:
                    ctx.wait(tensor, ("biasev", fb - 1))
                i = nc.tensor.matmul(pbias[:], ones1[:], brow[:, 512 * fb:512 * (fb + 1)],
                                     start=True, stop=True)
                ctx.inc(i, mm, 1)
                M(("biasmm", fb), mm, ctx.n[mm])

            @block.vector
            def _(vector, fb=fb):
                ctx.wait(vector, ("biasmm", fb))
                i = vector.tensor_copy(bias_b[:, 512 * fb:512 * (fb + 1)], pbias[:])
                ctx.inc(i, ev, 1)
                M(("biasev", fb), ev, ctx.n[ev])

    for tt in range(NT):
        slab, off = tt // 4, tt % 4
        if off == 0 and (slab + 1 < NT // 4 or slab == NT // 4 - 1):
            @block.sync
            def _(sync, slab=slab):
                if slab + 1 < NT // 4:
                    if slab >= 1:
                        ctx.wait(sync, ("mmdone", 4 * (slab - 1) + 3))
                    for c in range(NCX):
                        ctx.inc(sync.dma_start(out=XB[(slab + 1) % 2][c][:],
                                               in_=xap[128 * c:128 * (c + 1), 512 * (slab + 1):512 * (slab + 2)]), ld, 16)
                    ctx.ckpt(sync, ld, ldc, ("xload", slab + 1))

        if addins:
            @block.sync
            def _(sync, tt=tt):
                if tt + 1 < NT:
                    if tt >= 1:
                        ctx.wait(sync, ("ev", tt - 1, NFB - 1))
                    for k, anm in enumerate(addins):
                        srcap = (ap[anm][128 * (tt + 1):128 * (tt + 2), :] if out_view is None
                                 else out_view(ap[anm], tt + 1))
                        ctx.inc(sync.dma_start(out=ADD[k][(tt + 1) % 2][:], in_=srcap), ld, 16)
                    ctx.ckpt(sync, ld, ldc, ("adld", tt + 1))

        @block.tensor
        def _(tensor, tt=tt, slab=slab, off=off):
            if bias_name and tt == 0:
                ctx.wait(tensor, ("biasev", fout // 512 - 1))
            ctx.wait(tensor, ("xload", slab))
            if tt == 0:
                ctx.wait(tensor, ("wload",))
            i = None
            for fb in range(NFB):
                if tt >= NSET:
                    ctx.wait(tensor, ("ev", tt - NSET, fb))
                for c in range(NCX):
                    i = nc.tensor.matmul(ps[NFB * (tt % NSET) + fb][:],
                                         XB[slab % 2][c][:, 128 * off:128 * (off + 1)],
                                         W[c][:, 512 * fb:512 * (fb + 1)],
                                         start=(c == 0), stop=(c == NCX - 1))
                v = ctx.inc(i, mm, 1)
                M(("mmf", tt, fb), mm, v)
            M(("mmdone", tt), mm, ctx.n[mm])

        @block.vector
        def _(vector, tt=tt):
            if addins:
                ctx.wait(vector, ("adld", tt))
            if tt >= 2:
                ctx.wait(vector, ("st", tt - 2))
            nsrc = len(addins) + (1 if bias_name else 0)
            for fb in range(NFB):
                ctx.wait(vector, ("mmf", tt, fb))
            for step in range(max(1, nsrc)):
                for fb in range(NFB):
                    sl = slice(512 * fb, 512 * (fb + 1))
                    pp = ps[NFB * (tt % NSET) + fb][:]
                    srcs = ([bias_b[:, sl]] if bias_name else []) + \
                           [ADD[k][tt % 2][:, sl] for k in range(len(addins))]
                    if nsrc == 0:
                        i = vector.tensor_copy(OB[tt % 2][:, sl], pp)
                    elif step == 0:
                        i = vector.tensor_tensor(out=OB[tt % 2][:, sl], in0=pp, in1=srcs[0], op=ALU.add)
                    else:
                        i = vector.tensor_tensor(out=OB[tt % 2][:, sl], in0=OB[tt % 2][:, sl], in1=srcs[step], op=ALU.add)
                    if step == max(1, nsrc) - 1:
                        v = ctx.inc(i, ev, 1)
                        M(("ev", tt, fb), ev, v)

        @block.gpsimd
        def _(gpsimd, tt=tt):
            if tt >= 1:
                ctx.ckpt(gpsimd, st, stc, ("st", tt - 1))
            ctx.wait(gpsimd, ("ev", tt, NFB - 1))
            dst = out[128 * tt:128 * (tt + 1), :] if out_view is None else out_view(out, tt)
            src = OB[tt % 2][:]
            ctx.inc(gpsimd.dma_start(out=dst, in_=src), st, 16)


def tm_matmul_sems(tag):
    return ["ld", "wld", "mm", "ev", "c", "a", "trn", "g", "evx", "evy", "st", "ldc", "wldc", "stc", "s10_cc"]




def emit_stage6(ctx, ap, stack, deps, tag="s6"):
    ctx.scope = tag
    nc, block = ctx.nc, ctx.block
    sb = lambda n, s, d: alloc_sbuf(nc, stack, f"{tag}_{n}", s, d)
    psa = lambda n, s, d: alloc_psum(nc, stack, f"{tag}_{n}", s, d)

    qkv = ap["qkv1_fm"]
    out_fm = ap["attnPV_fm"]

    qs = [sb(f"q{j}", [128, 2048], BF16) for j in range(2)]
    ks = [sb(f"k{j}", [128, 2048], BF16) for j in range(2)]
    vs = [sb(f"v{j}", [128, 2048], BF16) for j in range(2)]
    vt = [sb(f"vt{j}", [128, 2048], BF16) for j in range(2)]
    sq2 = sb("sq2", [128, 2048], BF16)
    P = [[sb(f"P{s}_{j}", [128, 2048], BF16) for j in range(4)] for s in range(2)]
    ptg = sb("ptg", [128, 16 * 512], BF16)
    ob = [sb(f"ob{j}", [128, 512], BF16) for j in range(2)]
    tri = sb("tri", [128, 128], BF16)
    idb = sb("idb", [128, 128], BF16)
    ones128 = sb("ones128", [128, 128], BF16)
    kmaxv = sb("kmaxv", [128, 2], F32)
    qmaxv = sb("qmaxv", [128, 2], F32)
    bnd = sb("bnd", [128, 2], F32)
    nb = [sb(f"nb{j}", [128, 1], F32) for j in range(2)]
    ls = [sb(f"l{j}", [128, 1], F32) for j in range(8)]
    lsb = [sb(f"lb{j}", [128, 1], F32) for j in range(8)]
    li = [sb(f"li{j}", [128, 1], F32) for j in range(8)]
    dmy = [sb(f"dmy{j}", [128, 32], F32) for j in range(2)]

    pss = psa("pss", [128, 2048], F32)
    pso = psa("pso", [128, 512], F32)
    ptr = [psa(f"ptr{j}", [128, 512], BF16) for j in range(2)]
    pnm = psa("pnm", [128, 512], F32)

    ld, c, a, trn, st = ("ld", "c", "a", "trn", "st")
    ldc, stc = "ldc", "stc"
    M = ctx.mark
    NIT = 32
    sepv = lambda e: e.tensor_copy(dmy[1][:], dmy[0][:])

    @block.sync
    def _(sync):
        for s_, v_ in deps:
            sync.wait_ge(ctx.h[s_], v_)
        for nm, tile in (("tri_maskb", tri), ("identb", idb),
                         ("ones128b", ones128)):
            ctx.inc(sync.dma_start(out=tile[:], in_=ap[nm]), ld, 16)
        for tile, r0 in ((qs[0], 0), (ks[0], 2048), (vs[0], 4096)):
            ctx.inc(sync.dma_start(out=tile[:], in_=qkv[r0:r0 + 128, 0:2048]), ld, 16)
        ctx.ckpt(sync, ld, ldc, ("ld", 0))

    for eng in ("tensor", "vector", "scalar", "gpsimd"):
        @getattr(block, eng)
        def _(e):
            for s_, v_ in deps:
                e.wait_ge(ctx.h[s_], v_)

    @block.vector
    def _(vector):
        vector.memset(dmy[0][:], 0.0)

    def emit_norms(j):
        for which, src in (("k", ks), ("q", qs)):
            mx = kmaxv if which == "k" else qmaxv
            @block.vector
            def _(vector, j=j, which=which, src=src):
                if which == "k":
                    ctx.wait(vector, ("ld", j))
                    if j >= 1:
                        ctx.wait(vector, ("nmm", j - 1, "q", 3))
                else:
                    ctx.wait(vector, ("nmm", j, "k", 3))
                i = vector.tensor_tensor(out=sq2[:], in0=src[j % 2][:], in1=src[j % 2][:], op=ALU.mult)
                v = ctx.inc(i, c, 1)
                M(("sq_", j, which), c, v)

            for ch in range(4):
                @block.tensor
                def _(tensor, j=j, which=which, ch=ch):
                    ctx.wait(tensor, ("sq_", j, which))
                    key = ("nrd", j, which, ch - 1) if ch >= 1 else \
                          (("nrd", j, "k", 3) if which == "q" else
                           (("nrd", j - 1, "q", 3) if j >= 1 else None))
                    if key is not None:
                        ctx.wait(tensor, key)
                    i = nc.tensor.matmul(pnm[:], ones128[:], sq2[:, 512 * ch:512 * (ch + 1)],
                                         start=True, stop=True)
                    v = ctx.inc(i, trn, 1)
                    M(("nmm", j, which, ch), trn, v)

                @block.vector
                def _(vector, j=j, which=which, ch=ch, mx=mx):
                    ctx.wait(vector, ("nmm", j, which, ch))
                    vector.tensor_reduce(mx[:, 0:1], pnm[:], axis=AX.X, op=ALU.max)
                    sepv(vector)
                    if ch == 0:
                        i = vector.tensor_copy(mx[:, 1:2], mx[:, 0:1])
                    else:
                        i = vector.tensor_tensor(out=mx[:, 1:2], in0=mx[:, 1:2], in1=mx[:, 0:1], op=ALU.max)
                    v = ctx.inc(i, c, 1)
                    M(("nrd", j, which, ch), c, v)

        @block.vector
        def _(vector, j=j):
            ctx.wait(vector, ("nrd", j, "q", 3))
            i = vector.tensor_tensor(out=bnd[:, 0:1], in0=kmaxv[:, 1:2], in1=qmaxv[:, 1:2], op=ALU.mult)
            v = ctx.inc(i, c, 1)
            M(("bnd2", j), c, v)

        @block.scalar
        def _(scalar, j=j):
            ctx.wait(scalar, ("bnd2", j))
            i = scalar.activation(out=bnd[:, 1:2], in_=bnd[:, 0:1], func=AF.Sqrt, scale=1.0)
            v = ctx.inc(i, a, 1)
            M(("bnds", j), a, v)

        @block.vector
        def _(vector, j=j):
            ctx.wait(vector, ("bnds", j))
            i = vector.tensor_scalar(out=nb[j % 2][:], in0=bnd[:, 1:2], scalar1=-1.0, scalar2=None, op0=ALU.mult)
            v = ctx.inc(i, c, 1)
            M(("nbr", j), c, v)

    emit_norms(0)
    bank_owner = {}

    for it in range(NIT):
        b, qh = it // 16, it % 16
        bb = it % 2

        @block.sync
        def _(sync, it=it):
            if it + 1 < NIT:
                nb_, nqh = (it + 1) // 16, (it + 1) % 16
                if it >= 1:
                    ctx.wait(sync, ("pv", it - 1, 3))
                    ctx.wait(sync, ("sq_", it - 1, "q"))
                for tile, r0 in ((qs[(it + 1) % 2], 0), (ks[(it + 1) % 2], 2048), (vs[(it + 1) % 2], 4096)):
                    ctx.inc(sync.dma_start(out=tile[:], in_=qkv[r0 + 128 * nqh:r0 + 128 * (nqh + 1),
                                                                2048 * nb_:2048 * (nb_ + 1)]), ld, 16)
                ctx.ckpt(sync, ld, ldc, ("ld", it + 1))

        for g4 in range(4):
            @block.tensor
            def _(tensor, it=it, g4=g4):
                if g4 == 0:
                    ctx.wait(tensor, ("ld", it))
                    if it >= 2:
                        ctx.wait(tensor, ("pv", it - 2, 3))
                key = ("vte", it, g4 - 2) if g4 >= 2 else (("pte", it - 1, "last") if it >= 1 else None)
                if key is not None:
                    ctx.wait(tensor, key)
                i = None
                for jj in range(4):
                    j = 4 * g4 + jj
                    i = nc.tensor.transpose(ptr[g4 % 2][:, 128 * jj:128 * (jj + 1)],
                                            vs[it % 2][:, 128 * j:128 * (j + 1)], idb[:])
                v = ctx.inc(i, trn, 1)
                M(("vtr", it, g4), trn, v)

            @block.vector
            def _(vector, it=it, g4=g4):
                ctx.wait(vector, ("vtr", it, g4))
                if g4 == 0 and it >= 2:
                    ctx.wait(vector, ("pv", it - 2, 3))
                i = vector.tensor_copy(vt[it % 2][:, 512 * g4:512 * (g4 + 1)], ptr[g4 % 2][:])
                v = ctx.inc(i, c, 1)
                M(("vte", it, g4), c, v)

        def emit_ptrpv(qg, it=it, b=b, qh=qh):
            for kc128 in range(4 * (qg + 1)):
                @block.tensor
                def _(tensor, it=it, qg=qg, kc128=kc128):
                    if kc128 == 0:
                        for qt in range(4):
                            ctx.wait(tensor, ("psc", it, qg, qt))
                    key = ("pte", it, qg, kc128 - 2) if kc128 >= 2 else \
                          (("vte", it, 2 + kc128) if qg == 0 else ("pte", it, qg - 1, 4 * qg - 2 + kc128))
                    ctx.wait(tensor, key)
                    i = None
                    for qt in range(4):
                        i = nc.tensor.transpose(ptr[kc128 % 2][:, 128 * qt:128 * (qt + 1)],
                                                P[qg % 2][qt][:, 128 * kc128:128 * (kc128 + 1)], idb[:])
                    v = ctx.inc(i, trn, 1)
                    M(("ptr", it, qg, kc128), trn, v)

                @block.vector
                def _(vector, it=it, qg=qg, kc128=kc128):
                    ctx.wait(vector, ("ptr", it, qg, kc128))
                    i = vector.tensor_copy(ptg[:, 512 * kc128:512 * (kc128 + 1)], ptr[kc128 % 2][:])
                    v = ctx.inc(i, c, 1)
                    M(("pte", it, qg, kc128), c, v)
                    if qg == 3 and kc128 == 15:
                        M(("pte", it, "last"), c, v)

            @block.tensor
            def _(tensor, it=it, qg=qg):
                for kc128 in range(4 * (qg + 1)):
                    ctx.wait(tensor, ("pte", it, qg, kc128))
                if not (it == 0 and qg == 0):
                    pit, pqg = (it, qg - 1) if qg >= 1 else (it - 1, 3)
                    ctx.wait(tensor, ("pve", pit, pqg))
                i = None
                for kc128 in range(4 * (qg + 1)):
                    i = nc.tensor.matmul(pso[:], vt[it % 2][:, 128 * kc128:128 * (kc128 + 1)],
                                         ptg[:, 512 * kc128:512 * (kc128 + 1)],
                                         start=(kc128 == 0), stop=(kc128 == 4 * (qg + 1) - 1))
                v = ctx.inc(i, trn, 1)
                M(("pv", it, qg), trn, v)

            @block.scalar
            def _(scalar, it=it, qg=qg):
                ctx.wait(scalar, ("pv", it, qg))
                if not (it == 0 and qg < 2):
                    pit, pqg = (it, qg - 2) if qg >= 2 else (it - 1, qg + 2)
                    ctx.wait(scalar, ("pvst", pit, pqg))
                i = scalar.activation(out=ob[qg % 2][:], in_=pso[:], func=AF.Copy)
                v = ctx.inc(i, a, 1)
                M(("pve", it, qg), a, v)

            @block.gpsimd
            def _(gpsimd, it=it, qg=qg, b=b, qh=qh):
                if not (it == 0 and qg == 0):
                    pit, pqg = (it, qg - 1) if qg >= 1 else (it - 1, 3)
                    ctx.ckpt(gpsimd, st, stc, ("pvst", pit, pqg))
                ctx.wait(gpsimd, ("pve", it, qg))
                ctx.inc(gpsimd.dma_start(out=out_fm[128 * qh:128 * (qh + 1),
                                                    2048 * b + 512 * qg:2048 * b + 512 * (qg + 1)],
                                         in_=ob[qg % 2][:]), st, 16)

        for qg in range(4):
            L = 512 * (qg + 1)
            for qt in range(4):
                qt_g = 4 * qg + qt

                Lq = 128 * (qt_g + 1)
                nbk = (Lq + 511) // 512
                off = (512 * qt) if qg == 0 else (1024 * (qt % 2) if qg == 1 else 0)
                split = 1024 if Lq > 1024 else None
                mybanks = list(range(off // 512, off // 512 + nbk))
                bank_wait = {bk: bank_owner[bk] for bk in mybanks if bk in bank_owner}
                for bk in mybanks:
                    half = split is not None and bk < (off + split) // 512
                    bank_owner[bk] = (("expa" if half else "exp"), it, qg, qt)

                @block.tensor
                def _(tensor, it=it, qg=qg, qt=qt, qt_g=qt_g, Lq=Lq, off=off, bank_wait=bank_wait):
                    if qg == 0 and qt == 0:
                        ctx.wait(tensor, ("nbr", it))
                    done = set()

                    def need(col):
                        mk = bank_wait.get((off + col) // 512)
                        if mk is not None and mk not in done:
                            ctx.wait(tensor, mk)
                            done.add(mk)

                    i = None
                    nfull = (Lq - 128) // 512
                    for kc in range(nfull):
                        need(512 * kc)
                        need(512 * (kc + 1) - 1)
                        i = nc.tensor.matmul(pss[:, off + 512 * kc:off + 512 * (kc + 1)],
                                             qs[it % 2][:, 128 * qt_g:128 * (qt_g + 1)],
                                             ks[it % 2][:, 512 * kc:512 * (kc + 1)],
                                             start=True, stop=True)
                    rem = Lq - 512 * nfull
                    need(512 * nfull)
                    need(Lq - 1)
                    i = nc.tensor.matmul(pss[:, off + 512 * nfull:off + 512 * nfull + rem],
                                         qs[it % 2][:, 128 * qt_g:128 * (qt_g + 1)],
                                         ks[it % 2][:, 512 * nfull:512 * nfull + rem],
                                         start=True, stop=False)
                    i = nc.tensor.matmul(pss[:, off + 128 * qt_g:off + 128 * (qt_g + 1)],
                                         idb[:], tri[:], start=False, stop=True)
                    v = ctx.inc(i, trn, 1)
                    M(("sc", it, qg, qt), trn, v)

                @block.scalar
                def _(scalar, it=it, qg=qg, qt=qt, Lq=Lq, off=off, split=split):
                    ctx.wait(scalar, ("sc", it, qg, qt))
                    if qg >= 2:
                        ctx.wait(scalar, ("ptr", it, qg - 2, 4 * (qg - 1) - 1))
                    elif it >= 1:
                        ctx.wait(scalar, ("ptr", it - 1, qg + 2, 4 * (qg + 3) - 1))
                    s4i = 4 * (qg % 2) + qt
                    if split is not None:
                        i = scalar.activation(out=P[qg % 2][qt][:, 0:split], in_=pss[:, off:off + split],
                                              func=AF.Exp, bias=nb[it % 2][:], accum_out=ls[s4i][:])
                        v = ctx.inc(i, a, 1)
                        M(("expa", it, qg, qt), a, v)
                        scalar.activation(out=P[qg % 2][qt][:, split:Lq], in_=pss[:, off + split:off + Lq],
                                          func=AF.Exp, bias=nb[it % 2][:], accum_out=lsb[s4i][:])
                    else:
                        scalar.activation(out=P[qg % 2][qt][:, 0:Lq], in_=pss[:, off:off + Lq], func=AF.Exp,
                                          bias=nb[it % 2][:], accum_out=ls[s4i][:])
                    i = scalar.activation(out=dmy[1][:], in_=dmy[0][:], func=AF.Copy)
                    v = ctx.inc(i, a, 1)
                    M(("exp", it, qg, qt), a, v)

                @block.vector
                def _(vector, it=it, qg=qg, qt=qt, L=L, Lq=Lq, split=split):
                    ctx.wait(vector, ("exp", it, qg, qt))
                    s4i = 4 * (qg % 2) + qt
                    if split is not None:
                        vector.tensor_tensor(out=lsb[s4i][:], in0=ls[s4i][:], in1=lsb[s4i][:], op=ALU.add)
                        sepv(vector)
                        vector.reciprocal(out=li[s4i][:], in_=lsb[s4i][:])
                    else:
                        vector.reciprocal(out=li[s4i][:], in_=ls[s4i][:])
                    if Lq < L:
                        vector.memset(P[qg % 2][qt][:, Lq:L], 0.0)
                    else:
                        sepv(vector)
                    i = vector.tensor_scalar(out=P[qg % 2][qt][:, 0:Lq], in0=P[qg % 2][qt][:, 0:Lq],
                                             scalar1=li[s4i][:], scalar2=None, op0=ALU.mult)
                    v = ctx.inc(i, c, 1)
                    M(("psc", it, qg, qt), c, v)

            if qg >= 1:
                emit_ptrpv(qg - 1)
            if qg == 1 and it + 1 < NIT:
                emit_norms(it + 1)
        emit_ptrpv(3)


def stage6_sems(tag="s6"):
    return ["ld", "wld", "mm", "ev", "c", "a", "trn", "g", "evx", "evy", "st", "ldc", "wldc", "stc", "s10_cc"]




def emit_stage7(ctx, ap, stack, deps, tag="s7"):
    ctx.scope = tag
    nc, block = ctx.nc, ctx.block
    sb = lambda n, s, d: alloc_sbuf(nc, stack, f"{tag}_{n}", s, d)
    psa = lambda n, s, d: alloc_psum(nc, stack, f"{tag}_{n}", s, d)

    woa = sb("woa", [128, 256], BF16)
    A_sb = sb("A", [128, 256], BF16)
    wo = [sb(f"wo{j}", [128, 2048], BF16) for j in range(2)]
    at = [sb(f"at{j}", [128, 16 * 512], BF16) for j in range(2)]
    h2 = [sb(f"h2{j}", [128, 16 * 512], BF16) for j in range(2)]
    tnh = sb("tnh", [16, 512], F32)
    hsc = sb("hsc", [16, 512], F32)
    mmo = [sb(f"mmo{j}", [16, 512], BF16) for j in range(2)]
    pw = [psa(f"pw{j}", [128, 512], F32) for j in range(2)]
    pm = psa("pm", [128, 512], F32)
    ph = psa("ph", [128, 512], F32)

    ld, wld, c, a, trn, st = ("ld", "wld", "c", "a", "trn", "st")
    ldc, wldc, stc = ("ldc", "wldc", "stc")
    M = ctx.mark

    @block.gpsimd
    def _(gpsimd):
        for s_, v_ in deps:
            gpsimd.wait_ge(ctx.h[s_], v_)
        ar = ap["A_in"].rearrange("(c p) r -> p c r", p=128)
        ctx.inc(gpsimd.dma_start(out=A_sb[:].rearrange("p (c r) -> p c r", r=16), in_=ar), wld, 16)
        wr = ap["Wo_in"].rearrange("(c p) e -> p c e", p=128)
        ctx.inc(gpsimd.dma_start(out=wo[0][:].rearrange("p (c e) -> p c e", e=128),
                                 in_=wr[:, :, 0:128]), wld, 16)
        ctx.ckpt(gpsimd, wld, wldc, ("wload", 0))

    for eng in ("tensor", "vector", "scalar", "sync"):
        @getattr(block, eng)
        def _(e):
            for s_, v_ in deps:
                e.wait_ge(ctx.h[s_], v_)

    for e2 in range(16):
        if e2 + 1 < 16:
            @block.gpsimd
            def _(gpsimd, e2=e2):
                if e2 >= 1:
                    ctx.wait(gpsimd, ("woamm", e2 - 1))
                wr = ap["Wo_in"].rearrange("(c p) e -> p c e", p=128)
                ctx.inc(gpsimd.dma_start(out=wo[(e2 + 1) % 2][:].rearrange("p (c e) -> p c e", e=128),
                                         in_=wr[:, :, 128 * (e2 + 1):128 * (e2 + 2)]), wld, 16)
                ctx.ckpt(gpsimd, wld, wldc, ("wload", e2 + 1))

        @block.tensor
        def _(tensor, e2=e2):
            ctx.wait(tensor, ("wload", e2))
            if e2 >= 2:
                ctx.wait(tensor, ("woaev", e2 - 2))
            Av = A_sb[:].rearrange("p (c r) -> p c r", r=16)
            i = None
            for cc in range(16):
                i = nc.tensor.matmul(pw[e2 % 2][:, 0:16], wo[e2 % 2][:, 128 * cc:128 * (cc + 1)],
                                     Av[:, cc, :], start=(cc == 0), stop=(cc == 15))
            v = ctx.inc(i, trn, 1)
            M(("woamm", e2), trn, v)

        @block.vector
        def _(vector, e2=e2):
            ctx.wait(vector, ("woamm", e2))
            i = vector.tensor_copy(woa[:, 16 * e2:16 * (e2 + 1)], pw[e2 % 2][:, 0:16])
            v = ctx.inc(i, c, 1)
            M(("woaev", e2), c, v)

    @block.sync
    def _(sync):
        arr = ap["attnPV_fm"].rearrange("(c p) t -> p c t", p=128)
        h2r = ap["h2_fm"].rearrange("(c p) t -> p c t", p=128)
        ctx.inc(sync.dma_start(out=at[0][:].rearrange("p (c t) -> p c t", t=512),
                               in_=arr[:, :, 0:512]), ld, 16)
        ctx.inc(sync.dma_start(out=h2[0][:].rearrange("p (c t) -> p c t", t=512),
                               in_=h2r[:, :, 0:512]), ld, 16)
        ctx.ckpt(sync, ld, ldc, ("tld", 0))

    for chk in range(8):
        @block.sync
        def _(sync, chk=chk):
            if chk + 1 < 8:
                if chk >= 1:
                    ctx.wait(sync, ("modmm", chk - 1))
                    ctx.wait(sync, ("hamm", chk - 1))
                arr = ap["attnPV_fm"].rearrange("(c p) t -> p c t", p=128)
                h2r = ap["h2_fm"].rearrange("(c p) t -> p c t", p=128)
                ctx.inc(sync.dma_start(out=at[(chk + 1) % 2][:].rearrange("p (c t) -> p c t", t=512),
                                       in_=arr[:, :, 512 * (chk + 1):512 * (chk + 2)]), ld, 16)
                ctx.inc(sync.dma_start(out=h2[(chk + 1) % 2][:].rearrange("p (c t) -> p c t", t=512),
                                       in_=h2r[:, :, 512 * (chk + 1):512 * (chk + 2)]), ld, 16)
                ctx.ckpt(sync, ld, ldc, ("tld", chk + 1))

        @block.tensor
        def _(tensor, chk=chk):
            if chk == 0:
                ctx.wait(tensor, ("woaev", 15))
            ctx.wait(tensor, ("tld", chk))
            if chk >= 1:
                ctx.wait(tensor, ("tev", chk - 1))
            i = None
            for cc in range(16):
                i = nc.tensor.matmul(pm[0:16, 0:512],
                                     woa[:, 16 * cc:16 * (cc + 1)],
                                     at[chk % 2][:, 512 * cc:512 * (cc + 1)],
                                     start=(cc == 0), stop=(cc == 15))
            v = ctx.inc(i, trn, 1)
            M(("modmm", chk), trn, v)
            Av = A_sb[:].rearrange("p (c r) -> p c r", r=16)
            i = None
            for cc in range(16):
                i = nc.tensor.matmul(ph[0:16, 0:512], Av[:, cc, :],
                                     h2[chk % 2][:, 512 * cc:512 * (cc + 1)],
                                     start=(cc == 0), stop=(cc == 15))
            v = ctx.inc(i, trn, 1)
            M(("hamm", chk), trn, v)

        @block.scalar
        def _(scalar, chk=chk):
            ctx.wait(scalar, ("modmm", chk))
            if chk >= 1:
                ctx.wait(scalar, ("mmul", chk - 1))
            scalar.activation(out=tnh[:], in_=pm[0:16, 0:512], func=AF.Tanh)
            ctx.wait(scalar, ("hamm", chk))
            i = scalar.activation(out=hsc[:], in_=ph[0:16, 0:512], func=AF.Copy, scale=0.1)
            v = ctx.inc(i, a, 1)
            M(("tev", chk), a, v)

        @block.vector
        def _(vector, chk=chk):
            ctx.wait(vector, ("tev", chk))
            if chk >= 2:
                ctx.wait(vector, ("mst", chk - 2))
            i = vector.tensor_tensor(out=mmo[chk % 2][:], in0=tnh[:], in1=hsc[:], op=ALU.mult)
            v = ctx.inc(i, c, 1)
            M(("mmul", chk), c, v)

        @block.gpsimd
        def _(gpsimd, chk=chk):
            if chk >= 1:
                ctx.ckpt(gpsimd, st, stc, ("mst", chk - 1))
            ctx.wait(gpsimd, ("mmul", chk))
            ctx.inc(gpsimd.dma_start(out=ap["mm_fm"][:, 512 * chk:512 * (chk + 1)], in_=mmo[chk % 2][:]), st, 16)


def stage7_sems(tag="s7"):
    return ["ld", "wld", "mm", "ev", "c", "a", "trn", "g", "evx", "evy", "st", "ldc", "wldc", "stc", "s10_cc"]




def emit_stage8a(ctx, ap, stack, deps, tag="s8a"):
    ctx.scope = tag
    nc, block = ctx.nc, ctx.block
    sb = lambda n, s, d: alloc_sbuf(nc, stack, f"{tag}_{n}", s, d)
    psa = lambda n, s, d: alloc_psum(nc, stack, f"{tag}_{n}", s, d)

    TG = 1024
    NTG = T // TG

    h2 = [sb(f"h2{j}", [128, 16 * TG], BF16) for j in range(2)]
    mm = [sb(f"mm{j}", [16, TG], BF16) for j in range(2)]
    B_sb = sb("B", [16, 8192], BF16)
    wi = [sb(f"wi{j}", [128, 2048], BF16) for j in range(2)]
    wu = [sb(f"wu{j}", [128, 2048], BF16) for j in range(2)]
    gsb = [sb(f"gsb{j}", [128, 512], F32) for j in range(2)]
    gu = [sb(f"gu{j}", [128, TG], BF16) for j in range(2)]
    pg = [psa(f"pg{j}", [128, 512], F32) for j in range(4)]
    pu = [psa(f"pu{j}", [128, 512], F32) for j in range(4)]

    ld, wld, c, a, trn, st = ("ld", "wld", "c", "a", "trn", "st")
    ldc, wldc, stc = ("ldc", "wldc", "stc")
    M = ctx.mark

    @block.gpsimd
    def _(gpsimd):
        for s_, v_ in deps:
            gpsimd.wait_ge(ctx.h[s_], v_)
        ctx.inc(gpsimd.dma_start(out=B_sb[:], in_=ap["B_in"]), wld, 16)
        for w_sb, wt in ((wi[0], "W_inT"), (wu[0], "W_upT")):
            wr = ap[wt].rearrange("(c p) f -> p c f", p=128)
            ctx.inc(gpsimd.dma_start(out=w_sb[:].rearrange("p (c f) -> p c f", f=128),
                                     in_=wr[:, :, 0:128]), wld, 16)
        ctx.ckpt(gpsimd, wld, wldc, ("wload", 0, 0))

    for eng in ("tensor", "vector", "scalar", "sync"):
        @getattr(block, eng)
        def _(e):
            for s_, v_ in deps:
                e.wait_ge(ctx.h[s_], v_)

    @block.sync
    def _(sync):
        h2r = ap["h2_fm"].rearrange("(c p) t -> p c t", p=128)
        ctx.inc(sync.dma_start(out=h2[0][:].rearrange("p (c t) -> p c t", t=TG),
                               in_=h2r[:, :, 0:TG]), ld, 16)
        ctx.inc(sync.dma_start(out=mm[0][:], in_=ap["mm_fm"][:, 0:TG]), ld, 16)
        ctx.ckpt(sync, ld, ldc, ("xld", 0))

    for tg in range(NTG):
        if tg + 1 < NTG:
            @block.sync
            def _(sync, tg=tg):
                if tg >= 1:
                    ctx.wait(sync, ("gmm", tg - 1, 63))
                h2r = ap["h2_fm"].rearrange("(c p) t -> p c t", p=128)
                ctx.inc(sync.dma_start(out=h2[(tg + 1) % 2][:].rearrange("p (c t) -> p c t", t=TG),
                                       in_=h2r[:, :, TG * (tg + 1):TG * (tg + 2)]), ld, 16)
                ctx.inc(sync.dma_start(out=mm[(tg + 1) % 2][:], in_=ap["mm_fm"][:, TG * (tg + 1):TG * (tg + 2)]), ld, 16)
                ctx.ckpt(sync, ld, ldc, ("xld", tg + 1))

        for fc in range(64):
            nfc = fc + 1 if fc + 1 < 64 else (0 if tg + 1 < NTG else None)
            ntg = tg if fc + 1 < 64 else tg + 1
            if nfc is not None:
                @block.gpsimd
                def _(gpsimd, tg=tg, fc=fc, nfc=nfc, ntg=ntg):
                    pr_tg, pr_fc = (tg, fc - 1) if fc >= 1 else (tg - 1, 63)
                    if pr_fc >= 0 and pr_tg >= 0:
                        ctx.wait(gpsimd, ("gmm", pr_tg, pr_fc))
                    if ntg == 0:
                        for w_sb, wt in ((wi[(nfc) % 2], "W_inT"), (wu[(nfc) % 2], "W_upT")):
                            wr = ap[wt].rearrange("(c p) f -> p c f", p=128)
                            ctx.inc(gpsimd.dma_start(out=w_sb[:].rearrange("p (c f) -> p c f", f=128),
                                                     in_=wr[:, :, 128 * nfc:128 * (nfc + 1)]), wld, 16)
                    else:
                        for w_sb, wt in ((wi[(nfc) % 2], "win_b"), (wu[(nfc) % 2], "wup_b")):
                            ctx.inc(gpsimd.dma_start(out=w_sb[:], in_=ap[wt][nfc, :, :]), wld, 16)
                    ctx.ckpt(gpsimd, wld, wldc, ("wload", ntg, nfc))

            @block.tensor
            def _(tensor, tg=tg, fc=fc):
                ctx.wait(tensor, ("wload", tg, fc))
                if fc == 0:
                    ctx.wait(tensor, ("xld", tg))
                i = None
                for h in range(2):
                    pr_tg, pr_fc = (tg, fc - 2) if fc >= 2 else ((tg - 1, 62 + fc) if tg >= 1 else (None, None))
                    if pr_tg is not None:
                        ctx.wait(tensor, ("guev", pr_tg, pr_fc, h))
                    bank = 2 * (fc % 2) + h
                    for cc in range(16):
                        nc.tensor.matmul(pg[bank][:], wi[fc % 2][:, 128 * cc:128 * (cc + 1)],
                                         h2[tg % 2][:, TG * cc + 512 * h:TG * cc + 512 * (h + 1)],
                                         start=(cc == 0), stop=False)
                    nc.tensor.matmul(pg[bank][:], B_sb[:, 128 * fc:128 * (fc + 1)],
                                     mm[tg % 2][:, 512 * h:512 * (h + 1)], start=False, stop=True)
                    for cc in range(16):
                        i = nc.tensor.matmul(pu[bank][:], wu[fc % 2][:, 128 * cc:128 * (cc + 1)],
                                             h2[tg % 2][:, TG * cc + 512 * h:TG * cc + 512 * (h + 1)],
                                             start=(cc == 0), stop=(cc == 15))
                    v = ctx.inc(i, trn, 1)
                    M(("gmm_h", tg, fc, h), trn, v)
                M(("gmm", tg, fc), trn, ctx.n[trn])

            @block.scalar
            def _(scalar, tg=tg, fc=fc):
                for h in range(2):
                    ctx.wait(scalar, ("gmm_h", tg, fc, h))
                    if not (tg == 0 and fc == 0):
                        pr = (tg, fc - 1) if fc >= 1 else (tg - 1, 63)
                        ctx.wait(scalar, ("gumul", pr[0], pr[1], h))
                    i = scalar.activation(out=gsb[h][:], in_=pg[2 * (fc % 2) + h][:], func=AF.Silu)
                    v = ctx.inc(i, a, 1)
                    M(("silu", tg, fc, h), a, v)

            @block.vector
            def _(vector, tg=tg, fc=fc):
                for h in range(2):
                    ctx.wait(vector, ("silu", tg, fc, h))
                    if fc >= 2 or tg >= 1:
                        pr_tg, pr_fc = (tg, fc - 2) if fc >= 2 else (tg - 1, 62 + fc)
                        ctx.wait(vector, ("gust", pr_tg, pr_fc))
                    i = vector.tensor_tensor(out=gu[fc % 2][:, 512 * h:512 * (h + 1)],
                                             in0=pu[2 * (fc % 2) + h][:], in1=gsb[h][:], op=ALU.mult)
                    v = ctx.inc(i, c, 1)
                    M(("gumul", tg, fc, h), c, v)
                    M(("guev", tg, fc, h), c, v)

            @block.gpsimd
            def _(gpsimd, tg=tg, fc=fc):
                if fc >= 1 or tg >= 1:
                    pr = (tg, fc - 1) if fc >= 1 else (tg - 1, 63)
                    ctx.ckpt(gpsimd, st, stc, ("gust", pr[0], pr[1]))
                ctx.wait(gpsimd, ("gumul", tg, fc, 1))
                ctx.inc(gpsimd.dma_start(out=ap["gu_fm"][128 * fc:128 * (fc + 1), TG * tg:TG * (tg + 1)],
                                         in_=gu[fc % 2][:]), st, 16)

            if tg == 0:
                @block.gpsimd
                def _(gpsimd, fc=fc):
                    ctx.inc(gpsimd.dma_start(out=ap["win_b"][fc, :, :], in_=wi[fc % 2][:]), st, 16)
                    ctx.inc(gpsimd.dma_start(out=ap["wup_b"][fc, :, :], in_=wu[fc % 2][:]), st, 16)


def emit_stage8c(ctx, ap, stack, deps, tag="s8c"):
    ctx.scope = tag
    nc, block = ctx.nc, ctx.block
    sb = lambda n, s, d: alloc_sbuf(nc, stack, f"{tag}_{n}", s, d)
    psa = lambda n, s, d: alloc_psum(nc, stack, f"{tag}_{n}", s, d)

    wdn = [sb(f"wdn{j}", [128, 16 * 2048], BF16) for j in range(2)]
    gus = [sb(f"gus{j}", [128, 16 * 512], BF16) for j in range(2)]
    xev = [sb(f"xev{j}", [128, 2048], F32) for j in range(2)]
    px = [psa(f"px{j}", [128, 512], F32) for j in range(8)]

    ld, wld, c, trn, st = ("ld", "wld", "c", "trn", "st")
    ldc, wldc, stc = ("ldc", "wldc", "stc")
    M = ctx.mark

    for eng in ("tensor", "vector", "scalar", "sync", "gpsimd"):
        @getattr(block, eng)
        def _(e):
            for s_, v_ in deps:
                e.wait_ge(ctx.h[s_], v_)

    @block.gpsimd
    def _(gpsimd):
        wr = ap["W_downT"].rearrange("(b c p) d -> b p c d", p=128, c=16)
        ctx.inc(gpsimd.dma_start(out=wdn[0][:].rearrange("p (c d) -> p c d", d=2048),
                                 in_=wr[0, :, :, :]), wld, 16)
        ctx.ckpt(gpsimd, wld, wldc, ("wdload", 0))

    for fcb in range(4):

        for g4 in range(8):
            @block.sync
            def _(sync, fcb=fcb, g4=g4):
                if g4 == 0 and fcb == 0:
                    gur = ap["gu_fm"].rearrange("(b c p) t -> b p c t", p=128, c=16)
                    ctx.inc(sync.dma_start(out=gus[0][:].rearrange("p (c t) -> p c t", t=512),
                                           in_=gur[0, :, :, 0:512]), ld, 16)
                    ctx.ckpt(sync, ld, ldc, ("guld", 0, 0))
                nfcb, ng4 = (fcb, g4 + 1) if g4 + 1 < 8 else (fcb + 1, 0)
                if nfcb < 4:
                    if g4 >= 1 or fcb >= 1:
                        pr2 = (fcb, g4 - 1) if g4 >= 1 else (fcb - 1, 7)
                        ctx.wait(sync, ("xmm", pr2[0], pr2[1], 3))
                    gur = ap["gu_fm"].rearrange("(b c p) t -> b p c t", p=128, c=16)
                    ctx.inc(sync.dma_start(out=gus[ng4 % 2][:].rearrange("p (c t) -> p c t", t=512),
                                           in_=gur[nfcb, :, :, 512 * ng4:512 * (ng4 + 1)]), ld, 16)
                    ctx.ckpt(sync, ld, ldc, ("guld", nfcb, ng4))

            for ttl in range(4):
                tt = 4 * g4 + ttl

                @block.tensor
                def _(tensor, fcb=fcb, g4=g4, ttl=ttl, tt=tt):
                    if ttl == 0:
                        ctx.wait(tensor, ("guld", fcb, g4))
                        if g4 == 0:
                            ctx.wait(tensor, ("wdload", fcb))
                        if g4 == 1 and fcb == 0:
                            pass
                    i = None
                    for nb in range(4):
                        key = ("xev", fcb, tt - 2, nb) if tt >= 2 else \
                              (("xev", fcb - 1, 30 + tt, nb) if fcb >= 1 else None)
                        if key is not None:
                            ctx.wait(tensor, key)
                        for cc in range(16):
                            i = nc.tensor.matmul(px[4 * (tt % 2) + nb][:],
                                                 gus[g4 % 2][:, 512 * cc + 128 * ttl:512 * cc + 128 * (ttl + 1)],
                                                 wdn[fcb % 2][:, 2048 * cc + 512 * nb:2048 * cc + 512 * (nb + 1)],
                                                 start=(cc == 0), stop=(cc == 15))
                    v = ctx.inc(i, trn, 1)
                    M(("xmm", fcb, g4, ttl), trn, v)

                @block.vector
                def _(vector, fcb=fcb, g4=g4, tt=tt, ttl=ttl):
                    ctx.wait(vector, ("xmm", fcb, g4, ttl))
                    if tt >= 2 or fcb >= 1:
                        pr = (fcb, tt - 2) if tt >= 2 else (fcb - 1, 30 + tt)
                        ctx.wait(vector, ("xst", pr[0], pr[1]))
                    i = None
                    for nb in range(4):
                        i = vector.tensor_copy(xev[tt % 2][:, 512 * nb:512 * (nb + 1)], px[4 * (tt % 2) + nb][:])
                        v = ctx.inc(i, c, 1)
                        M(("xev", fcb, tt, nb), c, v)

                @block.gpsimd
                def _(gpsimd, fcb=fcb, tt=tt):
                    if tt >= 1 or fcb >= 1:
                        pr = (fcb, tt - 1) if tt >= 1 else (fcb - 1, 31)
                        ctx.ckpt(gpsimd, st, stc, ("xst", pr[0], pr[1]))
                    ctx.wait(gpsimd, ("xev", fcb, tt, 3))
                    ctx.inc(gpsimd.dma_start(out=ap["x1_tm"][128 * tt:128 * (tt + 1), :],
                                             in_=xev[tt % 2][:], accum_op=ALU.add), st, 16)
                if fcb == 3 and tt == 31:
                    @block.gpsimd
                    def _(gpsimd, fcb=fcb, tt=tt):
                        ctx.ckpt(gpsimd, st, stc, ("xst", fcb, tt))

            if g4 == 0 and fcb + 1 < 4:
                @block.gpsimd
                def _(gpsimd, fcb=fcb):
                    if fcb >= 1:
                        ctx.wait(gpsimd, ("xmm", fcb - 1, 7, 3))
                    wr = ap["W_downT"].rearrange("(b c p) d -> b p c d", p=128, c=16)
                    ctx.inc(gpsimd.dma_start(out=wdn[(fcb + 1) % 2][:].rearrange("p (c d) -> p c d", d=2048),
                                             in_=wr[fcb + 1, :, :, :]), wld, 16)
            if g4 == 6 and fcb + 1 < 4:
                @block.gpsimd
                def _(gpsimd, fcb=fcb):
                    ctx.ckpt(gpsimd, wld, wldc, ("wdload", fcb + 1))


def stage8a_sems(tag="s8a"):
    return ["ld", "wld", "mm", "ev", "c", "a", "trn", "g", "evx", "evy", "st", "ldc", "wldc", "stc", "s10_cc"]


def stage8c_sems(tag="s8c"):
    return ["ld", "wld", "mm", "ev", "c", "a", "trn", "g", "evx", "evy", "st", "ldc", "wldc", "stc", "s10_cc"]



NB = 7


def emit_stage9(ctx, ap, stack, deps=(), tag="s9"):
    ctx.scope = tag
    nc, block = ctx.nc, ctx.block
    sb = lambda n, s, d: alloc_sbuf(nc, stack, f"{tag}_{n}", s, d)
    ps = lambda n, s, d: alloc_psum(nc, stack, f"{tag}_{n}", s, d)

    x1_tm, xln_tm = ap["x1_tm"], ap["x_ln_tm"]
    out_fm = {"xn": ap["xn1_fm"], "y": ap["x_n_fm"]}

    xt = [sb(f"xt{j}", [128, D], F32) for j in range(2)]
    xn = [sb(f"xn{j}", [128, D], F32) for j in range(3)]
    z = [sb(f"z{j}", [128, D], F32) for j in range(2)]
    zb = [sb(f"zb{j}", [128, D], F32) for j in range(2)]
    zlb = [sb(f"zlb{j}", [128, D], BF16) for j in range(2)]
    xnb = [sb(f"xnb{j}", [128, D], BF16) for j in range(NB)]
    yb = [sb(f"yb{j}", [128, D], BF16) for j in range(NB)]
    sq = sb("sq", [128, D], F32)
    gB = sb("gB", [128, D], BF16)
    bB = sb("bB", [128, D], BF16)
    idb = sb("idb", [128, 128], BF16)
    scB = sb("scB", [128, 1], F32)
    va = [sb(f"va{j}", [128, 1], F32) for j in range(2)]
    vm = [sb(f"vm{j}", [128, 1], F32) for j in range(2)]
    vs = [sb(f"vs{j}", [128, 1], F32) for j in range(2)]
    vb = [sb(f"vb{j}", [128, 1], F32) for j in range(2)]
    vm2 = [sb(f"vm2{j}", [128, 1], F32) for j in range(2)]
    vs2 = [sb(f"vs2{j}", [128, 1], F32) for j in range(2)]
    vb2 = [sb(f"vb2{j}", [128, 1], F32) for j in range(2)]
    vs3 = [sb(f"vs3{j}", [128, 1], F32) for j in range(2)]
    vb3 = [sb(f"vb3{j}", [128, 1], F32) for j in range(2)]
    vc = sb("vc", [128, 1], F32)
    vc2 = sb("vc2", [128, 1], F32)
    vc3 = sb("vc3", [128, 1], F32)
    sn = sb("sn", [128, 1], F32)
    veps5 = sb("veps5", [128, 1], F32)
    dmy = [sb(f"dmy{j}", [128, 32], F32) for j in range(2)]
    ex = {s: sb(f"ex_{s}", [128, NC * 512], BF16) for s in ("xn", "y")}
    psx = [ps(f"psx{j}", [128, 512], BF16) for j in range(2)]
    psy = [ps(f"psy{j}", [128, 512], BF16) for j in range(2)]

    sepv = lambda eng: eng.tensor_copy(dmy[1][:], dmy[0][:])
    sepa = lambda eng: eng.activation(out=dmy[1][:], in_=dmy[0][:], func=AF.Copy)

    ld, c, a, trn, evx, evy, st = ("ld", "c", "a", "trn", "evx", "evy", "st")
    ldc, stc = "ldc", "stc"
    M = ctx.mark
    NG = NT // 4

    for eng in ("tensor", "vector", "scalar", "gpsimd"):
        @getattr(block, eng)
        def _(e):
            for s_, v_ in deps:
                e.wait_ge(ctx.h[s_], v_)

    @block.sync
    def _(sync):
        for s_, v_ in deps:
            sync.wait_ge(ctx.h[s_], v_)
        for nm, tile in (("identb", idb), ("gammaB", gB), ("betaB", bB), ("scaleB", scB)):
            ctx.inc(sync.dma_start(out=tile[:], in_=ap[nm]), ld, 16)
        ctx.inc(sync.dma_start(out=xt[0][:], in_=x1_tm[0:128, :]), ld, 16)
        ctx.ckpt(sync, ld, ldc, ("ld_tile", 0))
        M(("bcev", 2), ldc, ctx.n[ldc])

    @block.vector
    def _(vector):
        vector.memset(veps5[:], 1e-5)
        vector.memset(dmy[0][:], 0.0)

    for t in range(NT + 8):
        g = (t - 7) // 4 if (t - 7) % 4 == 0 and 0 <= (t - 7) // 4 < NG else None

        if g is not None:
            if g > 0:
                @block.gpsimd
                def _(gpsimd, g=g):
                    ctx.ckpt(gpsimd, st, stc, ("st_f", g - 1))
            for ch in range(NC):
                @block.tensor
                def _(tensor, g=g, ch=ch):
                    if ch == 0:
                        ctx.wait(tensor, ("pD", 4 * g + 3))
                        ctx.wait(tensor, ("pG", 4 * g + 3))
                        ctx.wait(tensor, ("xnbC", 4 * g + 3))
                    for strm, src, bank in (("xn", xnb, psx), ("y", yb, psy)):
                        key = ("ev_" + strm, g, ch - 2) if ch >= 2 else (("ev_" + strm, g - 1, NC - 2 + ch) if g > 0 else None)
                        if key is not None:
                            ctx.wait(tensor, key)
                        p = bank[ch % 2]
                        i = None
                        for j in range(4):
                            i = nc.tensor.transpose(p[:, 128 * j:128 * (j + 1)],
                                                    src[(4 * g + j) % NB][:, 128 * ch:128 * (ch + 1)],
                                                    idb[:])
                        v = ctx.inc(i, trn, 1)
                        M(("trn", g, ch, strm), trn, v)

                @block.vector
                def _(vector, g=g, ch=ch):
                    ctx.wait(vector, ("trn", g, ch, "xn"))
                    if g > 0:
                        ctx.wait(vector, ("st_f", g - 1))
                    i = vector.tensor_copy(ex["xn"][:, 512 * ch:512 * (ch + 1)], psx[ch % 2][:])
                    v = ctx.inc(i, evx, 1)
                    M(("ev_xn", g, ch), evx, v)

                @block.scalar
                def _(scalar, g=g, ch=ch):
                    ctx.wait(scalar, ("trn", g, ch, "y"))
                    if g > 0:
                        ctx.wait(scalar, ("st_f", g - 1))
                    i = scalar.activation(out=ex["y"][:, 512 * ch:512 * (ch + 1)], in_=psy[ch % 2][:],
                                          func=AF.Copy)
                    v = ctx.inc(i, evy, 1)
                    M(("ev_y", g, ch), evy, v)

                @block.gpsimd
                def _(gpsimd, g=g, ch=ch):
                    ctx.wait(gpsimd, ("ev_xn", g, ch))
                    ctx.inc(gpsimd.dma_start(out=out_fm["xn"][128 * ch:128 * (ch + 1), 512 * g:512 * (g + 1)],
                                             in_=ex["xn"][:, 512 * ch:512 * (ch + 1)]), st, 16)
                    ctx.wait(gpsimd, ("ev_y", g, ch))
                    ctx.inc(gpsimd.dma_start(out=out_fm["y"][128 * ch:128 * (ch + 1), 512 * g:512 * (g + 1)],
                                             in_=ex["y"][:, 512 * ch:512 * (ch + 1)]), st, 16)

        if t + 1 < NT or t == NT - 1:
            @block.sync
            def _(sync, t=t):
                tt = t + 1
                if tt < NT:
                    if tt >= 2:
                        ctx.wait(sync, ("pB", tt - 2))
                    ctx.inc(sync.dma_start(out=xt[tt % 2][:], in_=x1_tm[128 * tt:128 * (tt + 1), :]), ld, 16)
                if 1 <= t < NT:
                    ctx.ckpt(sync, ld, ldc, ("ld_tile", t))

        @block.vector
        def _(vector, t=t):
            if t < NT:
                ctx.wait(vector, ("ld_tile", t))
                if t >= 2:
                    ctx.wait(vector, ("pB", t - 2))
                if t >= 3:
                    ctx.wait(vector, ("pD", t - 3))
                    ctx.wait(vector, ("xnbC", t - 3))
                if t == 0:
                    ctx.wait(vector, ("bcev", 2))
                vector.tensor_reduce(va[t % 2][:], xt[t % 2][:], axis=AX.X, op=ALU.add)
                sepv(vector)
                i = vector.tensor_scalar(out=vm[t % 2][:], in0=va[t % 2][:], scalar1=-1.0 / D, scalar2=None, op0=ALU.mult)
                v = ctx.inc(i, c, 1)
                M(("pA", t), c, v)
            if 0 <= t - 1 < NT:
                u = t - 1
                ctx.wait(vector, ("pB", u))
                vector.reciprocal(out=vc[:], in_=vb[u % 2][:])
                sepv(vector)
                vector.tensor_scalar(out=xn[u % 3][:], in0=xn[u % 3][:], scalar1=vc[:], scalar2=None, op0=ALU.mult)
                sepv(vector)
                vector.tensor_reduce(va[u % 2][:], xn[u % 3][:], axis=AX.X, op=ALU.add)
                sepv(vector)
                i = vector.tensor_scalar(out=vm2[u % 2][:], in0=va[u % 2][:], scalar1=-1.0 / D, scalar2=None, op0=ALU.mult)
                v = ctx.inc(i, c, 1)
                M(("pC", u), c, v)
            if 0 <= t - 2 < NT:
                u = t - 2
                ctx.wait(vector, ("pD", u))
                vector.reciprocal(out=vc2[:], in_=vb2[u % 2][:])
                sepv(vector)
                vector.tensor_scalar(out=z[u % 2][:], in0=z[u % 2][:], scalar1=vc2[:], scalar2=None, op0=ALU.mult)
                if u >= 2:
                    ctx.wait(vector, ("pG", u - 2))
                vector.tensor_tensor(out=zb[u % 2][:], in0=z[u % 2][:], in1=gB[:], op=ALU.mult)
                sepv(vector)
                i = vector.tensor_tensor(out=zb[u % 2][:], in0=zb[u % 2][:], in1=bB[:], op=ALU.add)
                v = ctx.inc(i, c, 1)
                M(("pE", u), c, v)
            if 0 <= t - 3 < NT:
                u = t - 3
                ctx.wait(vector, ("pF", u))
                vector.tensor_scalar(out=vc3[:], in0=vb3[u % 2][:], scalar1=1e-12, scalar2=None, op0=ALU.max)
                sepv(vector)
                vector.reciprocal(out=sn[:], in_=vc3[:])
                sepv(vector)
                vector.tensor_scalar(out=vc3[:], in0=sn[:], scalar1=scB[:], scalar2=None, op0=ALU.mult)
                if u >= 2:
                    ctx.wait(vector, ("st_tm", u - 2))
                vector.tensor_copy(zlb[u % 2][:], zb[u % 2][:])
                if u >= NB:
                    ctx.wait(vector, ("trn", (u - NB) // 4, NC - 1, "y"))
                sepv(vector)
                i = vector.tensor_scalar(out=yb[u % NB][:], in0=zb[u % 2][:], scalar1=vc3[:], scalar2=None, op0=ALU.mult)
                v = ctx.inc(i, c, 1)
                M(("pG", u), c, v)

        @block.scalar
        def _(scalar, t=t):
            if t < NT:
                ctx.wait(scalar, ("pA", t))
                scalar.activation(out=xn[t % 3][:], in_=xt[t % 2][:], func=AF.Identity, bias=vm[t % 2][:])
                scalar.activation(out=sq[:], in_=xn[t % 3][:], func=AF.Square, accum_out=vs[t % 2][:])
                sepa(scalar)
                i = scalar.activation(out=vb[t % 2][:], in_=vs[t % 2][:], func=AF.Sqrt, scale=1.0 / D, bias=veps5[:])
                v = ctx.inc(i, a, 1)
                M(("pB", t), a, v)
            if 0 <= t - 1 < NT:
                u = t - 1
                ctx.wait(scalar, ("pC", u))
                if u >= 2:
                    ctx.wait(scalar, ("pE", u - 2))
                scalar.activation(out=z[u % 2][:], in_=xn[u % 3][:], func=AF.Identity, bias=vm2[u % 2][:])
                scalar.activation(out=sq[:], in_=z[u % 2][:], func=AF.Square, accum_out=vs2[u % 2][:])
                i = scalar.activation(out=vb2[u % 2][:], in_=vs2[u % 2][:], func=AF.Sqrt, scale=1.0 / D, bias=veps5[:])
                v = ctx.inc(i, a, 1)
                M(("pD", u), a, v)
            if 0 <= t - 2 < NT:
                u = t - 2
                ctx.wait(scalar, ("pE", u))
                scalar.activation(out=sq[:], in_=zb[u % 2][:], func=AF.Square, accum_out=vs3[u % 2][:])
                i = scalar.activation(out=vb3[u % 2][:], in_=vs3[u % 2][:], func=AF.Sqrt)
                v = ctx.inc(i, a, 1)
                M(("pF", u), a, v)

        if 0 <= t - 1 < NT:
            @block.gpsimd
            def _(gpsimd, t=t):
                u = t - 1
                ctx.wait(gpsimd, ("pC", u))
                if u >= NB:
                    ctx.wait(gpsimd, ("trn", (u - NB) // 4, NC - 1, "y"))
                i = gpsimd.tensor_copy(xnb[u % NB][:], xn[u % 3][:])
                v = ctx.inc(i, "g", 1)
                M(("xnbC", u), "g", v)

        if 0 <= t - 3 < NT:
            @block.gpsimd
            def _(gpsimd, t=t):
                u = t - 3
                if u >= 1:
                    ctx.ckpt(gpsimd, st, stc, ("st_tm", u - 1))
                ctx.wait(gpsimd, ("pG", u))
                ctx.inc(gpsimd.dma_start(out=xln_tm[128 * u:128 * (u + 1), :], in_=zlb[u % 2][:]), st, 16)


def stage9_sems(tag="s9"):
    return ["ld", "wld", "mm", "ev", "c", "a", "trn", "g", "evx", "evy", "st", "ldc", "wldc", "stc", "s10_cc"]




def emit_stage10kvT(ctx, ap, stack, deps, tag="s10kt"):
    ctx.scope = tag
    nc, block = ctx.nc, ctx.block
    sb = lambda n, s, d: alloc_sbuf(nc, stack, f"{tag}_{n}", s, d)
    psa = lambda n, s, d: alloc_psum(nc, stack, f"{tag}_{n}", s, d)

    qkv = ap["qkv1L_fm"]
    kv = [sb(f"kv{j}", [128, T], BF16) for j in range(2)]
    ev = [sb(f"ev{j}", [128, 512], BF16) for j in range(2)]
    idb = sb("idb", [128, 128], BF16)
    pst = [psa(f"pst{j}", [128, 512], BF16) for j in range(2)]

    ld, c, a, trn, st = ("ld", "c", "a", "trn", "st")
    ldc, stc = "ldc", "stc"
    M = ctx.mark

    @block.sync
    def _(sync):
        for s_, v_ in deps:
            sync.wait_ge(ctx.h[s_], v_)
        ctx.inc(sync.dma_start(out=idb[:], in_=ap["identb"]), ld, 16)

    for eng in ("tensor", "vector", "scalar", "gpsimd"):
        @getattr(block, eng)
        def _(e):
            for s_, v_ in deps:
                e.wait_ge(ctx.h[s_], v_)

    for ks in range(8):
        kvh, isv = ks // 2, ks % 2
        r0 = 128 * ks
        col0 = 512 * isv + 128 * kvh

        @block.sync
        def _(sync, ks=ks, r0=r0):
            if ks >= 1:
                ctx.wait(sync, ("kvtr", ks - 1, 7))
            ctx.inc(sync.dma_start(out=kv[ks % 2][:], in_=qkv[r0:r0 + 128, :]), ld, 16)
            ctx.ckpt(sync, ld, ldc, ("kvld", ks))

        for g4 in range(8):
            @block.tensor
            def _(tensor, ks=ks, g4=g4):
                if g4 == 0:
                    ctx.wait(tensor, ("kvld", ks))
                key = ("kve", ks, g4 - 2) if g4 >= 2 else (("kve", ks - 1, 6 + g4) if ks >= 1 else None)
                if key is not None:
                    ctx.wait(tensor, key)
                i = None
                for jj in range(4):
                    j = 4 * g4 + jj
                    i = nc.tensor.transpose(pst[g4 % 2][:, 128 * jj:128 * (jj + 1)],
                                            kv[ks % 2][:, 128 * j:128 * (j + 1)], idb[:])
                v = ctx.inc(i, trn, 1)
                M(("kvtr", ks, g4), trn, v)

            @block.vector
            def _(vector, ks=ks, g4=g4):
                ctx.wait(vector, ("kvtr", ks, g4))
                key = ("kvst", ks, g4 - 2) if g4 >= 2 else (("kvst", ks - 1, 6 + g4) if ks >= 1 else None)
                if key is not None:
                    ctx.wait(vector, key)
                i = vector.tensor_copy(ev[g4 % 2][:], pst[g4 % 2][:])
                v = ctx.inc(i, c, 1)
                M(("kve", ks, g4), c, v)

            @block.gpsimd
            def _(gpsimd, ks=ks, g4=g4, col0=col0):
                if not (ks == 0 and g4 == 0):
                    pk, pg = (ks, g4 - 1) if g4 >= 1 else (ks - 1, 7)
                    ctx.ckpt(gpsimd, st, stc, ("kvst", pk, pg))
                ctx.wait(gpsimd, ("kve", ks, g4))
                dst = ap["kv_tm_loc"].rearrange("(J P) f -> P J f", P=128)
                i = gpsimd.dma_start(out=dst[:, 4 * g4:4 * (g4 + 1), col0:col0 + 128],
                                     in_=ev[g4 % 2][:].rearrange("p (j d) -> p j d", d=128))
                ctx.inc(i, st, 16)
        if ks == 7:
            @block.gpsimd
            def _(gpsimd, ks=ks):
                ctx.ckpt(gpsimd, st, stc, ("kvst", 7, 7))

    @block.gpsimd
    def _(gpsimd):
        gpsimd.wait_ge(ctx.h[stc], ctx.n[stc])
        i = gpsimd.collective_compute(
            "AllGather", ALU.bypass,
            replica_groups=[list(range(8))],
            ins=[ap["kv_tm_loc"].opt()],
            outs=[ap["kv_all"].opt()],
        )
        ctx.inc_cc(i, "s10_cc")


def emit_stage10qre(ctx, ap, stack, deps, tag="s10qr"):
    ctx.scope = tag
    nc, block = ctx.nc, ctx.block
    sb = lambda n, s, d: alloc_sbuf(nc, stack, f"{tag}_{n}", s, d)

    qkv = ap["qkv1L_fm"]
    ib = [sb(f"ib{j}", [128, T], BF16) for j in range(2)]
    ob = [sb(f"ob{j}", [128, T], BF16) for j in range(2)]

    ld, c, a, trn, st = ("ld", "c", "a", "trn", "st")
    ldc, stc = "ldc", "stc"
    M = ctx.mark

    @block.sync
    def _(sync):
        for s_, v_ in deps:
            sync.wait_ge(ctx.h[s_], v_)
        ctx.inc(sync.dma_start(out=ib[0][:], in_=qkv[1024:1152, :]), ld, 16)
        ctx.ckpt(sync, ld, ldc, ("qld", 0))

    for eng in ("tensor", "vector", "scalar", "gpsimd"):
        @getattr(block, eng)
        def _(e):
            for s_, v_ in deps:
                e.wait_ge(ctx.h[s_], v_)

    for qh in range(16):
        if qh + 1 < 16:
            @block.sync
            def _(sync, qh=qh):
                if qh >= 1:
                    ctx.wait(sync, ("qre", qh - 1))
                r0 = 1024 + 128 * (qh + 1)
                ctx.inc(sync.dma_start(out=ib[(qh + 1) % 2][:], in_=qkv[r0:r0 + 128, :]), ld, 16)
                ctx.ckpt(sync, ld, ldc, ("qld", qh + 1))

        @block.vector
        def _(vector, qh=qh):
            ctx.wait(vector, ("qld", qh))
            if qh >= 2:
                ctx.wait(vector, ("qst", qh - 2))
            iv = ib[qh % 2][:].rearrange("p (s b) -> p s b", s=2).transpose([0, 2, 1])
            ov = ob[qh % 2][:].rearrange("p (b s) -> p b s", s=2)
            i = vector.tensor_copy(ov, iv)
            v = ctx.inc(i, c, 1)
            M(("qre", qh), c, v)

        @block.gpsimd
        def _(gpsimd, qh=qh):
            if qh >= 1:
                ctx.ckpt(gpsimd, st, stc, ("qst", qh - 1))
            ctx.wait(gpsimd, ("qre", qh))
            ctx.inc(gpsimd.dma_start(out=ap["q_bmaj"][128 * qh:128 * (qh + 1), :], in_=ob[qh % 2][:]), st, 16)
    @block.gpsimd
    def _(gpsimd):
        ctx.ckpt(gpsimd, st, stc, ("qst", 15))


def stage10b_sems(tag="s10b"):
    return ["ld", "wld", "mm", "ev", "c", "a", "trn", "g", "evx", "evy", "st", "ldc", "wldc", "stc", "s10_cc"]




def emit_stage11(ctx, ap, stack, deps, tag="s11"):
    ctx.scope = tag
    nc, block = ctx.nc, ctx.block
    sb = lambda n, s, d: alloc_sbuf(nc, stack, f"{tag}_{n}", s, d)
    psa = lambda n, s, d: alloc_psum(nc, stack, f"{tag}_{n}", s, d)

    kv = [[sb(f"kv{j}_{x}", [128, 1024], BF16) for j in range(8)] for x in range(2)]
    qg = [sb(f"qg{x}", [128, 2048], BF16) for x in range(2)]
    kp = [sb(f"kp{x}", [128, 1024], BF16) for x in range(2)]
    stb = [sb(f"stb{x}", [128, 512], F32) for x in range(2)]
    pb = [sb(f"pb{x}", [128, 512], BF16) for x in range(2)]
    ptb = [sb(f"ptb{x}", [128, 512], BF16) for x in range(2)]
    osb = [sb(f"osb{x}", [128, 512], BF16) for x in range(2)]
    orb = [sb(f"orb{x}", [128, 512], BF16) for x in range(2)]
    nm4 = [sb(f"nm4{x}", [128, 4], F32) for x in range(2)]
    l4 = [sb(f"l4{x}", [128, 4], F32) for x in range(2)]
    li4 = [sb(f"li4{x}", [128, 4], F32) for x in range(2)]
    idb = sb("idb", [128, 128], BF16)
    idf = sb("idf", [128, 128], F32)
    msk = sb("msk", [128, 64], BF16)
    dmy = [sb(f"dmy{j}", [128, 32], F32) for j in range(2)]

    pkt = [psa(f"pkt{x}", [128, 512], BF16) for x in range(2)]
    pst_ = psa("pst", [128, 512], F32)
    ps_ = psa("ps", [128, 512], F32)
    ppt = psa("ppt", [128, 512], BF16)
    po = psa("po", [128, 512], F32)

    ld, c, a, trn, st = ("ld", "c", "a", "trn", "st")
    ldc, stc = "ldc", "stc"
    M = ctx.mark
    sepv = lambda e: e.tensor_copy(dmy[1][:], dmy[0][:])
    kva = ap["kv_all"].rearrange("(s n) f -> s n f", s=16)

    @block.sync
    def _(sync):
        for s_, v_ in deps:
            sync.wait_ge(ctx.h[s_], v_)
        ctx.inc(sync.dma_start(out=idb[:], in_=ap["identb"]), ld, 16)
        ctx.inc(sync.dma_start(out=idf[:], in_=ap["identf"]), ld, 16)
        ctx.inc(sync.dma_start(out=msk[:], in_=ap["mask_l1b"]), ld, 16)
        qr = ap["q_bmaj"].rearrange("(h p) t -> p h t", p=128)
        ctx.inc(sync.dma_start(out=qg[0][:].rearrange("p (h t) -> p h t", t=128),
                               in_=qr[:, :, 0:128]), ld, 16)
        for j in range(8):
            ctx.inc(sync.dma_start(out=kv[0][j][:],
                                   in_=kva[:, 8 * j:8 * (j + 1), :].transpose([1, 0, 2])), ld, 16)
        ctx.ckpt(sync, ld, ldc, ("gld", 0))

    @block.vector
    def _(vector):
        for s_, v_ in deps:
            vector.wait_ge(ctx.h[s_], v_)
        vector.memset(dmy[0][:], 0.0)

    for eng in ("tensor", "scalar", "gpsimd"):
        @getattr(block, eng)
        def _(e):
            for s_, v_ in deps:
                e.wait_ge(ctx.h[s_], v_)

    NIT = 32 * 4
    for it in range(NIT):
        G, kvh = it // 4, it % 4
        bb = (it // 4) % 2
        p2 = it % 2

        if kvh == 0 and G + 1 < 32:
            @block.sync
            def _(sync, G=G):
                if G >= 1:
                    ctx.wait(sync, ("pv", 4 * G - 1))
                qr = ap["q_bmaj"].rearrange("(h p) t -> p h t", p=128)
                ctx.inc(sync.dma_start(out=qg[(G + 1) % 2][:].rearrange("p (h t) -> p h t", t=128),
                                       in_=qr[:, :, 128 * (G + 1):128 * (G + 2)]), ld, 16)
                for j in range(8):
                    ctx.inc(sync.dma_start(out=kv[(G + 1) % 2][j][:],
                                           in_=kva[:, 64 * (G + 1) + 8 * j:64 * (G + 1) + 8 * (j + 1), :].transpose([1, 0, 2])), ld, 16)
                ctx.ckpt(sync, ld, ldc, ("gld", G + 1))

        for g4 in range(2):
            @block.tensor
            def _(tensor, it=it, G=G, kvh=kvh, g4=g4, bb=bb):
                if g4 == 0 and kvh == 0:
                    ctx.wait(tensor, ("gld", G))
                key = ("kpe", it - 1, g4) if it >= 1 else None
                if key is not None:
                    ctx.wait(tensor, key)
                i = None
                for jj in range(4):
                    j = 4 * g4 + jj
                    i = nc.tensor.transpose(pkt[g4][:, 128 * jj:128 * (jj + 1)],
                                            kv[bb][j][:, 128 * kvh:128 * (kvh + 1)], idb[:])
                v = ctx.inc(i, trn, 1)
                M(("ktr", it, g4), trn, v)

            @block.vector
            def _(vector, it=it, g4=g4, p2=p2):
                ctx.wait(vector, ("ktr", it, g4))
                if it >= 2:
                    ctx.wait(vector, ("stmm", it - 2))
                i = vector.tensor_copy(kp[p2][:, 512 * g4:512 * (g4 + 1)], pkt[g4][:])
                v = ctx.inc(i, c, 1)
                M(("kpe", it, g4), c, v)

        @block.tensor
        def _(tensor, it=it, G=G, kvh=kvh, bb=bb, p2=p2):
            ctx.wait(tensor, ("kpe", it, 1))
            if it >= 1:
                ctx.wait(tensor, ("stev", it - 1))
            i = None
            for j in range(8):
                nc.tensor.matmul(pst_[:, 64 * j:64 * (j + 1)],
                                 kp[p2][:, 128 * j:128 * (j + 1)],
                                 qg[bb][:].rearrange("p (h t) -> p h t", t=128)[:, 4 * kvh:4 * kvh + 4, 16 * j:16 * (j + 1)],
                                 start=True, stop=False)
                i = nc.tensor.matmul(pst_[:, 64 * j:64 * (j + 1)], idb[:], msk[:],
                                     start=False, stop=True)
            v = ctx.inc(i, trn, 1)
            M(("stmm", it), trn, v)

        @block.vector
        def _(vector, it=it, p2=p2):
            ctx.wait(vector, ("stmm", it))
            if it >= 2:
                ctx.wait(vector, ("str", it - 2))
            i = vector.tensor_copy(stb[p2][:], pst_[:])
            v = ctx.inc(i, c, 1)
            M(("stev", it), c, v)

        @block.tensor
        def _(tensor, it=it, p2=p2):
            ctx.wait(tensor, ("stev", it))
            if it >= 1:
                ctx.wait(tensor, ("exp", it - 1, 3))
            i = None
            for cc in range(4):
                i = nc.tensor.transpose(ps_[:, 128 * cc:128 * (cc + 1)],
                                        stb[p2][:, 128 * cc:128 * (cc + 1)], idf[:])
            v = ctx.inc(i, trn, 1)
            M(("str", it), trn, v)

        @block.vector
        def _(vector, it=it, p2=p2):
            ctx.wait(vector, ("str", it))
            if it >= 2:
                ctx.wait(vector, ("exp", it - 2, 3))
            i = vector.tensor_reduce(nm4[p2][:], ps_[:].rearrange("p (c k) -> p c k", k=128),
                                     axis=AX.X, op=ALU.max, negate=True)
            v = ctx.inc(i, c, 1)
            M(("nm", it), c, v)

        @block.scalar
        def _(scalar, it=it, p2=p2):
            ctx.wait(scalar, ("nm", it))
            if it >= 2:
                ctx.wait(scalar, ("psc", it - 2, 3))
            for cc in range(4):
                scalar.activation(out=pb[p2][:, 128 * cc:128 * (cc + 1)],
                                  in_=ps_[:, 128 * cc:128 * (cc + 1)], func=AF.Exp,
                                  bias=nm4[p2][:, cc:cc + 1], accum_out=l4[p2][:, cc:cc + 1])
            i = scalar.activation(out=dmy[1][:], in_=dmy[0][:], func=AF.Copy)
            v = ctx.inc(i, a, 1)
            M(("exp", it, 3), a, v)

        @block.vector
        def _(vector, it=it, p2=p2):
            ctx.wait(vector, ("exp", it, 3))
            vector.reciprocal(out=li4[p2][:], in_=l4[p2][:])
            sepv(vector)
            i = None
            for cc in range(4):
                i = vector.tensor_scalar(out=pb[p2][:, 128 * cc:128 * (cc + 1)],
                                         in0=pb[p2][:, 128 * cc:128 * (cc + 1)],
                                         scalar1=li4[p2][:, cc:cc + 1], scalar2=None, op0=ALU.mult)
                if cc == 3:
                    v = ctx.inc(i, c, 1)
                    M(("psc", it, 3), c, v)

        @block.tensor
        def _(tensor, it=it, p2=p2):
            ctx.wait(tensor, ("psc", it, 3))
            if it >= 1:
                ctx.wait(tensor, ("pte", it - 1))
            i = None
            for cc in range(4):
                i = nc.tensor.transpose(ppt[:, 128 * cc:128 * (cc + 1)],
                                        pb[p2][:, 128 * cc:128 * (cc + 1)], idb[:])
            v = ctx.inc(i, trn, 1)
            M(("ptr", it), trn, v)

        @block.vector
        def _(vector, it=it, p2=p2):
            ctx.wait(vector, ("ptr", it))
            if it >= 2:
                ctx.wait(vector, ("pv", it - 2))
            i = vector.tensor_copy(ptb[p2][:], ppt[:])
            v = ctx.inc(i, c, 1)
            M(("pte", it), c, v)

        @block.tensor
        def _(tensor, it=it, kvh=kvh, bb=bb, p2=p2):
            ctx.wait(tensor, ("pte", it))
            if it >= 1:
                ctx.wait(tensor, ("oev", it - 1))
            i = None
            for j in range(8):
                i = nc.tensor.matmul(po[:, 64 * j:64 * (j + 1)],
                                     kv[bb][j][:, 512 + 128 * kvh:512 + 128 * (kvh + 1)],
                                     ptb[p2][:, 128 * (j // 2) + 64 * (j % 2):128 * (j // 2) + 64 * (j % 2) + 64],
                                     start=True, stop=True)
            v = ctx.inc(i, trn, 1)
            M(("pv", it), trn, v)

        @block.scalar
        def _(scalar, it=it, p2=p2):
            ctx.wait(scalar, ("pv", it))
            if it >= 2:
                ctx.wait(scalar, ("orr", it - 2))
            i = scalar.activation(out=osb[p2][:], in_=po[:], func=AF.Copy)
            v = ctx.inc(i, a, 1)
            M(("oev", it), a, v)

        @block.vector
        def _(vector, it=it, p2=p2):
            ctx.wait(vector, ("oev", it))
            if it >= 2:
                ctx.wait(vector, ("ost", it - 2))
            iv = osb[p2][:].rearrange("p (j h t) -> p j h t", h=4, t=16).transpose([0, 2, 1, 3])
            ov = orb[p2][:].rearrange("p (h j t) -> p h j t", h=4, t=16)
            i = vector.tensor_copy(ov, iv)
            v = ctx.inc(i, c, 1)
            M(("orr", it), c, v)

        @block.gpsimd
        def _(gpsimd, it=it, G=G, kvh=kvh, p2=p2):
            if it >= 1:
                ctx.ckpt(gpsimd, st, stc, ("ost", it - 1))
            ctx.wait(gpsimd, ("orr", it))
            i = None
            for ql in range(4):
                qh = 4 * kvh + ql
                i = gpsimd.dma_start(out=ap["attn1_fm"][128 * qh:128 * (qh + 1), 128 * G:128 * (G + 1)],
                                     in_=orb[p2][:, 128 * ql:128 * (ql + 1)])
                ctx.inc(i, st, 16)


def stage11_sems(tag="s11"):
    return ["ld", "wld", "mm", "ev", "c", "a", "trn", "g", "evx", "evy", "st", "ldc", "wldc", "stc", "s10_cc"]




def emit_stage13a(ctx, ap, stack, deps, tag="s13a"):
    ctx.scope = tag
    nc, block = ctx.nc, ctx.block
    sb = lambda n, s, d: alloc_sbuf(nc, stack, f"{tag}_{n}", s, d)
    psa = lambda n, s, d: alloc_psum(nc, stack, f"{tag}_{n}", s, d)

    mus = [sb(f"mus{j}", [128, 16 * 512], BF16) for j in range(2)]
    sqs = [sb(f"sqs{j}", [128, 16 * 512], BF16) for j in range(2)]
    rt = sb("rt", [128, 512], F32)
    rn = sb("rn", [128, 512], F32)
    ones128 = sb("ones128", [128, 128], BF16)
    mno = [sb(f"mno{j}", [128, 512], BF16) for j in range(2)]
    dmy = [sb(f"dmy{j}", [128, 32], F32) for j in range(2)]
    pn = psa("pn", [128, 512], F32)

    ld, wld, c, a, trn, st = ("ld", "wld", "c", "a", "trn", "st")
    ldc, wldc, stc = ("ldc", "wldc", "stc")
    M = ctx.mark
    mur = ap["mu_in"].rearrange("(c p) e -> p c e", p=128)

    @block.sync
    def _(sync):
        for s_, v_ in deps:
            sync.wait_ge(ctx.h[s_], v_)
        ctx.inc(sync.dma_start(out=ones128[:], in_=ap["ones128b"]), ld, 16)
        ctx.ckpt(sync, ld, ldc, ("aux",))

    @block.vector
    def _(vector):
        for s_, v_ in deps:
            vector.wait_ge(ctx.h[s_], v_)
        vector.memset(dmy[0][:], 0.0)

    for eng in ("tensor", "scalar", "gpsimd"):
        @getattr(block, eng)
        def _(e):
            for s_, v_ in deps:
                e.wait_ge(ctx.h[s_], v_)

    @block.gpsimd
    def _(gpsimd):
        ctx.inc(gpsimd.dma_start(out=mus[0][:].rearrange("p (c e) -> p c e", e=512),
                                 in_=mur[:, :, 0:512]), wld, 16)
        ctx.ckpt(gpsimd, wld, wldc, ("muld", 0))

    for esc in range(4):
        if esc + 1 < 4:
            @block.gpsimd
            def _(gpsimd, esc=esc):
                if esc >= 1:
                    ctx.wait(gpsimd, ("mne", esc - 1, 15))
                    ctx.wait(gpsimd, ("sqA", esc - 1))
                ctx.inc(gpsimd.dma_start(out=mus[(esc + 1) % 2][:].rearrange("p (c e) -> p c e", e=512),
                                         in_=mur[:, :, 512 * (esc + 1):512 * (esc + 2)]), wld, 16)
                ctx.ckpt(gpsimd, wld, wldc, ("muld", esc + 1))

        @block.scalar
        def _(scalar, esc=esc):
            ctx.wait(scalar, ("muld", esc))
            if esc == 0:
                ctx.wait(scalar, ("aux",))
            if esc >= 2:
                ctx.wait(scalar, ("nmm", esc - 2, 15))
            i = scalar.activation(out=sqs[esc % 2][:], in_=mus[esc % 2][:], func=AF.Square)
            v = ctx.inc(i, a, 1)
            M(("sqA", esc), a, v)

        for dc in range(16):
            @block.tensor
            def _(tensor, esc=esc, dc=dc):
                if dc == 0:
                    ctx.wait(tensor, ("sqA", esc))
                    if esc >= 1:
                        ctx.wait(tensor, ("rev", esc - 1))
                i = nc.tensor.matmul(pn[:], ones128[:], sqs[esc % 2][:, 512 * dc:512 * (dc + 1)],
                                     start=(dc == 0), stop=(dc == 15))
                v = ctx.inc(i, trn, 1)
                M(("nmm", esc, dc), trn, v)

        @block.scalar
        def _(scalar, esc=esc):
            ctx.wait(scalar, ("nmm", esc, 15))
            i = scalar.activation(out=rt[:], in_=pn[:], func=AF.Sqrt)
            v = ctx.inc(i, a, 1)
            M(("rts", esc), a, v)

        @block.vector
        def _(vector, esc=esc):
            ctx.wait(vector, ("rts", esc))
            vector.tensor_scalar(out=rt[:], in0=rt[:], scalar1=1e-12, scalar2=None, op0=ALU.max)
            vector.tensor_copy(dmy[1][:], dmy[0][:])
            i = vector.reciprocal(out=rn[:], in_=rt[:])
            v = ctx.inc(i, c, 1)
            M(("rev", esc), c, v)

        for dc in range(16):
            @block.vector
            def _(vector, esc=esc, dc=dc):
                if dc == 0:
                    ctx.wait(vector, ("rev", esc))
                if dc >= 2 or esc >= 1:
                    pr = (esc, dc - 2) if dc >= 2 else (esc - 1, 14 + dc)
                    ctx.wait(vector, ("mnst", pr[0], pr[1]))
                i = vector.tensor_tensor(out=mno[dc % 2][:], in0=mus[esc % 2][:, 512 * dc:512 * (dc + 1)],
                                         in1=rn[:], op=ALU.mult)
                v = ctx.inc(i, c, 1)
                M(("mne", esc, dc), c, v)

            @block.gpsimd
            def _(gpsimd, esc=esc, dc=dc):
                if dc >= 1 or esc >= 1:
                    pr = (esc, dc - 1) if dc >= 1 else (esc - 1, 15)
                    ctx.ckpt(gpsimd, st, stc, ("mnst", pr[0], pr[1]))
                ctx.wait(gpsimd, ("mne", esc, dc))
                ctx.inc(gpsimd.dma_start(out=ap["mun_fm"][128 * dc:128 * (dc + 1), 512 * esc:512 * (esc + 1)],
                                         in_=mno[dc % 2][:]), st, 16)
    @block.gpsimd
    def _(gpsimd):
        ctx.ckpt(gpsimd, st, stc, ("mnst", 3, 15))


def stage13a_sems(tag="s13a"):
    return ["ld", "wld", "mm", "ev", "c", "a", "trn", "g", "evx", "evy", "st", "ldc", "wldc", "stc", "s10_cc"]


def emit_stage13b(ctx, ap, stack, deps, b, tag=None):
    tag = tag or f"s13b{b}"
    ctx.scope = tag
    nc, block = ctx.nc, ctx.block
    sb = lambda n, s, d: alloc_sbuf(nc, stack, f"{tag}_{n}", s, d)
    psa = lambda n, s, d: alloc_psum(nc, stack, f"{tag}_{n}", s, d)

    xns = [sb(f"xns{j}", [128, 2048], BF16) for j in range(16)]
    muns = [sb(f"muns{j}", [128, 16 * 128], BF16) for j in range(2)]
    lg = [sb(f"lg{j}", [128, 2048], F32) for j in range(2)]
    dspb = [sb(f"dspb{j}", [128, 2048], BF16) for j in range(2)]
    dspc = [sb(f"dspc{j}", [128, 2048], BF16) for j in range(2)]
    lsum = [sb(f"ls{j}", [128, 1], F32) for j in range(2)]
    linv = sb("linv", [128, 1], F32)
    idb = sb("idb", [128, 128], BF16)
    ones_c = sb("ones_c", [128, 1], BF16)
    rdt = sb("rdt", [128, 16], F32)
    dtmp = sb("dtmp", [128, 16], F32)
    evd = [sb(f"evd{j}", [128, 512], BF16) for j in range(2)]
    dmy = [sb(f"dmy{j}", [128, 32], F32) for j in range(2)]
    pl = psa("pl", [128, 2048], F32)
    ptd = [psa(f"ptd{j}", [128, 512], BF16) for j in range(2)]
    pdn = psa("pdn", [128, 16], F32)

    ld, wld, c, a, trn, st = ("ld", "wld", "c", "a", "trn", "st")
    ldc, wldc, stc = ("ldc", "wldc", "stc")
    M = ctx.mark
    sepv = lambda e: e.tensor_copy(dmy[1][:], dmy[0][:])

    @block.sync
    def _(sync):
        for s_, v_ in deps:
            sync.wait_ge(ctx.h[s_], v_)
        ctx.inc(sync.dma_start(out=idb[:], in_=ap["identb"]), ld, 16)
        for dc in range(16):
            ctx.inc(sync.dma_start(out=xns[dc][:],
                                   in_=ap["x_n_fm"][128 * dc:128 * (dc + 1), 2048 * b:2048 * (b + 1)]), ld, 16)
        ctx.ckpt(sync, ld, ldc, ("xld",))

    @block.vector
    def _(vector):
        for s_, v_ in deps:
            vector.wait_ge(ctx.h[s_], v_)
        vector.memset(dmy[0][:], 0.0)
        vector.memset(ones_c[:], 1.0)

    for eng in ("tensor", "scalar", "gpsimd"):
        @getattr(block, eng)
        def _(e):
            for s_, v_ in deps:
                e.wait_ge(ctx.h[s_], v_)

    @block.gpsimd
    def _(gpsimd):
        mr = ap["mun_fm"].rearrange("(c p) e -> p c e", p=128)
        ctx.inc(gpsimd.dma_start(out=muns[0][:].rearrange("p (c e) -> p c e", e=128),
                                 in_=mr[:, :, 0:128]), wld, 16)
        ctx.ckpt(gpsimd, wld, wldc, ("mld", 0))

    def emit_dn_dtr(esc):
        @block.tensor
        def _(tensor, esc=esc):
            ctx.wait(tensor, ("dexp", esc))
            i = None
            for nt in range(16):
                i = nc.tensor.matmul(pdn[:, nt:nt + 1],
                                     dspb[esc % 2][:, 128 * nt:128 * (nt + 1)],
                                     ones_c[:], start=(esc == 0), stop=(esc == 15))
            v = ctx.inc(i, trn, 1)
            M(("dnmm", esc), trn, v)

        for g4 in range(4):
            @block.tensor
            def _(tensor, esc=esc, g4=g4):
                if g4 == 0:
                    ctx.wait(tensor, ("dsc", esc))
                key = ("dte", esc, g4 - 2) if g4 >= 2 else (("dte", esc - 1, 2 + g4) if esc >= 1 else None)
                if key is not None:
                    ctx.wait(tensor, key)
                i = None
                for jj in range(4):
                    j = 4 * g4 + jj
                    i = nc.tensor.transpose(ptd[g4 % 2][:, 128 * jj:128 * (jj + 1)],
                                            dspc[esc % 2][:, 128 * j:128 * (j + 1)], idb[:])
                v = ctx.inc(i, trn, 1)
                M(("dtr", esc, g4), trn, v)

            @block.vector
            def _(vector, esc=esc, g4=g4):
                ctx.wait(vector, ("dtr", esc, g4))
                key = ("dst_", esc, g4 - 2) if g4 >= 2 else (("dst_", esc - 1, 2 + g4) if esc >= 1 else None)
                if key is not None:
                    ctx.wait(vector, key)
                i = vector.tensor_copy(evd[g4 % 2][:], ptd[g4 % 2][:])
                v = ctx.inc(i, c, 1)
                M(("dte", esc, g4), c, v)

            @block.gpsimd
            def _(gpsimd, esc=esc, g4=g4):
                if not (esc == 0 and g4 == 0):
                    pr = (esc, g4 - 1) if g4 >= 1 else (esc - 1, 3)
                    ctx.ckpt(gpsimd, st, stc, ("dst_", pr[0], pr[1]))
                ctx.wait(gpsimd, ("dte", esc, g4))
                dd = ap["dispatch_tm"].rearrange("(B J P) e -> B P J e", B=2, P=128)[b]
                ctx.inc(gpsimd.dma_start(out=dd[:, 4 * g4:4 * (g4 + 1), 128 * esc:128 * (esc + 1)],
                                         in_=evd[g4 % 2][:].rearrange("p (j n) -> p j n", n=128)), st, 16)

    for esc in range(16):
        if esc + 1 < 16:
            @block.gpsimd
            def _(gpsimd, esc=esc):
                if esc >= 1:
                    ctx.wait(gpsimd, ("lmm", esc - 1))
                mr = ap["mun_fm"].rearrange("(c p) e -> p c e", p=128)
                ctx.inc(gpsimd.dma_start(out=muns[(esc + 1) % 2][:].rearrange("p (c e) -> p c e", e=128),
                                         in_=mr[:, :, 128 * (esc + 1):128 * (esc + 2)]), wld, 16)
                ctx.ckpt(gpsimd, wld, wldc, ("mld", esc + 1))

        @block.tensor
        def _(tensor, esc=esc):
            ctx.wait(tensor, ("mld", esc))
            if esc == 0:
                ctx.wait(tensor, ("xld",))
            if esc >= 1:
                ctx.wait(tensor, ("lev", esc - 1))
            i = None
            for nb in range(4):
                for dc in range(16):
                    i = nc.tensor.matmul(pl[:, 512 * nb:512 * (nb + 1)],
                                         muns[esc % 2][:, 128 * dc:128 * (dc + 1)],
                                         xns[dc][:, 512 * nb:512 * (nb + 1)],
                                         start=(dc == 0), stop=(dc == 15))
            v = ctx.inc(i, trn, 1)
            M(("lmm", esc), trn, v)

        @block.vector
        def _(vector, esc=esc):
            ctx.wait(vector, ("lmm", esc))
            if esc >= 2:
                ctx.wait(vector, ("dexp", esc - 2))
            i = vector.tensor_copy(lg[esc % 2][:], pl[:])
            v = ctx.inc(i, c, 1)
            M(("lev", esc), c, v)

        @block.scalar
        def _(scalar, esc=esc):
            ctx.wait(scalar, ("lev", esc))
            if esc >= 2:
                ctx.wait(scalar, ("cmst", esc - 2))
                ctx.wait(scalar, ("dnmm", esc - 2))
                ctx.wait(scalar, ("dsc", esc - 2))
            scalar.activation(out=dspb[esc % 2][:], in_=lg[esc % 2][:], func=AF.Exp,
                              accum_out=lsum[esc % 2][:])
            i = scalar.activation(out=dmy[1][:], in_=dmy[0][:], func=AF.Copy)
            v = ctx.inc(i, a, 1)
            M(("dexp", esc), a, v)

        @block.vector
        def _(vector, esc=esc):
            ctx.wait(vector, ("dexp", esc))
            vector.reciprocal(out=linv[:], in_=lsum[esc % 2][:])
            sepv(vector)
            if esc >= 2:
                ctx.wait(vector, ("dtr", esc - 2, 3))
            i = vector.tensor_scalar(out=dspc[esc % 2][:], in0=dspb[esc % 2][:],
                                     scalar1=linv[:], scalar2=None, op0=ALU.mult)
            v = ctx.inc(i, c, 1)
            M(("dsc", esc), c, v)

        @block.gpsimd
        def _(gpsimd, esc=esc):
            if esc >= 1:
                ctx.ckpt(gpsimd, st, stc, ("cmst", esc - 1))
            ctx.wait(gpsimd, ("dexp", esc))
            ctx.inc(gpsimd.dma_start(
                out=ap["combine_fm"][2048 * b + 128 * esc:2048 * b + 128 * (esc + 1), :],
                in_=dspb[esc % 2][:]), st, 16)

        if esc >= 1:
            emit_dn_dtr(esc - 1)

    emit_dn_dtr(15)

    @block.vector
    def _(vector):
        ctx.wait(vector, ("dnmm", 15))
        vector.tensor_copy(dtmp[:], pdn[:])
        sepv(vector)
        i = vector.reciprocal(out=rdt[:], in_=dtmp[:])
        v = ctx.inc(i, c, 1)
        M(("rdtv",), c, v)

    @block.gpsimd
    def _(gpsimd):
        ctx.ckpt(gpsimd, st, stc, ("cmst", 15))
        ctx.wait(gpsimd, ("rdtv",))
        ctx.inc(gpsimd.dma_start(out=ap["rden_d"][128 * b:128 * (b + 1), :], in_=rdt[:]), st, 16)
        ctx.ckpt(gpsimd, st, stc, ("dst_", 15, 3))


def stage13b_sems(tag):
    return ["ld", "wld", "mm", "ev", "c", "a", "trn", "g", "evx", "evy", "st", "ldc", "wldc", "stc", "s10_cc"]


def emit_stage13c(ctx, ap, stack, deps, b, tag=None):
    tag = tag or f"s13c{b}"
    ctx.scope = tag
    nc, block = ctx.nc, ctx.block
    sb = lambda n, s, d: alloc_sbuf(nc, stack, f"{tag}_{n}", s, d)
    psa = lambda n, s, d: alloc_psum(nc, stack, f"{tag}_{n}", s, d)

    lgt = [sb(f"lgt{j}", [128, 2048], F32) for j in range(2)]
    cmb = [sb(f"cmb{j}", [128, 2048], BF16) for j in range(2)]
    nmx = [sb(f"nmx{j}", [128, 1], F32) for j in range(2)]
    lsum = [sb(f"ls{j}", [128, 1], F32) for j in range(2)]
    linv = sb("linv", [128, 1], F32)
    idb = sb("idb", [128, 128], BF16)
    evc = [sb(f"evc{j}", [128, 512], BF16) for j in range(2)]
    dmy = [sb(f"dmy{j}", [128, 32], F32) for j in range(2)]
    ptc = [psa(f"ptc{j}", [128, 512], BF16) for j in range(2)]

    ld, c, a, trn, st = ("ld", "c", "a", "trn", "st")
    ldc, stc = "ldc", "stc"
    M = ctx.mark
    sepv = lambda e: e.tensor_copy(dmy[1][:], dmy[0][:])

    @block.sync
    def _(sync):
        for s_, v_ in deps:
            sync.wait_ge(ctx.h[s_], v_)
        ctx.inc(sync.dma_start(out=idb[:], in_=ap["identb"]), ld, 16)
        ctx.inc(sync.dma_start(out=lgt[0][:], in_=ap["logT"][0:128, :]), ld, 16)
        ctx.ckpt(sync, ld, ldc, ("lld", 0))

    @block.vector
    def _(vector):
        for s_, v_ in deps:
            vector.wait_ge(ctx.h[s_], v_)
        vector.memset(dmy[0][:], 0.0)

    for eng in ("tensor", "scalar", "gpsimd"):
        @getattr(block, eng)
        def _(e):
            for s_, v_ in deps:
                e.wait_ge(ctx.h[s_], v_)

    for nt in range(16):
        if nt + 1 < 16:
            @block.sync
            def _(sync, nt=nt):
                if nt >= 1:
                    ctx.wait(sync, ("cexp", nt - 1))
                ctx.inc(sync.dma_start(out=lgt[(nt + 1) % 2][:],
                                       in_=ap["logT"][128 * (nt + 1):128 * (nt + 2), :]), ld, 16)
                ctx.ckpt(sync, ld, ldc, ("lld", nt + 1))

        @block.vector
        def _(vector, nt=nt):
            ctx.wait(vector, ("lld", nt))
            i = vector.tensor_reduce(nmx[nt % 2][:], lgt[nt % 2][:], axis=AX.X, op=ALU.max, negate=True)
            v = ctx.inc(i, c, 1)
            M(("cmax", nt), c, v)

        @block.scalar
        def _(scalar, nt=nt):
            ctx.wait(scalar, ("cmax", nt))
            if nt >= 2:
                ctx.wait(scalar, ("ctr", nt - 2, 3))
            scalar.activation(out=cmb[nt % 2][:], in_=lgt[nt % 2][:], func=AF.Exp,
                              bias=nmx[nt % 2][:], accum_out=lsum[nt % 2][:])
            i = scalar.activation(out=dmy[1][:], in_=dmy[0][:], func=AF.Copy)
            v = ctx.inc(i, a, 1)
            M(("cexp", nt), a, v)

        @block.vector
        def _(vector, nt=nt):
            ctx.wait(vector, ("cexp", nt))
            vector.reciprocal(out=linv[:], in_=lsum[nt % 2][:])
            sepv(vector)
            i = vector.tensor_scalar(out=cmb[nt % 2][:], in0=cmb[nt % 2][:],
                                     scalar1=linv[:], scalar2=None, op0=ALU.mult)
            v = ctx.inc(i, c, 1)
            M(("csc", nt), c, v)

        for g4 in range(4):
            @block.tensor
            def _(tensor, nt=nt, g4=g4):
                if g4 == 0:
                    ctx.wait(tensor, ("csc", nt))
                key = ("cte", nt, g4 - 2) if g4 >= 2 else (("cte", nt - 1, 2 + g4) if nt >= 1 else None)
                if key is not None:
                    ctx.wait(tensor, key)
                i = None
                for jj in range(4):
                    j = 4 * g4 + jj
                    i = nc.tensor.transpose(ptc[g4 % 2][:, 128 * jj:128 * (jj + 1)],
                                            cmb[nt % 2][:, 128 * j:128 * (j + 1)], idb[:])
                v = ctx.inc(i, trn, 1)
                M(("ctr", nt, g4), trn, v)

            @block.vector
            def _(vector, nt=nt, g4=g4):
                ctx.wait(vector, ("ctr", nt, g4))
                key = ("cst", nt, g4 - 2) if g4 >= 2 else (("cst", nt - 1, 2 + g4) if nt >= 1 else None)
                if key is not None:
                    ctx.wait(vector, key)
                i = vector.tensor_copy(evc[g4 % 2][:], ptc[g4 % 2][:])
                v = ctx.inc(i, c, 1)
                M(("cte", nt, g4), c, v)

            @block.gpsimd
            def _(gpsimd, nt=nt, g4=g4):
                if not (nt == 0 and g4 == 0):
                    pr = (nt, g4 - 1) if g4 >= 1 else (nt - 1, 3)
                    ctx.ckpt(gpsimd, st, stc, ("cst", pr[0], pr[1]))
                ctx.wait(gpsimd, ("cte", nt, g4))
                dd = ap["combine_fm"].rearrange("(J P) n -> P J n", P=128)
                ctx.inc(gpsimd.dma_start(out=dd[:, 4 * g4:4 * (g4 + 1), 128 * nt:128 * (nt + 1)],
                                         in_=evc[g4 % 2][:].rearrange("p (j n) -> p j n", n=128)), st, 16)
    @block.gpsimd
    def _(gpsimd):
        ctx.ckpt(gpsimd, st, stc, ("cst", 15, 3))


def stage13c_sems(tag):
    return ["ld", "wld", "mm", "ev", "c", "a", "trn", "g", "evx", "evy", "st", "ldc", "wldc", "stc", "s10_cc"]


def emit_stage13def(ctx, ap, stack, deps, b, tag=None):
    tag = tag or f"s13d{b}"
    ctx.scope = tag
    nc, block = ctx.nc, ctx.block
    sb = lambda n, s, d: alloc_sbuf(nc, stack, f"{tag}_{n}", s, d)
    psa = lambda n, s, d: alloc_psum(nc, stack, f"{tag}_{n}", s, d)

    SI = [sb(f"si{dc}", [128, 2048], BF16) for dc in range(16)]
    xls = [sb(f"xls{j}", [128, 2048], BF16) for j in range(16)]
    SO = xls
    w1s = [sb(f"w1s{j}", [128, 16 * 512], BF16) for j in range(2)]
    dsp = [w1s[j // 8][:, 512 * (j % 8):512 * (j % 8 + 1)] for j in range(16)]
    w2s = sb("w2s", [128, 4 * 2048], BF16)
    b1s = sb("b1s", [128, 64], F32)
    rdt = sb("rdt", [128, 16], F32)
    b2t = [sb(f"b2t{j}", [128, 2048], BF16) for j in range(2)]
    hts = [sb(f"hts{j}", [128, 512], BF16) for j in range(2)]
    ot = [w1s[j][:].bitcast(F32)[:, 0:2048] for j in range(2)]
    ps4 = [psa(f"ps4{j}", [128, 512], F32) for j in range(8)]

    ld, wld, c, a, trn, st = ("ld", "wld", "c", "a", "trn", "st")
    ldc, wldc, stc = ("ldc", "wldc", "stc")
    M = ctx.mark

    @block.sync
    def _(sync):
        for s_, v_ in deps:
            sync.wait_ge(ctx.h[s_], v_)
        for nc_ in range(16):
            ctx.inc(sync.dma_start(out=xls[nc_][:],
                                   in_=ap["x_ln_tm"][2048 * b + 128 * nc_:2048 * b + 128 * (nc_ + 1), :]), ld, 16)
        ctx.inc(sync.dma_start(out=rdt[:], in_=ap["rden_d"][128 * b:128 * (b + 1), :]), ld, 16)
        ctx.ckpt(sync, ld, ldc, ("xlld",))

    for eng in ("tensor", "vector", "scalar", "gpsimd"):
        @getattr(block, eng)
        def _(e):
            for s_, v_ in deps:
                e.wait_ge(ctx.h[s_], v_)

    for es4 in range(4):
        @block.sync
        def _(sync, es4=es4):
            dd = ap["dispatch_tm"].rearrange("(B J P) e -> B P J e", B=2, P=128)[b]
            for nc_ in range(16):
                ctx.inc(sync.dma_start(out=dsp[nc_],
                                       in_=dd[:, nc_, 512 * es4:512 * (es4 + 1)]), ld, 16)
            ctx.ckpt(sync, ld, ldc, ("dld", es4))

        for dc in range(16):
            @block.tensor
            def _(tensor, es4=es4, dc=dc):
                if dc == 0:
                    ctx.wait(tensor, ("dld", es4))
                    if es4 == 0:
                        ctx.wait(tensor, ("xlld",))
                pidx = dc % 2
                if 16 * es4 + dc >= 2:
                    pr = 16 * es4 + dc - 2
                    ctx.wait(tensor, ("siev", pr))
                i = None
                for nc_ in range(16):
                    i = nc.tensor.matmul(ps4[pidx][:], xls[nc_][:, 128 * dc:128 * (dc + 1)],
                                         dsp[nc_], start=(nc_ == 0), stop=(nc_ == 15))
                v = ctx.inc(i, trn, 1)
                M(("simm", es4, dc), trn, v)

            @block.vector
            def _(vector, es4=es4, dc=dc):
                ctx.wait(vector, ("simm", es4, dc))
                i = vector.tensor_copy(SI[dc][:, 512 * es4:512 * (es4 + 1)], ps4[dc % 2][:])
                v = ctx.inc(i, c, 1)
                M(("siev", 16 * es4 + dc), c, v)

    @block.gpsimd
    def _(gpsimd):
        ctx.wait(gpsimd, ("simm", 3, 15))
        w1r = ap["W1_in"].rearrange("E (c p) h -> E p c h", p=128)
        ctx.inc(gpsimd.dma_start(out=w1s[0][:].rearrange("p (c h) -> p c h", h=512),
                                 in_=w1r[0]), wld, 16)
        ctx.inc(gpsimd.dma_start(out=b1s[:].rearrange("p (hc e) -> p hc e", e=16),
                                 in_=ap["b1T_in"].rearrange("(hc p) e -> p hc e", p=128)), wld, 16)
        w2r = ap["W2_in"].rearrange("E (hc p) d -> E p hc d", p=128)
        ctx.inc(gpsimd.dma_start(out=w2s[:].rearrange("p (hc d) -> p hc d", d=2048),
                                 in_=w2r[0]), wld, 16)
        ctx.inc(gpsimd.dma_start(out=b2t[0][:], in_=ap["b2B"][0]), wld, 16)
        ctx.ckpt(gpsimd, wld, wldc, ("w1ld", 0))
        M(("w2ld", 0), wldc, ctx.n[wldc])

    for e in range(16):
        if e + 1 < 16:
            @block.gpsimd
            def _(gpsimd, e=e):
                if e >= 1:
                    ctx.wait(gpsimd, ("h1mm", e - 1, 3))
                w1r = ap["W1_in"].rearrange("E (c p) h -> E p c h", p=128)
                ctx.inc(gpsimd.dma_start(out=w1s[(e + 1) % 2][:].rearrange("p (c h) -> p c h", h=512),
                                         in_=w1r[e + 1]), wld, 16)
                ctx.inc(gpsimd.dma_start(out=b2t[(e + 1) % 2][:], in_=ap["b2B"][e + 1]), wld, 16)
                ctx.ckpt(gpsimd, wld, wldc, ("w1ld", e + 1))

        for hc in range(4):
            @block.tensor
            def _(tensor, e=e, hc=hc):
                if hc == 0:
                    ctx.wait(tensor, ("w1ld", e))
                    if e == 0:
                        ctx.wait(tensor, ("siev", 63))
                    if e >= 2:
                        ctx.wait(tensor, ("gel", e - 2, 3))
                i = None
                for dc in range(16):
                    i = nc.tensor.matmul(ps4[4 + e % 2][:, 128 * hc:128 * (hc + 1)],
                                         w1s[e % 2][:, 512 * dc + 128 * hc:512 * dc + 128 * (hc + 1)],
                                         SI[dc][:, 128 * e:128 * (e + 1)],
                                         start=(dc == 0), stop=(dc == 15))
                v = ctx.inc(i, trn, 1)
                M(("h1mm", e, hc), trn, v)

        @block.scalar
        def _(scalar, e=e):
            ctx.wait(scalar, ("h1mm", e, 3))
            if e >= 2:
                ctx.wait(scalar, ("so4", e - 2))
            for hc in range(4):
                i = scalar.activation(out=hts[e % 2][:, 128 * hc:128 * (hc + 1)],
                                      in_=ps4[4 + e % 2][:, 128 * hc:128 * (hc + 1)], func=AF.Gelu,
                                      bias=b1s[:, 16 * hc + e:16 * hc + e + 1])
                v = ctx.inc(i, a, 1)
                M(("gel", e, hc), a, v)

        @block.tensor
        def _(tensor, e=e):
            ctx.wait(tensor, ("gel", e, 3))
            ctx.wait(tensor, ("w2ld", e))
            if e == 0:
                ctx.wait(tensor, ("siev", 63))
            if e >= 1:
                ctx.wait(tensor, ("soev", e - 1, 3))
            i = None
            for nb in range(4):
                for hc in range(4):
                    i = nc.tensor.matmul(ps4[nb][:],
                                         hts[e % 2][:, 128 * hc:128 * (hc + 1)],
                                         w2s[:, 2048 * hc + 512 * nb:2048 * hc + 512 * (nb + 1)],
                                         start=(hc == 0), stop=(hc == 3))
            v = ctx.inc(i, trn, 1)
            M(("so4", e), trn, v)

        @block.vector
        def _(vector, e=e):
            ctx.wait(vector, ("so4", e))
            i = None
            for nb in range(4):
                i = vector.tensor_tensor(out=SO[e][:, 512 * nb:512 * (nb + 1)],
                                         in0=ps4[nb][:], in1=b2t[e % 2][:, 512 * nb:512 * (nb + 1)],
                                         op=ALU.add)
                v = ctx.inc(i, c, 1)
                M(("soev", e, nb), c, v)

        if e == 15:
            @block.sync
            def _(sync):
                ctx.wait(sync, ("h1mm", 15, 3))
                for esc in range(16):
                    ctx.inc(sync.dma_start(
                        out=SI[esc][:],
                        in_=ap["combine_fm"][2048 * b + 128 * esc:2048 * b + 128 * (esc + 1), :]), ld, 16)
                ctx.ckpt(sync, ld, ldc, ("cmbld",))

        if e + 1 < 16:
            @block.gpsimd
            def _(gpsimd, e=e):
                ctx.wait(gpsimd, ("so4", e))
                w2r = ap["W2_in"].rearrange("E (hc p) d -> E p hc d", p=128)
                ctx.inc(gpsimd.dma_start(out=w2s[:].rearrange("p (hc d) -> p hc d", d=2048),
                                         in_=w2r[e + 1]), wld, 16)
                ctx.ckpt(gpsimd, wld, wldc, ("w2ld", e + 1))

    for nt in range(16):
        @block.tensor
        def _(tensor, nt=nt):
            if nt == 0:
                ctx.wait(tensor, ("cmbld",))
                ctx.wait(tensor, ("soev", 15, 3))
            i = None
            for nb in range(4):
                if nt >= 2:
                    ctx.wait(tensor, ("fevb", nt - 2, nb))
                for esc in range(16):
                    i = nc.tensor.matmul(ps4[4 * (nt % 2) + nb][:],
                                         SI[esc][:, 128 * nt:128 * (nt + 1)],
                                         SO[esc][:, 512 * nb:512 * (nb + 1)],
                                         start=(esc == 0), stop=(esc == 15))
            v = ctx.inc(i, trn, 1)
            M(("fmm", nt), trn, v)

        @block.vector
        def _(vector, nt=nt):
            ctx.wait(vector, ("fmm", nt))
            if nt >= 2:
                ctx.wait(vector, ("fst", nt - 2))
            i = None
            for nb in range(4):
                i = vector.tensor_scalar(out=ot[nt % 2][:, 512 * nb:512 * (nb + 1)],
                                         in0=ps4[4 * (nt % 2) + nb][:],
                                         scalar1=rdt[:, nt:nt + 1], scalar2=None, op0=ALU.mult)
                v = ctx.inc(i, c, 1)
                M(("fevb", nt, nb), c, v)
            M(("fev", nt), c, ctx.n[c])

        @block.gpsimd
        def _(gpsimd, nt=nt):
            if nt >= 1:
                ctx.ckpt(gpsimd, st, stc, ("fst", nt - 1))
            ctx.wait(gpsimd, ("fev", nt))
            ctx.inc(gpsimd.dma_start(out=ap["out_tm"][2048 * b + 128 * nt:2048 * b + 128 * (nt + 1), :],
                                     in_=ot[nt % 2], accum_op=ALU.add), st, 16)
    @block.gpsimd
    def _(gpsimd):
        ctx.ckpt(gpsimd, st, stc, ("fst", 15))


def stage13def_sems(tag):
    return ["ld", "wld", "mm", "ev", "c", "a", "trn", "g", "evx", "evy", "st", "ldc", "wldc", "stc", "s10_cc"]

GLOBAL_SEMS = ["ld", "wld", "mm", "ev", "c", "a", "trn", "g", "evx", "evy", "st",
               "ldc", "wldc", "stc", "s10_cc"]

SPECS = {
    "x_tm": ((T, D), F32, "ExternalInput"),
    "lnw": ((128, 32), F32, "ExternalInput"),
    "identb": ((128, 128), BF16, "ExternalInput"),
    "identf": ((128, 128), F32, "ExternalInput"),
    "ones1b": ((1, 128), BF16, "ExternalInput"),
    "ones128b": ((128, 128), BF16, "ExternalInput"),
    "mask128": ((128, 128), BF16, "ExternalInput"),
    "mask4": ((128, 4), F32, "ExternalInput"),
    "sel4": ((4, 128), BF16, "ExternalInput"),
    "tri_maskb": ((128, 128), BF16, "ExternalInput"),
    "mask_l1b": ((128, 64), BF16, "ExternalInput"),
    "w_qkv0T": ((D, 6144), F32, "ExternalInput"),
    "w_proj0T": ((D, D), F32, "ExternalInput"),
    "b_proj0": ((1, D), F32, "ExternalInput"),
    "WqT": ((D, D), F32, "ExternalInput"),
    "WkT": ((D, D), F32, "ExternalInput"),
    "WvT": ((D, D), F32, "ExternalInput"),
    "Wo_in": ((D, D), F32, "ExternalInput"),
    "A_in": ((D, 16), F32, "ExternalInput"),
    "B_in": ((16, 8192), F32, "ExternalInput"),
    "W_inT": ((D, 8192), F32, "ExternalInput"),
    "W_upT": ((D, 8192), F32, "ExternalInput"),
    "W_downT": ((8192, D), F32, "ExternalInput"),
    "gammaB": ((128, D), BF16, "ExternalInput"),
    "betaB": ((128, D), BF16, "ExternalInput"),
    "scaleB": ((128, 1), F32, "ExternalInput"),
    "w_qkv1T": ((D, 3072), F32, "ExternalInput"),
    "w_proj1T": ((D, D), F32, "ExternalInput"),
    "mu_in": ((D, D), F32, "ExternalInput"),
    "W1_in": ((16, D, 512), F32, "ExternalInput"),
    "W2_in": ((16, 512, D), F32, "ExternalInput"),
    "b1T_in": ((512, 16), F32, "ExternalInput"),
    "b2B": ((16, 128, D), BF16, "ExternalInput"),
    "xn_tm": ((T, D), F32, "Internal"),
    "xn_fm": ((D, T), BF16, "Internal"),
    "h1_fm": ((D, T), BF16, "Internal"),
    "h2_fm": ((D, T), BF16, "Internal"),
    "qkv0_fm": ((6144, T), BF16, "Internal"),
    "attn0_fm": ((D, T), BF16, "Internal"),
    "x1_tm": ((T, D), F32, "Internal"),
    "qkv1_fm": ((6144, T), BF16, "Internal"),
    "attnPV_fm": ((D, T), BF16, "Internal"),
    "mm_fm": ((16, T), BF16, "Internal"),
    "win_b": ((64, 128, D), BF16, "Internal"),
    "wup_b": ((64, 128, D), BF16, "Internal"),
    "gu_fm": ((8192, T), BF16, "Internal"),
    "xn1_fm": ((D, T), BF16, "Internal"),
    "x_ln_tm": ((T, D), BF16, "Internal"),
    "x_n_fm": ((D, T), BF16, "Internal"),
    "qkv1L_fm": ((1024, T), BF16, "Internal"),
    "q_bmaj": ((D, T), BF16, "Internal"),
    "kv_tm_loc": ((T, 1024), BF16, "Internal"),
    "kv_all": ((8 * T, 1024), BF16, "Internal"),
    "attn1_fm": ((D, T), BF16, "Internal"),
    "mun_fm": ((D, D), BF16, "Internal"),
    "dispatch_tm": ((2 * D, D), BF16, "Internal"),
    "combine_fm": ((2 * D, D), BF16, "Internal"),
    "rden_d": ((256, 16), F32, "Internal"),
    "out_tm": ((T, D), F32, "ExternalOutput"),
}


def host_prep(inp):
    bfc = lambda x: np.ascontiguousarray(x).astype(ml_dtypes.bfloat16)
    ii = np.arange(128)
    jj = np.arange(64)
    d = {}
    d["lnw"] = np.concatenate([
        inp["L0_ffn_ln1_w"].reshape(16, 128).T,
        inp["L0_ffn_ln2_w"].reshape(16, 128).T,
    ], axis=1).astype(np.float32).copy()
    d["identb"] = np.eye(128, dtype=ml_dtypes.bfloat16)
    d["identf"] = np.eye(128, dtype=np.float32)
    d["ones1b"] = np.ones((1, 128), dtype=ml_dtypes.bfloat16)
    d["ones128b"] = np.ones((128, 128), dtype=ml_dtypes.bfloat16)
    d["mask128"] = (ii[:, None] // 32 == ii[None, :] // 32).astype(ml_dtypes.bfloat16)
    d["mask4"] = (ii[:, None] // 32 == np.arange(4)[None, :]).astype(np.float32)
    d["sel4"] = (np.arange(4)[:, None] == ii[None, :] // 32).astype(ml_dtypes.bfloat16)
    d["tri_maskb"] = np.where(ii[None, :] <= ii[:, None], 0.0, -1e30).astype(ml_dtypes.bfloat16)
    d["mask_l1b"] = np.where((ii[:, None] // 16) == ((jj[None, :] % 16) // 2), 0.0, -1e30).astype(ml_dtypes.bfloat16)
    d["w_qkv0T"] = np.ascontiguousarray(inp["L0_attn_w_qkv"].T.astype(np.float32))
    d["w_proj0T"] = np.ascontiguousarray(inp["L0_attn_w_proj"].T.astype(np.float32))
    d["b_proj0"] = np.asarray(inp["L0_attn_b_proj"], np.float32).reshape(1, D).copy()
    d["WqT"] = np.ascontiguousarray(inp["L0_ffn_Wq"].T.astype(np.float32))
    d["WkT"] = np.ascontiguousarray(inp["L0_ffn_Wk"].T.astype(np.float32))
    d["WvT"] = np.ascontiguousarray(inp["L0_ffn_Wv"].T.astype(np.float32))
    d["Wo_in"] = np.ascontiguousarray(inp["L0_ffn_Wo"].astype(np.float32))
    d["A_in"] = np.ascontiguousarray(inp["L0_ffn_A"].astype(np.float32))
    d["B_in"] = np.ascontiguousarray(inp["L0_ffn_B"].astype(np.float32))
    d["W_inT"] = np.ascontiguousarray(inp["L0_ffn_W_in_base"].T.astype(np.float32))
    d["W_upT"] = np.ascontiguousarray(inp["L0_ffn_W_up_base"].T.astype(np.float32))
    d["W_downT"] = np.ascontiguousarray(inp["L0_ffn_W_down"].T.astype(np.float32))
    d["gammaB"] = np.broadcast_to(bfc(inp["L1_ffn_gamma"]).reshape(1, D), (128, D)).copy()
    d["betaB"] = np.broadcast_to(bfc(inp["L1_ffn_beta"]).reshape(1, D), (128, D)).copy()
    d["scaleB"] = np.broadcast_to(np.asarray(inp["L1_ffn_scale"], np.float32).reshape(1, 1), (128, 1)).copy()
    d["w_qkv1T"] = np.ascontiguousarray(inp["L1_attn_w_qkv"].T.astype(np.float32))
    d["w_proj1T"] = np.ascontiguousarray(inp["L1_attn_w_proj"].T.astype(np.float32))
    d["mu_in"] = np.ascontiguousarray(np.asarray(inp["L1_ffn_mu"], np.float32).reshape(D, D))
    d["W1_in"] = np.ascontiguousarray(inp["L1_ffn_W1"].astype(np.float32))
    d["W2_in"] = np.ascontiguousarray(inp["L1_ffn_W2"].astype(np.float32))
    d["b1T_in"] = np.ascontiguousarray(inp["L1_ffn_b1"].T.astype(np.float32))
    d["b2B"] = np.broadcast_to(bfc(inp["L1_ffn_b2"])[:, None, :], (16, 128, D)).copy()
    return d


def free_stage(stack):
    while stack:
        stack.pop().__exit__(None, None, None)


def build_module():
    nc = bass.Bass("TRN2", target_bir_lowering=False, debug=False)
    ap = {}
    for name, (shape, dtype, kind) in SPECS.items():
        if name == "kv_all":
            ap[name] = nc.dram_tensor(name, list(shape), dtype, addr_space="Shared").ap()
        elif kind == "Internal":
            ap[name] = nc.dram_tensor(name, list(shape), dtype).ap()
        else:
            ap[name] = nc.dram_tensor(name, list(shape), dtype, kind=kind).ap()
    stack = []
    with nc.Block() as block:
        ctx = Ctx(nc, block, GLOBAL_SEMS)

        def dd():
            return [(s_, ctx.n[s_]) for s_ in GLOBAL_SEMS
                    if s_ != "s10_cc" and ctx.n.get(s_, 0) > 0]

        emit_stage1(ctx, ap, stack)
        dX = dd(); free_stage(stack)
        emit_fm_matmul(ctx, ap, stack, "s2", dX, "xn_fm", "qkv0_fm",
                       [("w_qkv0T", 0, 4096, AF.Relu, None),
                        ("w_qkv0T", 4096, 6144, AF.Copy, None)])
        dX = dd(); free_stage(stack)
        emit_stage3(ctx, ap, stack, dX)
        dX = dd(); free_stage(stack)
        emit_tm_matmul(ctx, ap, stack, "s4", dX, "attn0_fm", "w_proj0T", "x1_tm",
                       bias_name="b_proj0", addins=("x_tm", "xn_tm"))
        dX = dd(); free_stage(stack)
        emit_fm_matmul(ctx, ap, stack, "s5", dX, "h1_fm", "qkv1_fm",
                       [("WqT", 0, 2048, AF.Copy, 1.0 / math.sqrt(128.0)),
                        ("WkT", 0, 2048, AF.Copy, None),
                        ("WvT", 0, 2048, AF.Copy, None)])
        dX = dd(); free_stage(stack)
        emit_stage6(ctx, ap, stack, dX)
        dX = dd(); free_stage(stack)
        emit_stage7(ctx, ap, stack, dX)
        dX = dd(); free_stage(stack)
        emit_stage8a(ctx, ap, stack, dX)
        dX = dd(); free_stage(stack)
        emit_stage8c(ctx, ap, stack, dX)
        dX = dd(); free_stage(stack)
        emit_stage9(ctx, ap, stack, dX)
        dX = dd(); free_stage(stack)
        def qre_view(obt, r):
            return obt[:].rearrange("p (b2 s) -> p s b2", s=2)[:, r // 4, 512 * (r % 4):512 * (r % 4) + 512]

        segs10 = []
        for g_ in range(4):
            segs10 += [("w_qkv1T", 768 * g_ + 512, 768 * g_ + 640, AF.Copy, None),
                       ("w_qkv1T", 768 * g_ + 640, 768 * g_ + 768, AF.Copy, None)]
        for g_ in range(4):
            segs10.append(("w_qkv1T", 768 * g_, 768 * g_ + 512, AF.Copy, 1.0 / math.sqrt(128.0),
                           "q_bmaj", qre_view))
        emit_fm_matmul(ctx, ap, stack, "s10", dX, "xn1_fm", "qkv1L_fm", segs10)
        dX = dd(); free_stage(stack)
        emit_stage10kvT(ctx, ap, stack, dX)
        dX = dd(); free_stage(stack)
        emit_stage13a(ctx, ap, stack, dX)
        dX = dd(); free_stage(stack)
        emit_stage13b(ctx, ap, stack, dX, 0)
        dX = dd(); free_stage(stack)
        emit_stage13b(ctx, ap, stack, dX, 1)
        dX = dd(); free_stage(stack)
        emit_stage11(ctx, ap, stack, dX + [("s10_cc", 1)])
        dX = dd(); free_stage(stack)

        def a1_view(out, tt):
            vv = out.rearrange("(s n) d -> s n d", s=2)
            return vv[:, 64 * tt:64 * (tt + 1), :].transpose([1, 0, 2])

        emit_tm_matmul(ctx, ap, stack, "s12", dX, "attn1_fm", "w_proj1T", "out_tm",
                       addins=("x1_tm",), out_view=a1_view)
        for b_ in range(2):
            dX = dd(); free_stage(stack)
            emit_stage13def(ctx, ap, stack, dX, b_)

        @block.vector
        def _(vector):
            for nm in GLOBAL_SEMS:
                if ctx.n[nm] > 0:
                    vector.wait_ge(ctx.h[nm], ctx.n[nm])

    free_stage(stack)
    for cm in reversed(ctx._cm):
        cm.__exit__(None, None, None)
    return nc


_NC_CACHE = None


def kernel(**inputs):
    global _NC_CACHE
    prep = host_prep(inputs)
    x = np.asarray(inputs["L0_attn_x"], np.float32)
    if _NC_CACHE is None:
        _NC_CACHE = build_module()
    nc = _NC_CACHE
    in_maps = []
    for cidx in range(8):
        m = {"x_tm": np.ascontiguousarray(x[2 * cidx:2 * cidx + 2].reshape(T, D))}
        m.update(prep)
        in_maps.append(m)
    res = run_bass_kernel_spmd(nc, in_maps, core_ids=list(range(8)))
    out = np.concatenate([res.results[cidx]["out_tm"].reshape(2, 2048, 2048)
                          for cidx in range(8)], axis=0)
    return out.astype(np.float32)


# version 08 (final)
# speedup vs baseline: 1.0924x; 1.0924x over previous
import sys
sys.path.insert(0, '/opt/trn_rl_repo')
import math
import numpy as np
import ml_dtypes
import concourse.bass as bass
import concourse.mybir as mybir
from concourse.bass_utils import run_bass_kernel_spmd



F32 = mybir.dt.float32
BF16 = mybir.dt.bfloat16
AF = mybir.ActivationFunctionType
ALU = mybir.AluOpType
AX = mybir.AxisListType

D = 2048
T = 4096
NT = T // 128
NC = D // 128


class Ctx:

    def __init__(self, nc, block, sem_names):
        self.nc = nc
        self.block = block
        self.h = {}
        self.n = {}
        self.marks = {}
        self.scope = "" 
        self._cm = []
        for nm in sem_names:
            cm = nc.semaphore(nm)
            self.h[nm] = cm.__enter__()
            self._cm.append(cm)
            self.n[nm] = 0

    def inc(self, instr, sem, amt=1):
        self.n[sem] += amt
        instr.then_inc(self.h[sem], amt)
        return self.n[sem]

    def inc_cc(self, instr, sem):
        self.n[sem] += 1
        instr.then_inc(self.h[sem])
        return self.n[sem]

    def mark(self, key, sem=None, val=None):
        key = (self.scope,) + tuple(key)
        assert key not in self.marks, key
        self.marks[key] = (sem, val)

    def wait(self, eng, key):
        key = (self.scope,) + tuple(key)
        sem, val = self.marks[key]
        eng.wait_ge(self.h[sem], val)

    def ckpt(self, eng, dma_sem, ck_sem, key):
        eng.wait_ge(self.h[dma_sem], self.n[dma_sem])
        eng.sem_inc(self.h[ck_sem], 1)
        self.n[ck_sem] += 1
        self.mark(key, ck_sem, self.n[ck_sem])

    def wait_maybe(self, eng, key):
        key = (self.scope,) + tuple(key)
        if key in self.marks:
            sem, val = self.marks[key]
            eng.wait_ge(self.h[sem], val)


def alloc_sbuf(nc, stack, name, shape, dtype):
    cm = nc.sbuf_tensor(name, shape, dtype)
    t = cm.__enter__()
    stack.append(cm)
    return t


def alloc_psum(nc, stack, name, shape, dtype):
    cm = nc.psum_tensor(name, shape, dtype)
    t = cm.__enter__()
    stack.append(cm)
    return t



NB = 6


def emit_stage1(ctx, ap, stack, tag="s1"):
    ctx.scope = tag
    nc, block = ctx.nc, ctx.block
    sb = lambda n, s, d: alloc_sbuf(nc, stack, f"{tag}_{n}", s, d)
    ps = lambda n, s, d: alloc_psum(nc, stack, f"{tag}_{n}", s, d)

    x_tm, xn_tm = ap["x_tm"], ap["xn_tm"]
    out_fm = {"xn": ap["xn_fm"], "h1": ap["h1_fm"], "h2": ap["h2_fm"]}

    xt = [sb(f"xt{j}", [128, D], F32) for j in range(2)]
    xn = [sb(f"xn{j}", [128, D], F32) for j in range(4)]
    xnb = [sb(f"xnb{j}", [128, D], BF16) for j in range(NB)]
    yb = [sb(f"yb{j}", [128, D], BF16) for j in range(NB)]
    sq = sb("sq", [128, D], F32)
    lnw = sb("lnw", [128, 32], F32)
    idb = sb("idb", [128, 128], BF16)
    va = [sb(f"va{j}", [128, 1], F32) for j in range(2)]
    vm = [sb(f"vm{j}", [128, 1], F32) for j in range(2)]
    vs = [sb(f"vs{j}", [128, 1], F32) for j in range(2)]
    vb = [sb(f"vb{j}", [128, 1], F32) for j in range(2)]
    vs2 = [sb(f"vs2{j}", [128, 1], F32) for j in range(2)]
    vb2 = [sb(f"vb2{j}", [128, 1], F32) for j in range(2)]
    vc = sb("vc", [128, 1], F32)
    vc2 = sb("vc2", [128, 1], F32)
    veps5 = sb("veps5", [128, 1], F32)
    veps6 = sb("veps6", [128, 1], F32)
    dmy = [sb(f"dmy{j}", [128, 32], F32) for j in range(2)]
    ex = {s: sb(f"ex_{s}", [128, NC * 512], BF16) for s in ("xn", "h1", "h2")}
    psx = [ps(f"psx{j}", [128, 512], BF16) for j in range(2)]
    psy = [ps(f"psy{j}", [128, 512], BF16) for j in range(2)]

    sepv = lambda eng: eng.tensor_copy(dmy[1][:], dmy[0][:])
    sepa = lambda eng: eng.activation(out=dmy[1][:], in_=dmy[0][:], func=AF.Copy)

    ld, c, a, trn, evx, evy, st = ("ld", "c", "a", "trn", "evx", "evy", "st")
    ldc, stc = "ldc", "stc"
    M = ctx.mark
    NG = NT // 4

    @block.sync
    def _(sync):
        ctx.inc(sync.dma_start(out=lnw[:], in_=ap["lnw"]), ld, 16)
        ctx.inc(sync.dma_start(out=idb[:], in_=ap["identb"]), ld, 16)
        ctx.inc(sync.dma_start(out=xt[0][:], in_=x_tm[0:128, :]), ld, 16)
        ctx.ckpt(sync, ld, ldc, ("ld_tile", 0))

    @block.vector
    def _(vector):
        vector.memset(veps5[:], 1e-5)
        vector.memset(veps6[:], 1e-6)
        vector.memset(dmy[0][:], 0.0)

    for t in range(NT + 8):
        g = (t - 6) // 4 if (t - 6) % 4 == 0 and 0 <= (t - 6) // 4 < NG else None

        if g is not None:
            if g > 0:
                @block.gpsimd
                def _(gpsimd, g=g):
                    ctx.ckpt(gpsimd, st, stc, ("st_xn", g - 1, 0))
                    for ch2 in range(NC):
                        if ch2 > 0:
                            M(("st_xn", g - 1, ch2), stc, ctx.n[stc])
                        M(("st_h2", g - 1, ch2), stc, ctx.n[stc])
            for ch in range(NC):
                @block.tensor
                def _(tensor, g=g, ch=ch):
                    if ch == 0:
                        ctx.wait(tensor, ("s2", 4 * g + 3))
                        ctx.wait(tensor, ("d3", 4 * g + 3))
                    for strm, src, bank in (("xn", xnb, psx), ("y", yb, psy)):
                        key = ("ev_" + strm, g, ch - 2) if ch >= 2 else (("ev_" + strm, g - 1, NC - 2 + ch) if g > 0 else None)
                        if key is not None:
                            ctx.wait(tensor, key)
                        p = bank[ch % 2]
                        i = None
                        for j in range(4):
                            i = nc.tensor.transpose(p[:, 128 * j:128 * (j + 1)],
                                                    src[(4 * g + j) % NB][:, 128 * ch:128 * (ch + 1)],
                                                    idb[:])
                        v = ctx.inc(i, trn, 1)
                        M(("trn", g, ch, strm), trn, v)

                @block.vector
                def _(vector, g=g, ch=ch):
                    ctx.wait(vector, ("trn", g, ch, "xn"))
                    if g > 0:
                        ctx.wait(vector, ("st_xn", g - 1, ch))
                    i = vector.tensor_copy(ex["xn"][:, 512 * ch:512 * (ch + 1)], psx[ch % 2][:])
                    v = ctx.inc(i, evx, 1)
                    M(("ev_xn", g, ch), evx, v)

                @block.scalar
                def _(scalar, g=g, ch=ch):
                    ctx.wait(scalar, ("trn", g, ch, "y"))
                    if g > 0:
                        ctx.wait(scalar, ("st_h2", g - 1, ch))
                    scalar.activation(out=ex["h1"][:, 512 * ch:512 * (ch + 1)], in_=psy[ch % 2][:],
                                      func=AF.Copy, scale=lnw[:, ch:ch + 1])
                    i = scalar.activation(out=ex["h2"][:, 512 * ch:512 * (ch + 1)], in_=psy[ch % 2][:],
                                          func=AF.Copy, scale=lnw[:, 16 + ch:16 + ch + 1])
                    v = ctx.inc(i, evy, 1)
                    M(("ev_y", g, ch), evy, v)

                @block.gpsimd
                def _(gpsimd, g=g, ch=ch):
                    ctx.wait(gpsimd, ("ev_xn", g, ch))
                    ctx.inc(gpsimd.dma_start(out=out_fm["xn"][128 * ch:128 * (ch + 1), 512 * g:512 * (g + 1)],
                                             in_=ex["xn"][:, 512 * ch:512 * (ch + 1)]), st, 16)
                    ctx.wait(gpsimd, ("ev_y", g, ch))
                    ctx.inc(gpsimd.dma_start(out=out_fm["h1"][128 * ch:128 * (ch + 1), 512 * g:512 * (g + 1)],
                                             in_=ex["h1"][:, 512 * ch:512 * (ch + 1)]), st, 16)
                    ctx.inc(gpsimd.dma_start(out=out_fm["h2"][128 * ch:128 * (ch + 1), 512 * g:512 * (g + 1)],
                                             in_=ex["h2"][:, 512 * ch:512 * (ch + 1)]), st, 16)

        if t + 1 < NT or t == NT - 1:
            @block.sync
            def _(sync, t=t):
                tt = t + 1
                if tt < NT:
                    if tt >= 2:
                        ctx.wait(sync, ("s1", tt - 2))
                    ctx.inc(sync.dma_start(out=xt[tt % 2][:], in_=x_tm[128 * tt:128 * (tt + 1), :]), ld, 16)
                if 1 <= t < NT:
                    ctx.ckpt(sync, ld, ldc, ("ld_tile", t))

        @block.vector
        def _(vector, t=t):
            if t < NT:
                ctx.wait(vector, ("ld_tile", t))
                if t >= 2:
                    ctx.wait(vector, ("s1", t - 2))
                if t >= 4:
                    ctx.wait(vector, ("st_tm", t - 4))
                    ctx.wait(vector, ("s2", t - 4))
                if t >= NB:
                    ctx.wait(vector, ("trn", (t - NB) // 4, NC - 1, "y"))
                vector.tensor_reduce(va[t % 2][:], xt[t % 2][:], axis=AX.X, op=ALU.add)
                sepv(vector)
                i = vector.tensor_scalar(out=vm[t % 2][:], in0=va[t % 2][:], scalar1=-1.0 / D, scalar2=None, op0=ALU.mult)
                v = ctx.inc(i, c, 1)
                M(("d1", t), c, v)
            if 0 <= t - 1 < NT:
                u = t - 1
                ctx.wait(vector, ("s1", u))
                vector.reciprocal(out=vc[:], in_=vb[u % 2][:])
                sepv(vector)
                i = vector.tensor_scalar(out=xn[u % 4][:], in0=xn[u % 4][:], scalar1=vc[:], scalar2=None, op0=ALU.mult)
                v = ctx.inc(i, c, 1)
                M(("d2", u), c, v)
            if 0 <= t - 2 < NT:
                u = t - 2
                ctx.wait(vector, ("s2", u))
                vector.reciprocal(out=vc2[:], in_=vb2[u % 2][:])
                sepv(vector)
                i = vector.tensor_scalar(out=yb[u % NB][:], in0=xn[u % 4][:], scalar1=vc2[:], scalar2=None, op0=ALU.mult)
                v = ctx.inc(i, c, 1)
                M(("d3", u), c, v)

        @block.scalar
        def _(scalar, t=t):
            if t < NT:
                ctx.wait(scalar, ("d1", t))
                scalar.activation(out=xn[t % 4][:], in_=xt[t % 2][:], func=AF.Identity, bias=vm[t % 2][:])
                scalar.activation(out=sq[:], in_=xn[t % 4][:], func=AF.Square, accum_out=vs[t % 2][:])
                sepa(scalar)
                i = scalar.activation(out=vb[t % 2][:], in_=vs[t % 2][:], func=AF.Sqrt, scale=1.0 / D, bias=veps5[:])
                v = ctx.inc(i, a, 1)
                M(("s1", t), a, v)
            if 0 <= t - 1 < NT:
                u = t - 1
                ctx.wait(scalar, ("d2", u))
                scalar.activation(out=sq[:], in_=xn[u % 4][:], func=AF.Square, accum_out=vs2[u % 2][:])
                scalar.activation(out=xnb[u % NB][:], in_=xn[u % 4][:], func=AF.Copy)
                i = scalar.activation(out=vb2[u % 2][:], in_=vs2[u % 2][:], func=AF.Sqrt, scale=1.0 / D, bias=veps6[:])
                v = ctx.inc(i, a, 1)
                M(("s2", u), a, v)

        if 0 <= t - 1 < NT:
            @block.gpsimd
            def _(gpsimd, t=t):
                u = t - 1
                ctx.wait(gpsimd, ("d2", u))
                ctx.inc(gpsimd.dma_start(out=xn_tm[128 * u:128 * (u + 1), :], in_=xn[u % 4][:]), st, 16)
                if u >= 1:
                    ctx.ckpt(gpsimd, st, stc, ("st_tm", u - 1))


def stage1_sems(tag="s1"):
    return ["ld", "wld", "mm", "ev", "c", "a", "trn", "g", "evx", "evy", "st", "ldc", "wldc", "stc", "s10_cc"]




def emit_fm_matmul(ctx, ap, stack, tag, deps, x_name, out_name, segs,
                   x_rows=D, t_cols=T, out_row0=0, evac_view=None):
    ctx.scope = tag
    nc, block = ctx.nc, ctx.block
    sb = lambda n, s, d: alloc_sbuf(nc, stack, f"{tag}_{n}", s, d)
    psa = lambda n, s, d: alloc_psum(nc, stack, f"{tag}_{n}", s, d)

    NCX = x_rows // 128
    NH = t_cols // 512 // 2
    HALF = t_cols // 2

    X = [sb(f"x{c}", [128, t_cols], BF16) for c in range(NCX)]
    wsl = [sb(f"w{j}", [128, NCX * 128], BF16) for j in range(2)]
    ob = [sb(f"o{j}", [128, t_cols], BF16) for j in range(2)]
    ps = [psa(f"ps{j}", [128, 512], F32) for j in range(2 * NH)]

    ld, wld, mm, ev, st = ("ld", "wld", "mm", "ev", "st")
    ldc, wldc, stc = ("ldc", "wldc", "stc")
    M = ctx.mark
    out = ap[out_name]

    chunks = []
    rows_used = {}
    for seg in segs:
        wt_name, f_lo, f_hi, func, scale = seg[:5]
        oname = seg[5] if len(seg) > 5 else out_name
        view = seg[6] if len(seg) > 6 else evac_view
        base = out_row0 if oname == out_name else 0
        for fc in range(f_lo // 128, f_hi // 128):
            r = rows_used.get(oname, 0)
            rows_used[oname] = r + 1
            chunks.append((ap[wt_name], fc, func, scale, ap[oname], base + 128 * r, view))
    NF = len(chunks)

    @block.sync
    def _(sync):
        for s_, v_ in deps:
            sync.wait_ge(ctx.h[s_], v_)
        for c in range(NCX):
            ctx.inc(sync.dma_start(out=X[c][:], in_=ap[x_name][128 * c:128 * (c + 1), :]), ld, 16)
        ctx.ckpt(sync, ld, ldc, ("xload",))

    @block.gpsimd
    def _(gpsimd):
        for s_, v_ in deps:
            gpsimd.wait_ge(ctx.h[s_], v_)
        wt, fc = chunks[0][0], chunks[0][1]
        wr = wt.rearrange("(c p) f -> p c f", p=128)
        i = gpsimd.dma_start(out=wsl[0][:].rearrange("p (c f) -> p c f", f=128),
                             in_=wr[:, :, 128 * fc:128 * (fc + 1)])
        ctx.inc(i, wld, 16)
        ctx.ckpt(gpsimd, wld, wldc, ("wload", 0))

    @block.tensor
    def _(tensor):
        for s_, v_ in deps:
            tensor.wait_ge(ctx.h[s_], v_)

    @block.vector
    def _(vector):
        for s_, v_ in deps:
            vector.wait_ge(ctx.h[s_], v_)

    @block.scalar
    def _(scalar):
        for s_, v_ in deps:
            scalar.wait_ge(ctx.h[s_], v_)

    for f in range(NF):
        wt, fc, func, scale, oap, orow, view = chunks[f]

        if f + 1 < NF:
            @block.gpsimd
            def _(gpsimd, f=f):
                wtn, fcn = chunks[f + 1][0], chunks[f + 1][1]
                if f >= 1:
                    ctx.wait(gpsimd, ("mm_half", f - 1, 1))
                wr = wtn.rearrange("(c p) f -> p c f", p=128)
                i = gpsimd.dma_start(out=wsl[(f + 1) % 2][:].rearrange("p (c f) -> p c f", f=128),
                                     in_=wr[:, :, 128 * fcn:128 * (fcn + 1)])
                ctx.inc(i, wld, 16)
                ctx.ckpt(gpsimd, wld, wldc, ("wload", f + 1))

        @block.tensor
        def _(tensor, f=f):
            if f == 0:
                ctx.wait(tensor, ("xload",))
            ctx.wait(tensor, ("wload", f))
            for h in range(2):
                if f >= 1:
                    ctx.wait(tensor, ("ev", f - 1, h))
                i = None
                for c in range(NCX):
                    for b in range(NH):
                        i = nc.tensor.matmul(ps[NH * h + b][:],
                                             wsl[f % 2][:, 128 * c:128 * (c + 1)],
                                             X[c][:, HALF * h + 512 * b:HALF * h + 512 * (b + 1)],
                                             start=(c == 0), stop=(c == NCX - 1))
                v = ctx.inc(i, mm, 1)
                M(("mm_half", f, h), mm, v)

        @block.scalar
        def _(scalar, f=f):
            for h in range(2):
                ctx.wait(scalar, ("mm_half", f, h))
                if f >= 2:
                    ctx.wait(scalar, ("st", f - 2))
                i = None
                for b in range(NH):
                    kw = {}
                    if scale is not None:
                        kw["scale"] = scale
                    if view is None:
                        dst = ob[f % 2][:, HALF * h + 512 * b:HALF * h + 512 * (b + 1)]
                    else:
                        dst = view(ob[f % 2], NH * h + b)
                    i = scalar.activation(out=dst, in_=ps[NH * h + b][:], func=func, **kw)
                v = ctx.inc(i, ev, 1)
                M(("ev", f, h), ev, v)

        @block.gpsimd
        def _(gpsimd, f=f):
            ctx.wait(gpsimd, ("ev", f, 1))
            ctx.inc(gpsimd.dma_start(out=oap[orow:orow + 128, :],
                                     in_=ob[f % 2][:]), st, 16)
            if f >= 1:
                ctx.ckpt(gpsimd, st, stc, ("st", f - 1))

    return {"st_final": (st, ctx.n[st])}


def fm_matmul_sems(tag):
    return ["ld", "wld", "mm", "ev", "c", "a", "trn", "g", "evx", "evy", "st", "ldc", "wldc", "stc", "s10_cc"]




def emit_stage3(ctx, ap, stack, deps, tag="s3"):
    ctx.scope = tag
    nc, block = ctx.nc, ctx.block
    sb = lambda n, s, d: alloc_sbuf(nc, stack, f"{tag}_{n}", s, d)
    psa = lambda n, s, d: alloc_psum(nc, stack, f"{tag}_{n}", s, d)

    qkv = ap["qkv0_fm"]
    out_fm = ap["attn0_fm"]

    k4 = [sb(f"k4{j}", [128, 2048], BF16) for j in range(2)]
    v4 = [sb(f"v4{j}", [128, 2048], BF16) for j in range(2)]
    q4 = [sb(f"q4{j}", [128, 2048], BF16) for j in range(2)]
    kt = [sb(f"kt{j}", [128, 2048], BF16) for j in range(2)]
    vt = [sb(f"vt{j}", [128, 16 * 129], BF16) for j in range(2)]
    vk4m = [sb(f"vk4m{j}", [128, 128], BF16) for j in range(2)]
    ks4 = [sb(f"ks4{j}", [128, 4], BF16) for j in range(2)]
    rec = [sb(f"rec{j}", [4, 512], F32) for j in range(2)]
    rec2 = [sb(f"rec2{j}", [4, 512], F32) for j in range(2)]
    recb = [sb(f"recb{j}", [4, 512], BF16) for j in range(2)]
    bcs = [sb(f"bcs{j}", [128, 512], BF16) for j in range(2)]
    attn = [sb(f"attn{j}", [128, 2048], BF16) for j in range(2)]
    idb = sb("idb", [128, 128], BF16)
    m128 = sb("m128", [128, 128], BF16)
    m4 = sb("m4", [128, 4], F32)
    s4 = sb("s4", [4, 128], BF16)
    dmy = [sb(f"dmy{j}", [128, 32], F32) for j in range(2)]

    pst = [psa(f"pst{j}", [128, 512], BF16) for j in range(2)]
    pvk = psa("pvk", [128, 129], F32)
    pa = [psa(f"pa{j}", [128, 512], F32) for j in range(2)]
    pb = [psa(f"pb{j}", [4, 512], F32) for j in range(2)]
    pbc = psa("pbc", [128, 512], F32)

    ld, c, a, trn, g, st = ("ld", "c", "a", "trn", "g", "st")
    ldc, stc = "ldc", "stc"
    M = ctx.mark
    NIT = 32

    @block.sync
    def _(sync):
        for s_, v_ in deps:
            sync.wait_ge(ctx.h[s_], v_)
        for nm, tile in (("identb", idb), ("mask128", m128), ("mask4", m4), ("sel4", s4)):
            ctx.inc(sync.dma_start(out=tile[:], in_=ap[nm]), ld, 16)
        for j, (tile, r0) in enumerate(((k4[0], 2048), (v4[0], 4096), (q4[0], 0))):
            ctx.inc(sync.dma_start(out=tile[:], in_=qkv[r0:r0 + 128, 0:2048]), ld, 16)
        ctx.ckpt(sync, ld, ldc, ("ld", 0))

    for eng in ("tensor", "vector", "scalar", "gpsimd"):
        @getattr(block, eng)
        def _(e):
            for s_, v_ in deps:
                e.wait_ge(ctx.h[s_], v_)

    @block.vector
    def _(vector):
        vector.memset(vt[0][:], 1.0)
        vector.memset(vt[1][:], 1.0)
        vector.memset(dmy[0][:], 0.0)

    for it in range(NIT):
        b, cb = it // 16, it % 16
        bb = it % 2

        @block.sync
        def _(sync, it=it):
            if it >= 1:
                ctx.ckpt(sync, ld, ldc, ("ld", it))
            if it + 1 < NIT:
                nb, ncb = (it + 1) // 16, (it + 1) % 16
                if it >= 1:
                    ctx.wait(sync, ("trans", it - 1, "k", 3))
                    ctx.wait(sync, ("trans", it - 1, "v", 3))
                    ctx.wait(sync, ("out2", it - 1, 3))
                for tile, r0 in ((k4[(it + 1) % 2], 2048), (v4[(it + 1) % 2], 4096), (q4[(it + 1) % 2], 0)):
                    ctx.inc(sync.dma_start(out=tile[:], in_=qkv[r0 + 128 * ncb:r0 + 128 * (ncb + 1),
                                                                2048 * nb:2048 * (nb + 1)]), ld, 16)

        for strm, src, dst in (("k", k4, kt), ("v", v4, vt)):
            for j0 in range(4):
                @block.tensor
                def _(tensor, it=it, strm=strm, j0=j0, src=src):
                    if strm == "k" and j0 == 0:
                        ctx.wait(tensor, ("ld", it))
                        if it >= 2:
                            ctx.wait(tensor, ("vk", it - 2))
                            ctx.wait(tensor, ("vtevac", it - 2))
                    key = ("tevac", it, strm, j0 - 2) if j0 >= 2 else \
                          (("tevac", it, "k", 2 + j0) if strm == "v" else
                           (("tevac", it - 1, "v", 2 + j0) if it >= 1 else None))
                    if key is not None:
                        ctx.wait(tensor, key)
                    i = None
                    for jj in range(4):
                        j = 4 * j0 + jj
                        i = nc.tensor.transpose(pst[j0 % 2][:, 128 * jj:128 * (jj + 1)],
                                                src[it % 2][:, 128 * j:128 * (j + 1)], idb[:])
                    v = ctx.inc(i, trn, 1)
                    M(("trans", it, strm, j0), trn, v)

                if strm == "k":
                    @block.vector
                    def _(vector, it=it, j0=j0, dst=dst):
                        ctx.wait(vector, ("trans", it, "k", j0))
                        if j0 == 0 and it >= 2:
                            ctx.wait(vector, ("vk", it - 2))
                        i = vector.tensor_copy(dst[it % 2][:, 512 * j0:512 * (j0 + 1)], pst[j0 % 2][:])
                        v = ctx.inc(i, c, 1)
                        M(("tevac", it, "k", j0), c, v)
                else:
                    @block.scalar
                    def _(scalar, it=it, j0=j0, dst=dst):
                        ctx.wait(scalar, ("trans", it, "v", j0))
                        if j0 == 0 and it >= 2:
                            ctx.wait(scalar, ("vk", it - 2))
                        vv = dst[it % 2][:].rearrange("p (j e) -> p j e", e=129)
                        i = scalar.activation(out=vv[:, 4 * j0:4 * (j0 + 1), 0:128],
                                              in_=pst[j0 % 2][:].rearrange("p (j e) -> p j e", e=128),
                                              func=AF.Copy)
                        v = ctx.inc(i, a, 1)
                        M(("tevac", it, "v", j0), a, v)
                        if j0 == 3:
                            M(("vtevac", it), a, v)

        @block.tensor
        def _(tensor, it=it):
            ctx.wait(tensor, ("tevac", it, "k", 3))
            ctx.wait(tensor, ("tevac", it, "v", 3))
            if it >= 1:
                ctx.wait(tensor, ("vkevac", it - 1))
            i = None
            vv = vt[it % 2][:].rearrange("p (j e) -> p j e", e=129)
            for j in range(16):
                i = nc.tensor.matmul(pvk[:], kt[it % 2][:, 128 * j:128 * (j + 1)],
                                     vv[:, j, :], start=(j == 0), stop=(j == 15))
            v = ctx.inc(i, trn, 1)
            M(("vk", it), trn, v)

        @block.vector
        def _(vector, it=it):
            ctx.wait(vector, ("vk", it))
            if it >= 2:
                ctx.wait(vector, ("out2", it - 2, 3))
            vector.tensor_tensor(out=vk4m[bb][:], in0=pvk[:, 0:128], in1=m128[:], op=ALU.mult)
            i = vector.tensor_scalar(out=ks4[bb][:], in0=m4[:], scalar1=pvk[:, 128:129], scalar2=None, op0=ALU.mult)
            v = ctx.inc(i, c, 1)
            M(("vkevac", it), c, v)

        for ch in range(4):
            @block.tensor
            def _(tensor, it=it, ch=ch):
                if ch == 0:
                    ctx.wait(tensor, ("vkevac", it))
                key = ("attnmul", it, ch - 2) if ch >= 2 else (("attnmul", it - 1, 2 + ch) if it >= 1 else None)
                if key is not None:
                    ctx.wait(tensor, key)
                keyb = ("div1", it, ch - 2) if ch >= 2 else (("div1", it - 1, 2 + ch) if it >= 1 else None)
                if keyb is not None:
                    ctx.wait(tensor, keyb)
                i = nc.tensor.matmul(pa[ch % 2][:], vk4m[bb][:], q4[it % 2][:, 512 * ch:512 * (ch + 1)],
                                     start=True, stop=True)
                i = nc.tensor.matmul(pb[ch % 2][:], ks4[bb][:], q4[it % 2][:, 512 * ch:512 * (ch + 1)],
                                     start=True, stop=True)
                v = ctx.inc(i, trn, 1)
                M(("out2", it, ch), trn, v)

            @block.vector
            def _(vector, it=it, ch=ch):
                ctx.wait(vector, ("out2", it, ch))
                key = ("bc", it, ch - 2) if ch >= 2 else (("bc", it - 1, 2 + ch) if it >= 1 else None)
                if key is not None:
                    ctx.wait(vector, key)
                vector.tensor_scalar(out=rec[ch % 2][:], in0=pb[ch % 2][:], scalar1=1e-15, scalar2=None, op0=ALU.add)
                vector.tensor_copy(dmy[1][:], dmy[0][:])
                vector.reciprocal(out=rec2[ch % 2][:], in_=rec[ch % 2][:])
                vector.tensor_copy(dmy[1][:], dmy[0][:])
                i = vector.tensor_copy(recb[ch % 2][:], rec2[ch % 2][:])
                v = ctx.inc(i, c, 1)
                M(("div1", it, ch), c, v)

            @block.tensor
            def _(tensor, it=it, ch=ch):
                ctx.wait(tensor, ("div1", it, ch))
                if not (it == 0 and ch == 0):
                    pc, pch = (it, ch - 1) if ch >= 1 else (it - 1, 3)
                    ctx.wait(tensor, ("bcevac", pc, pch))
                i = nc.tensor.matmul(pbc[:], s4[:], recb[ch % 2][:], start=True, stop=True)
                v = ctx.inc(i, trn, 1)
                M(("bc", it, ch), trn, v)

            @block.scalar
            def _(scalar, it=it, ch=ch):
                ctx.wait(scalar, ("bc", it, ch))
                key = ("attnmul", it, ch - 2) if ch >= 2 else (("attnmul", it - 1, 2 + ch) if it >= 1 else None)
                if key is not None:
                    ctx.wait(scalar, key)
                i = scalar.activation(out=bcs[ch % 2][:], in_=pbc[:], func=AF.Copy)
                v = ctx.inc(i, a, 1)
                M(("bcevac", it, ch), a, v)

            @block.vector
            def _(vector, it=it, ch=ch):
                ctx.wait(vector, ("bcevac", it, ch))
                if ch == 0 and it >= 2:
                    ctx.wait(vector, ("attst", it - 2))
                i = vector.tensor_tensor(out=attn[it % 2][:, 512 * ch:512 * (ch + 1)],
                                         in0=pa[ch % 2][:], in1=bcs[ch % 2][:], op=ALU.mult)
                v = ctx.inc(i, c, 1)
                M(("attnmul", it, ch), c, v)

        @block.gpsimd
        def _(gpsimd, it=it):
            if it >= 1:
                ctx.ckpt(gpsimd, st, stc, ("attst", it - 1))
            ctx.wait(gpsimd, ("attnmul", it, 3))
            ctx.inc(gpsimd.dma_start(out=out_fm[128 * cb:128 * (cb + 1), 2048 * b:2048 * (b + 1)],
                                     in_=attn[it % 2][:]), st, 16)


def stage3_sems(tag="s3"):
    return ["ld", "wld", "mm", "ev", "c", "a", "trn", "g", "evx", "evy", "st", "ldc", "wldc", "stc", "s10_cc"]




def emit_tm_matmul(ctx, ap, stack, tag, deps, x_name, wt_name, out_name,
                   bias_name=None, addins=(), din=D, fout=D, out_dt=F32,
                   out_view=None):
    ctx.scope = tag
    nc, block = ctx.nc, ctx.block
    sb = lambda n, s, d: alloc_sbuf(nc, stack, f"{tag}_{n}", s, d)
    psa = lambda n, s, d: alloc_psum(nc, stack, f"{tag}_{n}", s, d)

    NCX = din // 128
    NFB = fout // 512
    assert NFB <= 4
    NSET = 2 if 2 * NFB + (1 if bias_name else 0) <= 8 else 1

    W = [sb(f"w{c}", [128, fout], BF16) for c in range(NCX)]
    X = [sb(f"x{j}", [128, 4 * 128], BF16) for j in range(NCX)]
    X2 = [sb(f"x2_{j}", [128, 4 * 128], BF16) for j in range(NCX)]
    ADD = [[sb(f"ad{k}_{j}", [128, fout], F32) for j in range(2)] for k in range(len(addins))]
    OB = [sb(f"ob{j}", [128, fout], F32 if out_dt == F32 else out_dt) for j in range(2)]
    ps = [psa(f"ps{j}", [128, 512], F32) for j in range(NSET * NFB)]
    bias_b = sb("biasb", [128, fout], F32) if bias_name else None
    ones1 = sb("ones1", [1, 128], BF16) if bias_name else None
    brow = sb("brow", [1, fout], BF16) if bias_name else None
    pbias = psa("pbias", [128, 512], F32) if bias_name else None

    ld, wld, mm, ev, st = ("ld", "wld", "mm", "ev", "st")
    ldc, wldc, stc = ("ldc", "wldc", "stc")
    M = ctx.mark
    out = ap[out_name]
    xap = ap[x_name]
    XB = [X, X2]

    @block.sync
    def _(sync):
        for s_, v_ in deps:
            sync.wait_ge(ctx.h[s_], v_)
        if bias_name:
            ctx.inc(sync.dma_start(out=ones1[:], in_=ap["ones1b"]), ld, 16)
        for c in range(NCX):
            ctx.inc(sync.dma_start(out=X[c][:], in_=xap[128 * c:128 * (c + 1), 0:512]), ld, 16)
        for k, anm in enumerate(addins):
            srcap = ap[anm][0:128, :] if out_view is None else out_view(ap[anm], 0)
            ctx.inc(sync.dma_start(out=ADD[k][0][:], in_=srcap), ld, 16)
        ctx.ckpt(sync, ld, ldc, ("xload", 0))
        if addins:
            M(("adld", 0), ldc, ctx.n[ldc])
        if bias_name:
            M(("auxld0",), ldc, ctx.n[ldc])

    @block.gpsimd
    def _(gpsimd):
        for s_, v_ in deps:
            gpsimd.wait_ge(ctx.h[s_], v_)
        if bias_name:
            ctx.inc(gpsimd.dma_start(out=brow[:], in_=ap[bias_name]), wld, 16)
        wr = ap[wt_name].rearrange("(c p) f -> p c f", p=128)
        for c in range(NCX):
            ctx.inc(gpsimd.dma_start(out=W[c][:], in_=wr[:, c, :]), wld, 16)
        ctx.ckpt(gpsimd, wld, wldc, ("wload",))
        if bias_name:
            M(("auxld1",), wldc, ctx.n[wldc])

    for eng in ("tensor", "vector", "scalar"):
        @getattr(block, eng)
        def _(e):
            for s_, v_ in deps:
                e.wait_ge(ctx.h[s_], v_)

    if bias_name:
        for fb in range(fout // 512):
            @block.tensor
            def _(tensor, fb=fb):
                if fb == 0:
                    ctx.wait(tensor, ("auxld0",))
                    ctx.wait(tensor, ("auxld1",))
                else:
                    ctx.wait(tensor, ("biasev", fb - 1))
                i = nc.tensor.matmul(pbias[:], ones1[:], brow[:, 512 * fb:512 * (fb + 1)],
                                     start=True, stop=True)
                ctx.inc(i, mm, 1)
                M(("biasmm", fb), mm, ctx.n[mm])

            @block.vector
            def _(vector, fb=fb):
                ctx.wait(vector, ("biasmm", fb))
                i = vector.tensor_copy(bias_b[:, 512 * fb:512 * (fb + 1)], pbias[:])
                ctx.inc(i, ev, 1)
                M(("biasev", fb), ev, ctx.n[ev])

    for tt in range(NT):
        slab, off = tt // 4, tt % 4
        if off == 0 and (slab + 1 < NT // 4 or slab == NT // 4 - 1):
            @block.sync
            def _(sync, slab=slab):
                if slab + 1 < NT // 4:
                    if slab >= 1:
                        ctx.wait(sync, ("mmdone", 4 * (slab - 1) + 3))
                    for c in range(NCX):
                        ctx.inc(sync.dma_start(out=XB[(slab + 1) % 2][c][:],
                                               in_=xap[128 * c:128 * (c + 1), 512 * (slab + 1):512 * (slab + 2)]), ld, 16)
                    ctx.ckpt(sync, ld, ldc, ("xload", slab + 1))

        if addins:
            @block.sync
            def _(sync, tt=tt):
                if tt + 1 < NT:
                    if tt >= 1:
                        ctx.wait(sync, ("ev", tt - 1, NFB - 1))
                    for k, anm in enumerate(addins):
                        srcap = (ap[anm][128 * (tt + 1):128 * (tt + 2), :] if out_view is None
                                 else out_view(ap[anm], tt + 1))
                        ctx.inc(sync.dma_start(out=ADD[k][(tt + 1) % 2][:], in_=srcap), ld, 16)
                    ctx.ckpt(sync, ld, ldc, ("adld", tt + 1))

        @block.tensor
        def _(tensor, tt=tt, slab=slab, off=off):
            if bias_name and tt == 0:
                ctx.wait(tensor, ("biasev", fout // 512 - 1))
            ctx.wait(tensor, ("xload", slab))
            if tt == 0:
                ctx.wait(tensor, ("wload",))
            i = None
            for fb in range(NFB):
                if tt >= NSET:
                    ctx.wait(tensor, ("ev", tt - NSET, fb))
                for c in range(NCX):
                    i = nc.tensor.matmul(ps[NFB * (tt % NSET) + fb][:],
                                         XB[slab % 2][c][:, 128 * off:128 * (off + 1)],
                                         W[c][:, 512 * fb:512 * (fb + 1)],
                                         start=(c == 0), stop=(c == NCX - 1))
                v = ctx.inc(i, mm, 1)
                M(("mmf", tt, fb), mm, v)
            M(("mmdone", tt), mm, ctx.n[mm])

        @block.vector
        def _(vector, tt=tt):
            if addins:
                ctx.wait(vector, ("adld", tt))
            if tt >= 2:
                ctx.wait(vector, ("st", tt - 2))
            nsrc = len(addins) + (1 if bias_name else 0)
            for fb in range(NFB):
                ctx.wait(vector, ("mmf", tt, fb))
            for step in range(max(1, nsrc)):
                for fb in range(NFB):
                    sl = slice(512 * fb, 512 * (fb + 1))
                    pp = ps[NFB * (tt % NSET) + fb][:]
                    srcs = ([bias_b[:, sl]] if bias_name else []) + \
                           [ADD[k][tt % 2][:, sl] for k in range(len(addins))]
                    if nsrc == 0:
                        i = vector.tensor_copy(OB[tt % 2][:, sl], pp)
                    elif step == 0:
                        i = vector.tensor_tensor(out=OB[tt % 2][:, sl], in0=pp, in1=srcs[0], op=ALU.add)
                    else:
                        i = vector.tensor_tensor(out=OB[tt % 2][:, sl], in0=OB[tt % 2][:, sl], in1=srcs[step], op=ALU.add)
                    if step == max(1, nsrc) - 1:
                        v = ctx.inc(i, ev, 1)
                        M(("ev", tt, fb), ev, v)

        @block.gpsimd
        def _(gpsimd, tt=tt):
            if tt >= 1:
                ctx.ckpt(gpsimd, st, stc, ("st", tt - 1))
            ctx.wait(gpsimd, ("ev", tt, NFB - 1))
            dst = out[128 * tt:128 * (tt + 1), :] if out_view is None else out_view(out, tt)
            src = OB[tt % 2][:]
            ctx.inc(gpsimd.dma_start(out=dst, in_=src), st, 16)


def tm_matmul_sems(tag):
    return ["ld", "wld", "mm", "ev", "c", "a", "trn", "g", "evx", "evy", "st", "ldc", "wldc", "stc", "s10_cc"]




def emit_stage6(ctx, ap, stack, deps, tag="s6"):
    ctx.scope = tag
    nc, block = ctx.nc, ctx.block
    sb = lambda n, s, d: alloc_sbuf(nc, stack, f"{tag}_{n}", s, d)
    psa = lambda n, s, d: alloc_psum(nc, stack, f"{tag}_{n}", s, d)

    qkv = ap["qkv1_fm"]
    out_fm = ap["attnPV_fm"]

    qs = [sb(f"q{j}", [128, 2048], BF16) for j in range(2)]
    ks = [sb(f"k{j}", [128, 2048], BF16) for j in range(2)]
    vs = [sb(f"v{j}", [128, 2048], BF16) for j in range(2)]
    vt = [sb(f"vt{j}", [128, 2048], BF16) for j in range(2)]
    sq2 = sb("sq2", [128, 2048], BF16)
    P = [[sb(f"P{s}_{j}", [128, 2048], BF16) for j in range(4)] for s in range(2)]
    ptg = sb("ptg", [128, 16 * 512], BF16)
    ob = [sb(f"ob{j}", [128, 512], BF16) for j in range(2)]
    tri = sb("tri", [128, 128], BF16)
    idb = sb("idb", [128, 128], BF16)
    ones128 = sb("ones128", [128, 128], BF16)
    kmaxv = sb("kmaxv", [128, 2], F32)
    qmaxv = sb("qmaxv", [128, 2], F32)
    bnd = sb("bnd", [128, 2], F32)
    nb = [sb(f"nb{j}", [128, 1], F32) for j in range(2)]
    ls = [sb(f"l{j}", [128, 1], F32) for j in range(8)]
    lsb = [sb(f"lb{j}", [128, 1], F32) for j in range(8)]
    li = [sb(f"li{j}", [128, 1], F32) for j in range(8)]
    dmy = [sb(f"dmy{j}", [128, 32], F32) for j in range(2)]

    pss = psa("pss", [128, 2048], F32)
    pso = psa("pso", [128, 512], F32)
    ptr = [psa(f"ptr{j}", [128, 512], BF16) for j in range(2)]
    pnm = psa("pnm", [128, 512], F32)

    ld, c, a, trn, st = ("ld", "c", "a", "trn", "st")
    ldc, stc = "ldc", "stc"
    M = ctx.mark
    NIT = 32
    sepv = lambda e: e.tensor_copy(dmy[1][:], dmy[0][:])

    @block.sync
    def _(sync):
        for s_, v_ in deps:
            sync.wait_ge(ctx.h[s_], v_)
        for nm, tile in (("tri_maskb", tri), ("identb", idb),
                         ("ones128b", ones128)):
            ctx.inc(sync.dma_start(out=tile[:], in_=ap[nm]), ld, 16)
        for tile, r0 in ((qs[0], 0), (ks[0], 2048), (vs[0], 4096)):
            ctx.inc(sync.dma_start(out=tile[:], in_=qkv[r0:r0 + 128, 0:2048]), ld, 16)
        ctx.ckpt(sync, ld, ldc, ("ld", 0))

    for eng in ("tensor", "vector", "scalar", "gpsimd"):
        @getattr(block, eng)
        def _(e):
            for s_, v_ in deps:
                e.wait_ge(ctx.h[s_], v_)

    @block.vector
    def _(vector):
        vector.memset(dmy[0][:], 0.0)

    def emit_norms(j):
        for which, src in (("k", ks), ("q", qs)):
            mx = kmaxv if which == "k" else qmaxv
            @block.vector
            def _(vector, j=j, which=which, src=src):
                if which == "k":
                    ctx.wait(vector, ("ld", j))
                    if j >= 1:
                        ctx.wait(vector, ("nmm", j - 1, "q", 3))
                else:
                    ctx.wait(vector, ("nmm", j, "k", 3))
                i = vector.tensor_tensor(out=sq2[:], in0=src[j % 2][:], in1=src[j % 2][:], op=ALU.mult)
                v = ctx.inc(i, c, 1)
                M(("sq_", j, which), c, v)

            for ch in range(4):
                @block.tensor
                def _(tensor, j=j, which=which, ch=ch):
                    ctx.wait(tensor, ("sq_", j, which))
                    key = ("nrd", j, which, ch - 1) if ch >= 1 else \
                          (("nrd", j, "k", 3) if which == "q" else
                           (("nrd", j - 1, "q", 3) if j >= 1 else None))
                    if key is not None:
                        ctx.wait(tensor, key)
                    i = nc.tensor.matmul(pnm[:], ones128[:], sq2[:, 512 * ch:512 * (ch + 1)],
                                         start=True, stop=True)
                    v = ctx.inc(i, trn, 1)
                    M(("nmm", j, which, ch), trn, v)

                @block.vector
                def _(vector, j=j, which=which, ch=ch, mx=mx):
                    ctx.wait(vector, ("nmm", j, which, ch))
                    vector.tensor_reduce(mx[:, 0:1], pnm[:], axis=AX.X, op=ALU.max)
                    sepv(vector)
                    if ch == 0:
                        i = vector.tensor_copy(mx[:, 1:2], mx[:, 0:1])
                    else:
                        i = vector.tensor_tensor(out=mx[:, 1:2], in0=mx[:, 1:2], in1=mx[:, 0:1], op=ALU.max)
                    v = ctx.inc(i, c, 1)
                    M(("nrd", j, which, ch), c, v)

        @block.vector
        def _(vector, j=j):
            ctx.wait(vector, ("nrd", j, "q", 3))
            i = vector.tensor_tensor(out=bnd[:, 0:1], in0=kmaxv[:, 1:2], in1=qmaxv[:, 1:2], op=ALU.mult)
            v = ctx.inc(i, c, 1)
            M(("bnd2", j), c, v)

        @block.scalar
        def _(scalar, j=j):
            ctx.wait(scalar, ("bnd2", j))
            i = scalar.activation(out=bnd[:, 1:2], in_=bnd[:, 0:1], func=AF.Sqrt, scale=1.0)
            v = ctx.inc(i, a, 1)
            M(("bnds", j), a, v)

        @block.vector
        def _(vector, j=j):
            ctx.wait(vector, ("bnds", j))
            i = vector.tensor_scalar(out=nb[j % 2][:], in0=bnd[:, 1:2], scalar1=-1.0, scalar2=None, op0=ALU.mult)
            v = ctx.inc(i, c, 1)
            M(("nbr", j), c, v)

    emit_norms(0)
    bank_owner = {}

    for it in range(NIT):
        b, qh = it // 16, it % 16
        bb = it % 2

        @block.sync
        def _(sync, it=it):
            if it + 1 < NIT:
                nb_, nqh = (it + 1) // 16, (it + 1) % 16
                if it >= 1:
                    ctx.wait(sync, ("pv", it - 1, 3))
                    ctx.wait(sync, ("sq_", it - 1, "q"))
                for tile, r0 in ((qs[(it + 1) % 2], 0), (ks[(it + 1) % 2], 2048), (vs[(it + 1) % 2], 4096)):
                    ctx.inc(sync.dma_start(out=tile[:], in_=qkv[r0 + 128 * nqh:r0 + 128 * (nqh + 1),
                                                                2048 * nb_:2048 * (nb_ + 1)]), ld, 16)
                ctx.ckpt(sync, ld, ldc, ("ld", it + 1))

        for g4 in range(4):
            @block.tensor
            def _(tensor, it=it, g4=g4):
                if g4 == 0:
                    ctx.wait(tensor, ("ld", it))
                    if it >= 2:
                        ctx.wait(tensor, ("pv", it - 2, 3))
                key = ("vte", it, g4 - 2) if g4 >= 2 else (("pte", it - 1, "last") if it >= 1 else None)
                if key is not None:
                    ctx.wait(tensor, key)
                i = None
                for jj in range(4):
                    j = 4 * g4 + jj
                    i = nc.tensor.transpose(ptr[g4 % 2][:, 128 * jj:128 * (jj + 1)],
                                            vs[it % 2][:, 128 * j:128 * (j + 1)], idb[:])
                v = ctx.inc(i, trn, 1)
                M(("vtr", it, g4), trn, v)

            @block.vector
            def _(vector, it=it, g4=g4):
                ctx.wait(vector, ("vtr", it, g4))
                if g4 == 0 and it >= 2:
                    ctx.wait(vector, ("pv", it - 2, 3))
                i = vector.tensor_copy(vt[it % 2][:, 512 * g4:512 * (g4 + 1)], ptr[g4 % 2][:])
                v = ctx.inc(i, c, 1)
                M(("vte", it, g4), c, v)

        def emit_ptrpv(qg, it=it, b=b, qh=qh):
            for kc128 in range(4 * (qg + 1)):
                @block.tensor
                def _(tensor, it=it, qg=qg, kc128=kc128):
                    if kc128 == 0:
                        for qt in range(4):
                            ctx.wait(tensor, ("psc", it, qg, qt))
                    key = ("pte", it, qg, kc128 - 2) if kc128 >= 2 else \
                          (("vte", it, 2 + kc128) if qg == 0 else ("pte", it, qg - 1, 4 * qg - 2 + kc128))
                    ctx.wait(tensor, key)
                    i = None
                    for qt in range(4):
                        i = nc.tensor.transpose(ptr[kc128 % 2][:, 128 * qt:128 * (qt + 1)],
                                                P[qg % 2][qt][:, 128 * kc128:128 * (kc128 + 1)], idb[:])
                    v = ctx.inc(i, trn, 1)
                    M(("ptr", it, qg, kc128), trn, v)

                @block.vector
                def _(vector, it=it, qg=qg, kc128=kc128):
                    ctx.wait(vector, ("ptr", it, qg, kc128))
                    i = vector.tensor_copy(ptg[:, 512 * kc128:512 * (kc128 + 1)], ptr[kc128 % 2][:])
                    v = ctx.inc(i, c, 1)
                    M(("pte", it, qg, kc128), c, v)
                    if qg == 3 and kc128 == 15:
                        M(("pte", it, "last"), c, v)

            @block.tensor
            def _(tensor, it=it, qg=qg):
                for kc128 in range(4 * (qg + 1)):
                    ctx.wait(tensor, ("pte", it, qg, kc128))
                if not (it == 0 and qg == 0):
                    pit, pqg = (it, qg - 1) if qg >= 1 else (it - 1, 3)
                    ctx.wait(tensor, ("pve", pit, pqg))
                i = None
                for kc128 in range(4 * (qg + 1)):
                    i = nc.tensor.matmul(pso[:], vt[it % 2][:, 128 * kc128:128 * (kc128 + 1)],
                                         ptg[:, 512 * kc128:512 * (kc128 + 1)],
                                         start=(kc128 == 0), stop=(kc128 == 4 * (qg + 1) - 1))
                v = ctx.inc(i, trn, 1)
                M(("pv", it, qg), trn, v)

            @block.scalar
            def _(scalar, it=it, qg=qg):
                ctx.wait(scalar, ("pv", it, qg))
                if not (it == 0 and qg < 2):
                    pit, pqg = (it, qg - 2) if qg >= 2 else (it - 1, qg + 2)
                    ctx.wait(scalar, ("pvst", pit, pqg))
                i = scalar.activation(out=ob[qg % 2][:], in_=pso[:], func=AF.Copy)
                v = ctx.inc(i, a, 1)
                M(("pve", it, qg), a, v)

            @block.gpsimd
            def _(gpsimd, it=it, qg=qg, b=b, qh=qh):
                if not (it == 0 and qg == 0):
                    pit, pqg = (it, qg - 1) if qg >= 1 else (it - 1, 3)
                    ctx.ckpt(gpsimd, st, stc, ("pvst", pit, pqg))
                ctx.wait(gpsimd, ("pve", it, qg))
                ctx.inc(gpsimd.dma_start(out=out_fm[128 * qh:128 * (qh + 1),
                                                    2048 * b + 512 * qg:2048 * b + 512 * (qg + 1)],
                                         in_=ob[qg % 2][:]), st, 16)

        for qg in range(4):
            L = 512 * (qg + 1)
            for qt in range(4):
                qt_g = 4 * qg + qt

                Lq = 128 * (qt_g + 1)
                nbk = (Lq + 511) // 512
                off = (512 * qt) if qg == 0 else (1024 * (qt % 2) if qg == 1 else 0)
                split = 1024 if Lq > 1024 else None
                mybanks = list(range(off // 512, off // 512 + nbk))
                bank_wait = {bk: bank_owner[bk] for bk in mybanks if bk in bank_owner}
                for bk in mybanks:
                    half = split is not None and bk < (off + split) // 512
                    bank_owner[bk] = (("expa" if half else "exp"), it, qg, qt)

                @block.tensor
                def _(tensor, it=it, qg=qg, qt=qt, qt_g=qt_g, Lq=Lq, off=off, bank_wait=bank_wait):
                    if qg == 0 and qt == 0:
                        ctx.wait(tensor, ("nbr", it))
                    done = set()

                    def need(col):
                        mk = bank_wait.get((off + col) // 512)
                        if mk is not None and mk not in done:
                            ctx.wait(tensor, mk)
                            done.add(mk)

                    i = None
                    nfull = (Lq - 128) // 512
                    for kc in range(nfull):
                        need(512 * kc)
                        need(512 * (kc + 1) - 1)
                        i = nc.tensor.matmul(pss[:, off + 512 * kc:off + 512 * (kc + 1)],
                                             qs[it % 2][:, 128 * qt_g:128 * (qt_g + 1)],
                                             ks[it % 2][:, 512 * kc:512 * (kc + 1)],
                                             start=True, stop=True)
                    rem = Lq - 512 * nfull
                    need(512 * nfull)
                    need(Lq - 1)
                    i = nc.tensor.matmul(pss[:, off + 512 * nfull:off + 512 * nfull + rem],
                                         qs[it % 2][:, 128 * qt_g:128 * (qt_g + 1)],
                                         ks[it % 2][:, 512 * nfull:512 * nfull + rem],
                                         start=True, stop=False)
                    i = nc.tensor.matmul(pss[:, off + 128 * qt_g:off + 128 * (qt_g + 1)],
                                         idb[:], tri[:], start=False, stop=True)
                    v = ctx.inc(i, trn, 1)
                    M(("sc", it, qg, qt), trn, v)

                @block.scalar
                def _(scalar, it=it, qg=qg, qt=qt, Lq=Lq, off=off, split=split):
                    ctx.wait(scalar, ("sc", it, qg, qt))
                    if qg >= 2:
                        ctx.wait(scalar, ("ptr", it, qg - 2, 4 * (qg - 1) - 1))
                    elif it >= 1:
                        ctx.wait(scalar, ("ptr", it - 1, qg + 2, 4 * (qg + 3) - 1))
                    s4i = 4 * (qg % 2) + qt
                    if split is not None:
                        i = scalar.activation(out=P[qg % 2][qt][:, 0:split], in_=pss[:, off:off + split],
                                              func=AF.Exp, bias=nb[it % 2][:], accum_out=ls[s4i][:])
                        v = ctx.inc(i, a, 1)
                        M(("expa", it, qg, qt), a, v)
                        scalar.activation(out=P[qg % 2][qt][:, split:Lq], in_=pss[:, off + split:off + Lq],
                                          func=AF.Exp, bias=nb[it % 2][:], accum_out=lsb[s4i][:])
                    else:
                        scalar.activation(out=P[qg % 2][qt][:, 0:Lq], in_=pss[:, off:off + Lq], func=AF.Exp,
                                          bias=nb[it % 2][:], accum_out=ls[s4i][:])
                    i = scalar.activation(out=dmy[1][:], in_=dmy[0][:], func=AF.Copy)
                    v = ctx.inc(i, a, 1)
                    M(("exp", it, qg, qt), a, v)

                @block.vector
                def _(vector, it=it, qg=qg, qt=qt, L=L, Lq=Lq, split=split):
                    ctx.wait(vector, ("exp", it, qg, qt))
                    s4i = 4 * (qg % 2) + qt
                    if split is not None:
                        vector.tensor_tensor(out=lsb[s4i][:], in0=ls[s4i][:], in1=lsb[s4i][:], op=ALU.add)
                        sepv(vector)
                        vector.reciprocal(out=li[s4i][:], in_=lsb[s4i][:])
                    else:
                        vector.reciprocal(out=li[s4i][:], in_=ls[s4i][:])
                    if Lq < L:
                        vector.memset(P[qg % 2][qt][:, Lq:L], 0.0)
                    else:
                        sepv(vector)
                    i = vector.tensor_scalar(out=P[qg % 2][qt][:, 0:Lq], in0=P[qg % 2][qt][:, 0:Lq],
                                             scalar1=li[s4i][:], scalar2=None, op0=ALU.mult)
                    v = ctx.inc(i, c, 1)
                    M(("psc", it, qg, qt), c, v)

            if qg >= 1:
                emit_ptrpv(qg - 1)
            if qg == 1 and it + 1 < NIT:
                emit_norms(it + 1)
        emit_ptrpv(3)


def stage6_sems(tag="s6"):
    return ["ld", "wld", "mm", "ev", "c", "a", "trn", "g", "evx", "evy", "st", "ldc", "wldc", "stc", "s10_cc"]




def emit_stage7(ctx, ap, stack, deps, tag="s7"):
    ctx.scope = tag
    nc, block = ctx.nc, ctx.block
    sb = lambda n, s, d: alloc_sbuf(nc, stack, f"{tag}_{n}", s, d)
    psa = lambda n, s, d: alloc_psum(nc, stack, f"{tag}_{n}", s, d)

    woa = sb("woa", [128, 256], BF16)
    A_sb = sb("A", [128, 256], BF16)
    wo = [sb(f"wo{j}", [128, 2048], BF16) for j in range(2)]
    at = [sb(f"at{j}", [128, 16 * 512], BF16) for j in range(2)]
    h2 = [sb(f"h2{j}", [128, 16 * 512], BF16) for j in range(2)]
    tnh = sb("tnh", [16, 512], F32)
    hsc = sb("hsc", [16, 512], F32)
    mmo = [sb(f"mmo{j}", [16, 512], BF16) for j in range(2)]
    pw = [psa(f"pw{j}", [128, 512], F32) for j in range(2)]
    pm = psa("pm", [128, 512], F32)
    ph = psa("ph", [128, 512], F32)

    ld, wld, c, a, trn, st = ("ld", "wld", "c", "a", "trn", "st")
    ldc, wldc, stc = ("ldc", "wldc", "stc")
    M = ctx.mark

    @block.gpsimd
    def _(gpsimd):
        for s_, v_ in deps:
            gpsimd.wait_ge(ctx.h[s_], v_)
        ar = ap["A_in"].rearrange("(c p) r -> p c r", p=128)
        ctx.inc(gpsimd.dma_start(out=A_sb[:].rearrange("p (c r) -> p c r", r=16), in_=ar), wld, 16)
        wr = ap["Wo_in"].rearrange("(c p) e -> p c e", p=128)
        ctx.inc(gpsimd.dma_start(out=wo[0][:].rearrange("p (c e) -> p c e", e=128),
                                 in_=wr[:, :, 0:128]), wld, 16)
        ctx.ckpt(gpsimd, wld, wldc, ("wload", 0))

    for eng in ("tensor", "vector", "scalar", "sync"):
        @getattr(block, eng)
        def _(e):
            for s_, v_ in deps:
                e.wait_ge(ctx.h[s_], v_)

    for e2 in range(16):
        if e2 + 1 < 16:
            @block.gpsimd
            def _(gpsimd, e2=e2):
                if e2 >= 1:
                    ctx.wait(gpsimd, ("woamm", e2 - 1))
                wr = ap["Wo_in"].rearrange("(c p) e -> p c e", p=128)
                ctx.inc(gpsimd.dma_start(out=wo[(e2 + 1) % 2][:].rearrange("p (c e) -> p c e", e=128),
                                         in_=wr[:, :, 128 * (e2 + 1):128 * (e2 + 2)]), wld, 16)
                ctx.ckpt(gpsimd, wld, wldc, ("wload", e2 + 1))

        @block.tensor
        def _(tensor, e2=e2):
            ctx.wait(tensor, ("wload", e2))
            if e2 >= 2:
                ctx.wait(tensor, ("woaev", e2 - 2))
            Av = A_sb[:].rearrange("p (c r) -> p c r", r=16)
            i = None
            for cc in range(16):
                i = nc.tensor.matmul(pw[e2 % 2][:, 0:16], wo[e2 % 2][:, 128 * cc:128 * (cc + 1)],
                                     Av[:, cc, :], start=(cc == 0), stop=(cc == 15))
            v = ctx.inc(i, trn, 1)
            M(("woamm", e2), trn, v)

        @block.vector
        def _(vector, e2=e2):
            ctx.wait(vector, ("woamm", e2))
            i = vector.tensor_copy(woa[:, 16 * e2:16 * (e2 + 1)], pw[e2 % 2][:, 0:16])
            v = ctx.inc(i, c, 1)
            M(("woaev", e2), c, v)

    @block.sync
    def _(sync):
        arr = ap["attnPV_fm"].rearrange("(c p) t -> p c t", p=128)
        h2r = ap["h2_fm"].rearrange("(c p) t -> p c t", p=128)
        ctx.inc(sync.dma_start(out=at[0][:].rearrange("p (c t) -> p c t", t=512),
                               in_=arr[:, :, 0:512]), ld, 16)
        ctx.inc(sync.dma_start(out=h2[0][:].rearrange("p (c t) -> p c t", t=512),
                               in_=h2r[:, :, 0:512]), ld, 16)
        ctx.ckpt(sync, ld, ldc, ("tld", 0))

    for chk in range(8):
        @block.sync
        def _(sync, chk=chk):
            if chk + 1 < 8:
                if chk >= 1:
                    ctx.wait(sync, ("modmm", chk - 1))
                    ctx.wait(sync, ("hamm", chk - 1))
                arr = ap["attnPV_fm"].rearrange("(c p) t -> p c t", p=128)
                h2r = ap["h2_fm"].rearrange("(c p) t -> p c t", p=128)
                ctx.inc(sync.dma_start(out=at[(chk + 1) % 2][:].rearrange("p (c t) -> p c t", t=512),
                                       in_=arr[:, :, 512 * (chk + 1):512 * (chk + 2)]), ld, 16)
                ctx.inc(sync.dma_start(out=h2[(chk + 1) % 2][:].rearrange("p (c t) -> p c t", t=512),
                                       in_=h2r[:, :, 512 * (chk + 1):512 * (chk + 2)]), ld, 16)
                ctx.ckpt(sync, ld, ldc, ("tld", chk + 1))

        @block.tensor
        def _(tensor, chk=chk):
            if chk == 0:
                ctx.wait(tensor, ("woaev", 15))
            ctx.wait(tensor, ("tld", chk))
            if chk >= 1:
                ctx.wait(tensor, ("tev", chk - 1))
            i = None
            for cc in range(16):
                i = nc.tensor.matmul(pm[0:16, 0:512],
                                     woa[:, 16 * cc:16 * (cc + 1)],
                                     at[chk % 2][:, 512 * cc:512 * (cc + 1)],
                                     start=(cc == 0), stop=(cc == 15))
            v = ctx.inc(i, trn, 1)
            M(("modmm", chk), trn, v)
            Av = A_sb[:].rearrange("p (c r) -> p c r", r=16)
            i = None
            for cc in range(16):
                i = nc.tensor.matmul(ph[0:16, 0:512], Av[:, cc, :],
                                     h2[chk % 2][:, 512 * cc:512 * (cc + 1)],
                                     start=(cc == 0), stop=(cc == 15))
            v = ctx.inc(i, trn, 1)
            M(("hamm", chk), trn, v)

        @block.scalar
        def _(scalar, chk=chk):
            ctx.wait(scalar, ("modmm", chk))
            if chk >= 1:
                ctx.wait(scalar, ("mmul", chk - 1))
            scalar.activation(out=tnh[:], in_=pm[0:16, 0:512], func=AF.Tanh)
            ctx.wait(scalar, ("hamm", chk))
            i = scalar.activation(out=hsc[:], in_=ph[0:16, 0:512], func=AF.Copy, scale=0.1)
            v = ctx.inc(i, a, 1)
            M(("tev", chk), a, v)

        @block.vector
        def _(vector, chk=chk):
            ctx.wait(vector, ("tev", chk))
            if chk >= 2:
                ctx.wait(vector, ("mst", chk - 2))
            i = vector.tensor_tensor(out=mmo[chk % 2][:], in0=tnh[:], in1=hsc[:], op=ALU.mult)
            v = ctx.inc(i, c, 1)
            M(("mmul", chk), c, v)

        @block.gpsimd
        def _(gpsimd, chk=chk):
            if chk >= 1:
                ctx.ckpt(gpsimd, st, stc, ("mst", chk - 1))
            ctx.wait(gpsimd, ("mmul", chk))
            ctx.inc(gpsimd.dma_start(out=ap["mm_fm"][:, 512 * chk:512 * (chk + 1)], in_=mmo[chk % 2][:]), st, 16)


def stage7_sems(tag="s7"):
    return ["ld", "wld", "mm", "ev", "c", "a", "trn", "g", "evx", "evy", "st", "ldc", "wldc", "stc", "s10_cc"]




def emit_stage8a(ctx, ap, stack, deps, tag="s8a"):
    ctx.scope = tag
    nc, block = ctx.nc, ctx.block
    sb = lambda n, s, d: alloc_sbuf(nc, stack, f"{tag}_{n}", s, d)
    psa = lambda n, s, d: alloc_psum(nc, stack, f"{tag}_{n}", s, d)

    TG = 1024
    NTG = T // TG

    h2 = [sb(f"h2{j}", [128, 16 * TG], BF16) for j in range(2)]
    mm = [sb(f"mm{j}", [16, TG], BF16) for j in range(2)]
    B_sb = sb("B", [16, 8192], BF16)
    wi = [sb(f"wi{j}", [128, 2048], BF16) for j in range(2)]
    wu = [sb(f"wu{j}", [128, 2048], BF16) for j in range(2)]
    gsb = [sb(f"gsb{j}", [128, 512], F32) for j in range(2)]
    gu = [sb(f"gu{j}", [128, TG], BF16) for j in range(2)]
    pg = [psa(f"pg{j}", [128, 512], F32) for j in range(4)]
    pu = [psa(f"pu{j}", [128, 512], F32) for j in range(4)]

    ld, wld, c, a, trn, st = ("ld", "wld", "c", "a", "trn", "st")
    ldc, wldc, stc = ("ldc", "wldc", "stc")
    M = ctx.mark

    @block.gpsimd
    def _(gpsimd):
        for s_, v_ in deps:
            gpsimd.wait_ge(ctx.h[s_], v_)
        ctx.inc(gpsimd.dma_start(out=B_sb[:], in_=ap["B_in"]), wld, 16)
        for w_sb, wt in ((wi[0], "W_inT"), (wu[0], "W_upT")):
            wr = ap[wt].rearrange("(c p) f -> p c f", p=128)
            ctx.inc(gpsimd.dma_start(out=w_sb[:].rearrange("p (c f) -> p c f", f=128),
                                     in_=wr[:, :, 0:128]), wld, 16)
        ctx.ckpt(gpsimd, wld, wldc, ("wload", 0, 0))

    for eng in ("tensor", "vector", "scalar", "sync"):
        @getattr(block, eng)
        def _(e):
            for s_, v_ in deps:
                e.wait_ge(ctx.h[s_], v_)

    @block.sync
    def _(sync):
        h2r = ap["h2_fm"].rearrange("(c p) t -> p c t", p=128)
        ctx.inc(sync.dma_start(out=h2[0][:].rearrange("p (c t) -> p c t", t=TG),
                               in_=h2r[:, :, 0:TG]), ld, 16)
        ctx.inc(sync.dma_start(out=mm[0][:], in_=ap["mm_fm"][:, 0:TG]), ld, 16)
        ctx.ckpt(sync, ld, ldc, ("xld", 0))

    for tg in range(NTG):
        if tg + 1 < NTG:
            @block.sync
            def _(sync, tg=tg):
                if tg >= 1:
                    ctx.wait(sync, ("gmm", tg - 1, 63))
                h2r = ap["h2_fm"].rearrange("(c p) t -> p c t", p=128)
                ctx.inc(sync.dma_start(out=h2[(tg + 1) % 2][:].rearrange("p (c t) -> p c t", t=TG),
                                       in_=h2r[:, :, TG * (tg + 1):TG * (tg + 2)]), ld, 16)
                ctx.inc(sync.dma_start(out=mm[(tg + 1) % 2][:], in_=ap["mm_fm"][:, TG * (tg + 1):TG * (tg + 2)]), ld, 16)
                ctx.ckpt(sync, ld, ldc, ("xld", tg + 1))

        for fc in range(64):
            nfc = fc + 1 if fc + 1 < 64 else (0 if tg + 1 < NTG else None)
            ntg = tg if fc + 1 < 64 else tg + 1
            if nfc is not None:
                @block.gpsimd
                def _(gpsimd, tg=tg, fc=fc, nfc=nfc, ntg=ntg):
                    pr_tg, pr_fc = (tg, fc - 1) if fc >= 1 else (tg - 1, 63)
                    if pr_fc >= 0 and pr_tg >= 0:
                        ctx.wait(gpsimd, ("gmm", pr_tg, pr_fc))
                    if ntg == 0:
                        for w_sb, wt in ((wi[(nfc) % 2], "W_inT"), (wu[(nfc) % 2], "W_upT")):
                            wr = ap[wt].rearrange("(c p) f -> p c f", p=128)
                            ctx.inc(gpsimd.dma_start(out=w_sb[:].rearrange("p (c f) -> p c f", f=128),
                                                     in_=wr[:, :, 128 * nfc:128 * (nfc + 1)]), wld, 16)
                    else:
                        for w_sb, wt in ((wi[(nfc) % 2], "win_b"), (wu[(nfc) % 2], "wup_b")):
                            ctx.inc(gpsimd.dma_start(out=w_sb[:], in_=ap[wt][nfc, :, :]), wld, 16)
                    ctx.ckpt(gpsimd, wld, wldc, ("wload", ntg, nfc))

            @block.tensor
            def _(tensor, tg=tg, fc=fc):
                ctx.wait(tensor, ("wload", tg, fc))
                if fc == 0:
                    ctx.wait(tensor, ("xld", tg))
                i = None
                for h in range(2):
                    pr_tg, pr_fc = (tg, fc - 2) if fc >= 2 else ((tg - 1, 62 + fc) if tg >= 1 else (None, None))
                    if pr_tg is not None:
                        ctx.wait(tensor, ("guev", pr_tg, pr_fc, h))
                    bank = 2 * (fc % 2) + h
                    for cc in range(16):
                        nc.tensor.matmul(pg[bank][:], wi[fc % 2][:, 128 * cc:128 * (cc + 1)],
                                         h2[tg % 2][:, TG * cc + 512 * h:TG * cc + 512 * (h + 1)],
                                         start=(cc == 0), stop=False)
                    nc.tensor.matmul(pg[bank][:], B_sb[:, 128 * fc:128 * (fc + 1)],
                                     mm[tg % 2][:, 512 * h:512 * (h + 1)], start=False, stop=True)
                    for cc in range(16):
                        i = nc.tensor.matmul(pu[bank][:], wu[fc % 2][:, 128 * cc:128 * (cc + 1)],
                                             h2[tg % 2][:, TG * cc + 512 * h:TG * cc + 512 * (h + 1)],
                                             start=(cc == 0), stop=(cc == 15))
                    v = ctx.inc(i, trn, 1)
                    M(("gmm_h", tg, fc, h), trn, v)
                M(("gmm", tg, fc), trn, ctx.n[trn])

            @block.scalar
            def _(scalar, tg=tg, fc=fc):
                for h in range(2):
                    ctx.wait(scalar, ("gmm_h", tg, fc, h))
                    if not (tg == 0 and fc == 0):
                        pr = (tg, fc - 1) if fc >= 1 else (tg - 1, 63)
                        ctx.wait(scalar, ("gumul", pr[0], pr[1], h))
                    i = scalar.activation(out=gsb[h][:], in_=pg[2 * (fc % 2) + h][:], func=AF.Silu)
                    v = ctx.inc(i, a, 1)
                    M(("silu", tg, fc, h), a, v)

            @block.vector
            def _(vector, tg=tg, fc=fc):
                for h in range(2):
                    ctx.wait(vector, ("silu", tg, fc, h))
                    if fc >= 2 or tg >= 1:
                        pr_tg, pr_fc = (tg, fc - 2) if fc >= 2 else (tg - 1, 62 + fc)
                        ctx.wait(vector, ("gust", pr_tg, pr_fc))
                    i = vector.tensor_tensor(out=gu[fc % 2][:, 512 * h:512 * (h + 1)],
                                             in0=pu[2 * (fc % 2) + h][:], in1=gsb[h][:], op=ALU.mult)
                    v = ctx.inc(i, c, 1)
                    M(("gumul", tg, fc, h), c, v)
                    M(("guev", tg, fc, h), c, v)

            @block.gpsimd
            def _(gpsimd, tg=tg, fc=fc):
                if fc >= 1 or tg >= 1:
                    pr = (tg, fc - 1) if fc >= 1 else (tg - 1, 63)
                    ctx.ckpt(gpsimd, st, stc, ("gust", pr[0], pr[1]))
                ctx.wait(gpsimd, ("gumul", tg, fc, 1))
                ctx.inc(gpsimd.dma_start(out=ap["gu_fm"][128 * fc:128 * (fc + 1), TG * tg:TG * (tg + 1)],
                                         in_=gu[fc % 2][:]), st, 16)

            if tg == 0:
                @block.gpsimd
                def _(gpsimd, fc=fc):
                    ctx.inc(gpsimd.dma_start(out=ap["win_b"][fc, :, :], in_=wi[fc % 2][:]), st, 16)
                    ctx.inc(gpsimd.dma_start(out=ap["wup_b"][fc, :, :], in_=wu[fc % 2][:]), st, 16)


def emit_stage8c(ctx, ap, stack, deps, tag="s8c"):
    ctx.scope = tag
    nc, block = ctx.nc, ctx.block
    sb = lambda n, s, d: alloc_sbuf(nc, stack, f"{tag}_{n}", s, d)
    psa = lambda n, s, d: alloc_psum(nc, stack, f"{tag}_{n}", s, d)

    wdn = [sb(f"wdn{j}", [128, 16 * 2048], BF16) for j in range(2)]
    gus = [sb(f"gus{j}", [128, 16 * 512], BF16) for j in range(2)]
    xev = [sb(f"xev{j}", [128, 2048], F32) for j in range(2)]
    px = [psa(f"px{j}", [128, 512], F32) for j in range(8)]

    ld, wld, c, trn, st = ("ld", "wld", "c", "trn", "st")
    ldc, wldc, stc = ("ldc", "wldc", "stc")
    M = ctx.mark

    for eng in ("tensor", "vector", "scalar", "sync", "gpsimd"):
        @getattr(block, eng)
        def _(e):
            for s_, v_ in deps:
                e.wait_ge(ctx.h[s_], v_)

    @block.gpsimd
    def _(gpsimd):
        wr = ap["W_downT"].rearrange("(b c p) d -> b p c d", p=128, c=16)
        ctx.inc(gpsimd.dma_start(out=wdn[0][:].rearrange("p (c d) -> p c d", d=2048),
                                 in_=wr[0, :, :, :]), wld, 16)
        ctx.ckpt(gpsimd, wld, wldc, ("wdload", 0))

    for fcb in range(4):

        for g4 in range(8):
            @block.sync
            def _(sync, fcb=fcb, g4=g4):
                if g4 == 0 and fcb == 0:
                    gur = ap["gu_fm"].rearrange("(b c p) t -> b p c t", p=128, c=16)
                    ctx.inc(sync.dma_start(out=gus[0][:].rearrange("p (c t) -> p c t", t=512),
                                           in_=gur[0, :, :, 0:512]), ld, 16)
                    ctx.ckpt(sync, ld, ldc, ("guld", 0, 0))
                nfcb, ng4 = (fcb, g4 + 1) if g4 + 1 < 8 else (fcb + 1, 0)
                if nfcb < 4:
                    if g4 >= 1 or fcb >= 1:
                        pr2 = (fcb, g4 - 1) if g4 >= 1 else (fcb - 1, 7)
                        ctx.wait(sync, ("xmm", pr2[0], pr2[1], 3))
                    gur = ap["gu_fm"].rearrange("(b c p) t -> b p c t", p=128, c=16)
                    ctx.inc(sync.dma_start(out=gus[ng4 % 2][:].rearrange("p (c t) -> p c t", t=512),
                                           in_=gur[nfcb, :, :, 512 * ng4:512 * (ng4 + 1)]), ld, 16)
                    ctx.ckpt(sync, ld, ldc, ("guld", nfcb, ng4))

            for ttl in range(4):
                tt = 4 * g4 + ttl

                @block.tensor
                def _(tensor, fcb=fcb, g4=g4, ttl=ttl, tt=tt):
                    if ttl == 0:
                        ctx.wait(tensor, ("guld", fcb, g4))
                        if g4 == 0:
                            ctx.wait(tensor, ("wdload", fcb))
                        if g4 == 1 and fcb == 0:
                            pass
                    i = None
                    for nb in range(4):
                        key = ("xev", fcb, tt - 2, nb) if tt >= 2 else \
                              (("xev", fcb - 1, 30 + tt, nb) if fcb >= 1 else None)
                        if key is not None:
                            ctx.wait(tensor, key)
                        for cc in range(16):
                            i = nc.tensor.matmul(px[4 * (tt % 2) + nb][:],
                                                 gus[g4 % 2][:, 512 * cc + 128 * ttl:512 * cc + 128 * (ttl + 1)],
                                                 wdn[fcb % 2][:, 2048 * cc + 512 * nb:2048 * cc + 512 * (nb + 1)],
                                                 start=(cc == 0), stop=(cc == 15))
                    v = ctx.inc(i, trn, 1)
                    M(("xmm", fcb, g4, ttl), trn, v)

                @block.vector
                def _(vector, fcb=fcb, g4=g4, tt=tt, ttl=ttl):
                    ctx.wait(vector, ("xmm", fcb, g4, ttl))
                    if tt >= 2 or fcb >= 1:
                        pr = (fcb, tt - 2) if tt >= 2 else (fcb - 1, 30 + tt)
                        ctx.wait(vector, ("xst", pr[0], pr[1]))
                    i = None
                    for nb in range(4):
                        i = vector.tensor_copy(xev[tt % 2][:, 512 * nb:512 * (nb + 1)], px[4 * (tt % 2) + nb][:])
                        v = ctx.inc(i, c, 1)
                        M(("xev", fcb, tt, nb), c, v)

                @block.gpsimd
                def _(gpsimd, fcb=fcb, tt=tt):
                    if tt >= 1 or fcb >= 1:
                        pr = (fcb, tt - 1) if tt >= 1 else (fcb - 1, 31)
                        ctx.ckpt(gpsimd, st, stc, ("xst", pr[0], pr[1]))
                    ctx.wait(gpsimd, ("xev", fcb, tt, 3))
                    ctx.inc(gpsimd.dma_start(out=ap["x1_tm"][128 * tt:128 * (tt + 1), :],
                                             in_=xev[tt % 2][:], accum_op=ALU.add), st, 16)
                if fcb == 3 and tt == 31:
                    @block.gpsimd
                    def _(gpsimd, fcb=fcb, tt=tt):
                        ctx.ckpt(gpsimd, st, stc, ("xst", fcb, tt))

            if g4 == 0 and fcb + 1 < 4:
                @block.gpsimd
                def _(gpsimd, fcb=fcb):
                    if fcb >= 1:
                        ctx.wait(gpsimd, ("xmm", fcb - 1, 7, 3))
                    wr = ap["W_downT"].rearrange("(b c p) d -> b p c d", p=128, c=16)
                    ctx.inc(gpsimd.dma_start(out=wdn[(fcb + 1) % 2][:].rearrange("p (c d) -> p c d", d=2048),
                                             in_=wr[fcb + 1, :, :, :]), wld, 16)
            if g4 == 6 and fcb + 1 < 4:
                @block.gpsimd
                def _(gpsimd, fcb=fcb):
                    ctx.ckpt(gpsimd, wld, wldc, ("wdload", fcb + 1))


def stage8a_sems(tag="s8a"):
    return ["ld", "wld", "mm", "ev", "c", "a", "trn", "g", "evx", "evy", "st", "ldc", "wldc", "stc", "s10_cc"]


def stage8c_sems(tag="s8c"):
    return ["ld", "wld", "mm", "ev", "c", "a", "trn", "g", "evx", "evy", "st", "ldc", "wldc", "stc", "s10_cc"]



NB = 7


def emit_stage9(ctx, ap, stack, deps=(), tag="s9"):
    ctx.scope = tag
    nc, block = ctx.nc, ctx.block
    sb = lambda n, s, d: alloc_sbuf(nc, stack, f"{tag}_{n}", s, d)
    ps = lambda n, s, d: alloc_psum(nc, stack, f"{tag}_{n}", s, d)

    x1_tm, xln_tm = ap["x1_tm"], ap["x_ln_tm"]
    out_fm = {"xn": ap["xn1_fm"], "y": ap["x_n_fm"]}

    xt = [sb(f"xt{j}", [128, D], F32) for j in range(2)]
    xn = [sb(f"xn{j}", [128, D], F32) for j in range(3)]
    z = [sb(f"z{j}", [128, D], F32) for j in range(2)]
    zb = [sb(f"zb{j}", [128, D], F32) for j in range(2)]
    zlb = [sb(f"zlb{j}", [128, D], BF16) for j in range(2)]
    xnb = [sb(f"xnb{j}", [128, D], BF16) for j in range(NB)]
    yb = [sb(f"yb{j}", [128, D], BF16) for j in range(NB)]
    sq = sb("sq", [128, D], F32)
    gB = sb("gB", [128, D], BF16)
    bB = sb("bB", [128, D], BF16)
    idb = sb("idb", [128, 128], BF16)
    scB = sb("scB", [128, 1], F32)
    va = [sb(f"va{j}", [128, 1], F32) for j in range(2)]
    vm = [sb(f"vm{j}", [128, 1], F32) for j in range(2)]
    vs = [sb(f"vs{j}", [128, 1], F32) for j in range(2)]
    vb = [sb(f"vb{j}", [128, 1], F32) for j in range(2)]
    vm2 = [sb(f"vm2{j}", [128, 1], F32) for j in range(2)]
    vs2 = [sb(f"vs2{j}", [128, 1], F32) for j in range(2)]
    vb2 = [sb(f"vb2{j}", [128, 1], F32) for j in range(2)]
    vs3 = [sb(f"vs3{j}", [128, 1], F32) for j in range(2)]
    vb3 = [sb(f"vb3{j}", [128, 1], F32) for j in range(2)]
    vc = sb("vc", [128, 1], F32)
    vc2 = sb("vc2", [128, 1], F32)
    vc3 = sb("vc3", [128, 1], F32)
    sn = sb("sn", [128, 1], F32)
    veps5 = sb("veps5", [128, 1], F32)
    dmy = [sb(f"dmy{j}", [128, 32], F32) for j in range(2)]
    ex = {s: sb(f"ex_{s}", [128, NC * 512], BF16) for s in ("xn", "y")}
    psx = [ps(f"psx{j}", [128, 512], BF16) for j in range(2)]
    psy = [ps(f"psy{j}", [128, 512], BF16) for j in range(2)]

    sepv = lambda eng: eng.tensor_copy(dmy[1][:], dmy[0][:])
    sepa = lambda eng: eng.activation(out=dmy[1][:], in_=dmy[0][:], func=AF.Copy)

    ld, c, a, trn, evx, evy, st = ("ld", "c", "a", "trn", "evx", "evy", "st")
    ldc, stc = "ldc", "stc"
    M = ctx.mark
    NG = NT // 4

    for eng in ("tensor", "vector", "scalar", "gpsimd"):
        @getattr(block, eng)
        def _(e):
            for s_, v_ in deps:
                e.wait_ge(ctx.h[s_], v_)

    @block.sync
    def _(sync):
        for s_, v_ in deps:
            sync.wait_ge(ctx.h[s_], v_)
        for nm, tile in (("identb", idb), ("gammaB", gB), ("betaB", bB), ("scaleB", scB)):
            ctx.inc(sync.dma_start(out=tile[:], in_=ap[nm]), ld, 16)
        ctx.inc(sync.dma_start(out=xt[0][:], in_=x1_tm[0:128, :]), ld, 16)
        ctx.ckpt(sync, ld, ldc, ("ld_tile", 0))
        M(("bcev", 2), ldc, ctx.n[ldc])

    @block.vector
    def _(vector):
        vector.memset(veps5[:], 1e-5)
        vector.memset(dmy[0][:], 0.0)

    for t in range(NT + 8):
        g = (t - 7) // 4 if (t - 7) % 4 == 0 and 0 <= (t - 7) // 4 < NG else None

        if g is not None:
            if g > 0:
                @block.gpsimd
                def _(gpsimd, g=g):
                    ctx.ckpt(gpsimd, st, stc, ("st_f", g - 1))
            for ch in range(NC):
                @block.tensor
                def _(tensor, g=g, ch=ch):
                    if ch == 0:
                        ctx.wait(tensor, ("pD", 4 * g + 3))
                        ctx.wait(tensor, ("pG", 4 * g + 3))
                        ctx.wait(tensor, ("xnbC", 4 * g + 3))
                    for strm, src, bank in (("xn", xnb, psx), ("y", yb, psy)):
                        key = ("ev_" + strm, g, ch - 2) if ch >= 2 else (("ev_" + strm, g - 1, NC - 2 + ch) if g > 0 else None)
                        if key is not None:
                            ctx.wait(tensor, key)
                        p = bank[ch % 2]
                        i = None
                        for j in range(4):
                            i = nc.tensor.transpose(p[:, 128 * j:128 * (j + 1)],
                                                    src[(4 * g + j) % NB][:, 128 * ch:128 * (ch + 1)],
                                                    idb[:])
                        v = ctx.inc(i, trn, 1)
                        M(("trn", g, ch, strm), trn, v)

                @block.vector
                def _(vector, g=g, ch=ch):
                    ctx.wait(vector, ("trn", g, ch, "xn"))
                    if g > 0:
                        ctx.wait(vector, ("st_f", g - 1))
                    i = vector.tensor_copy(ex["xn"][:, 512 * ch:512 * (ch + 1)], psx[ch % 2][:])
                    v = ctx.inc(i, evx, 1)
                    M(("ev_xn", g, ch), evx, v)

                @block.scalar
                def _(scalar, g=g, ch=ch):
                    ctx.wait(scalar, ("trn", g, ch, "y"))
                    if g > 0:
                        ctx.wait(scalar, ("st_f", g - 1))
                    i = scalar.activation(out=ex["y"][:, 512 * ch:512 * (ch + 1)], in_=psy[ch % 2][:],
                                          func=AF.Copy)
                    v = ctx.inc(i, evy, 1)
                    M(("ev_y", g, ch), evy, v)

                @block.gpsimd
                def _(gpsimd, g=g, ch=ch):
                    ctx.wait(gpsimd, ("ev_xn", g, ch))
                    ctx.inc(gpsimd.dma_start(out=out_fm["xn"][128 * ch:128 * (ch + 1), 512 * g:512 * (g + 1)],
                                             in_=ex["xn"][:, 512 * ch:512 * (ch + 1)]), st, 16)
                    ctx.wait(gpsimd, ("ev_y", g, ch))
                    ctx.inc(gpsimd.dma_start(out=out_fm["y"][128 * ch:128 * (ch + 1), 512 * g:512 * (g + 1)],
                                             in_=ex["y"][:, 512 * ch:512 * (ch + 1)]), st, 16)

        if t + 1 < NT or t == NT - 1:
            @block.sync
            def _(sync, t=t):
                tt = t + 1
                if tt < NT:
                    if tt >= 2:
                        ctx.wait(sync, ("pB", tt - 2))
                    ctx.inc(sync.dma_start(out=xt[tt % 2][:], in_=x1_tm[128 * tt:128 * (tt + 1), :]), ld, 16)
                if 1 <= t < NT:
                    ctx.ckpt(sync, ld, ldc, ("ld_tile", t))

        @block.vector
        def _(vector, t=t):
            if t < NT:
                ctx.wait(vector, ("ld_tile", t))
                if t >= 2:
                    ctx.wait(vector, ("pB", t - 2))
                if t >= 3:
                    ctx.wait(vector, ("pD", t - 3))
                    ctx.wait(vector, ("xnbC", t - 3))
                if t == 0:
                    ctx.wait(vector, ("bcev", 2))
                vector.tensor_reduce(va[t % 2][:], xt[t % 2][:], axis=AX.X, op=ALU.add)
                sepv(vector)
                i = vector.tensor_scalar(out=vm[t % 2][:], in0=va[t % 2][:], scalar1=-1.0 / D, scalar2=None, op0=ALU.mult)
                v = ctx.inc(i, c, 1)
                M(("pA", t), c, v)
            if 0 <= t - 1 < NT:
                u = t - 1
                ctx.wait(vector, ("pB", u))
                vector.reciprocal(out=vc[:], in_=vb[u % 2][:])
                sepv(vector)
                vector.tensor_scalar(out=xn[u % 3][:], in0=xn[u % 3][:], scalar1=vc[:], scalar2=None, op0=ALU.mult)
                sepv(vector)
                vector.tensor_reduce(va[u % 2][:], xn[u % 3][:], axis=AX.X, op=ALU.add)
                sepv(vector)
                i = vector.tensor_scalar(out=vm2[u % 2][:], in0=va[u % 2][:], scalar1=-1.0 / D, scalar2=None, op0=ALU.mult)
                v = ctx.inc(i, c, 1)
                M(("pC", u), c, v)
            if 0 <= t - 2 < NT:
                u = t - 2
                ctx.wait(vector, ("pD", u))
                vector.reciprocal(out=vc2[:], in_=vb2[u % 2][:])
                sepv(vector)
                vector.tensor_scalar(out=z[u % 2][:], in0=z[u % 2][:], scalar1=vc2[:], scalar2=None, op0=ALU.mult)
                if u >= 2:
                    ctx.wait(vector, ("pG", u - 2))
                vector.tensor_tensor(out=zb[u % 2][:], in0=z[u % 2][:], in1=gB[:], op=ALU.mult)
                sepv(vector)
                i = vector.tensor_tensor(out=zb[u % 2][:], in0=zb[u % 2][:], in1=bB[:], op=ALU.add)
                v = ctx.inc(i, c, 1)
                M(("pE", u), c, v)
            if 0 <= t - 3 < NT:
                u = t - 3
                ctx.wait(vector, ("pF", u))
                vector.tensor_scalar(out=vc3[:], in0=vb3[u % 2][:], scalar1=1e-12, scalar2=None, op0=ALU.max)
                sepv(vector)
                vector.reciprocal(out=sn[:], in_=vc3[:])
                sepv(vector)
                vector.tensor_scalar(out=vc3[:], in0=sn[:], scalar1=scB[:], scalar2=None, op0=ALU.mult)
                if u >= 2:
                    ctx.wait(vector, ("st_tm", u - 2))
                vector.tensor_copy(zlb[u % 2][:], zb[u % 2][:])
                if u >= NB:
                    ctx.wait(vector, ("trn", (u - NB) // 4, NC - 1, "y"))
                sepv(vector)
                i = vector.tensor_scalar(out=yb[u % NB][:], in0=zb[u % 2][:], scalar1=vc3[:], scalar2=None, op0=ALU.mult)
                v = ctx.inc(i, c, 1)
                M(("pG", u), c, v)

        @block.scalar
        def _(scalar, t=t):
            if t < NT:
                ctx.wait(scalar, ("pA", t))
                scalar.activation(out=xn[t % 3][:], in_=xt[t % 2][:], func=AF.Identity, bias=vm[t % 2][:])
                scalar.activation(out=sq[:], in_=xn[t % 3][:], func=AF.Square, accum_out=vs[t % 2][:])
                sepa(scalar)
                i = scalar.activation(out=vb[t % 2][:], in_=vs[t % 2][:], func=AF.Sqrt, scale=1.0 / D, bias=veps5[:])
                v = ctx.inc(i, a, 1)
                M(("pB", t), a, v)
            if 0 <= t - 1 < NT:
                u = t - 1
                ctx.wait(scalar, ("pC", u))
                if u >= 2:
                    ctx.wait(scalar, ("pE", u - 2))
                scalar.activation(out=z[u % 2][:], in_=xn[u % 3][:], func=AF.Identity, bias=vm2[u % 2][:])
                scalar.activation(out=sq[:], in_=z[u % 2][:], func=AF.Square, accum_out=vs2[u % 2][:])
                i = scalar.activation(out=vb2[u % 2][:], in_=vs2[u % 2][:], func=AF.Sqrt, scale=1.0 / D, bias=veps5[:])
                v = ctx.inc(i, a, 1)
                M(("pD", u), a, v)
            if 0 <= t - 2 < NT:
                u = t - 2
                ctx.wait(scalar, ("pE", u))
                scalar.activation(out=sq[:], in_=zb[u % 2][:], func=AF.Square, accum_out=vs3[u % 2][:])
                i = scalar.activation(out=vb3[u % 2][:], in_=vs3[u % 2][:], func=AF.Sqrt)
                v = ctx.inc(i, a, 1)
                M(("pF", u), a, v)

        if 0 <= t - 1 < NT:
            @block.gpsimd
            def _(gpsimd, t=t):
                u = t - 1
                ctx.wait(gpsimd, ("pC", u))
                if u >= NB:
                    ctx.wait(gpsimd, ("trn", (u - NB) // 4, NC - 1, "y"))
                i = gpsimd.tensor_copy(xnb[u % NB][:], xn[u % 3][:])
                v = ctx.inc(i, "g", 1)
                M(("xnbC", u), "g", v)

        if 0 <= t - 3 < NT:
            @block.gpsimd
            def _(gpsimd, t=t):
                u = t - 3
                if u >= 1:
                    ctx.ckpt(gpsimd, st, stc, ("st_tm", u - 1))
                ctx.wait(gpsimd, ("pG", u))
                ctx.inc(gpsimd.dma_start(out=xln_tm[128 * u:128 * (u + 1), :], in_=zlb[u % 2][:]), st, 16)


def stage9_sems(tag="s9"):
    return ["ld", "wld", "mm", "ev", "c", "a", "trn", "g", "evx", "evy", "st", "ldc", "wldc", "stc", "s10_cc"]




def emit_stage10kvT(ctx, ap, stack, deps, tag="s10kt"):
    ctx.scope = tag
    nc, block = ctx.nc, ctx.block
    sb = lambda n, s, d: alloc_sbuf(nc, stack, f"{tag}_{n}", s, d)
    psa = lambda n, s, d: alloc_psum(nc, stack, f"{tag}_{n}", s, d)

    qkv = ap["qkv1L_fm"]
    kv = [sb(f"kv{j}", [128, T], BF16) for j in range(2)]
    ev = [sb(f"ev{j}", [128, 512], BF16) for j in range(2)]
    idb = sb("idb", [128, 128], BF16)
    pst = [psa(f"pst{j}", [128, 512], BF16) for j in range(2)]

    ld, c, a, trn, st = ("ld", "c", "a", "trn", "st")
    ldc, stc = "ldc", "stc"
    M = ctx.mark

    @block.sync
    def _(sync):
        for s_, v_ in deps:
            sync.wait_ge(ctx.h[s_], v_)
        ctx.inc(sync.dma_start(out=idb[:], in_=ap["identb"]), ld, 16)

    for eng in ("tensor", "vector", "scalar", "gpsimd"):
        @getattr(block, eng)
        def _(e):
            for s_, v_ in deps:
                e.wait_ge(ctx.h[s_], v_)

    for ks in range(8):
        kvh, isv = ks // 2, ks % 2
        r0 = 128 * ks
        col0 = 512 * isv + 128 * kvh

        @block.sync
        def _(sync, ks=ks, r0=r0):
            if ks >= 1:
                ctx.wait(sync, ("kvtr", ks - 1, 7))
            ctx.inc(sync.dma_start(out=kv[ks % 2][:], in_=qkv[r0:r0 + 128, :]), ld, 16)
            ctx.ckpt(sync, ld, ldc, ("kvld", ks))

        for g4 in range(8):
            @block.tensor
            def _(tensor, ks=ks, g4=g4):
                if g4 == 0:
                    ctx.wait(tensor, ("kvld", ks))
                key = ("kve", ks, g4 - 2) if g4 >= 2 else (("kve", ks - 1, 6 + g4) if ks >= 1 else None)
                if key is not None:
                    ctx.wait(tensor, key)
                i = None
                for jj in range(4):
                    j = 4 * g4 + jj
                    i = nc.tensor.transpose(pst[g4 % 2][:, 128 * jj:128 * (jj + 1)],
                                            kv[ks % 2][:, 128 * j:128 * (j + 1)], idb[:])
                v = ctx.inc(i, trn, 1)
                M(("kvtr", ks, g4), trn, v)

            @block.vector
            def _(vector, ks=ks, g4=g4):
                ctx.wait(vector, ("kvtr", ks, g4))
                key = ("kvst", ks, g4 - 2) if g4 >= 2 else (("kvst", ks - 1, 6 + g4) if ks >= 1 else None)
                if key is not None:
                    ctx.wait(vector, key)
                i = vector.tensor_copy(ev[g4 % 2][:], pst[g4 % 2][:])
                v = ctx.inc(i, c, 1)
                M(("kve", ks, g4), c, v)

            @block.gpsimd
            def _(gpsimd, ks=ks, g4=g4, col0=col0):
                if not (ks == 0 and g4 == 0):
                    pk, pg = (ks, g4 - 1) if g4 >= 1 else (ks - 1, 7)
                    ctx.ckpt(gpsimd, st, stc, ("kvst", pk, pg))
                ctx.wait(gpsimd, ("kve", ks, g4))
                dst = ap["kv_tm_loc"].rearrange("(J P) f -> P J f", P=128)
                i = gpsimd.dma_start(out=dst[:, 4 * g4:4 * (g4 + 1), col0:col0 + 128],
                                     in_=ev[g4 % 2][:].rearrange("p (j d) -> p j d", d=128))
                ctx.inc(i, st, 16)
        if ks == 7:
            @block.gpsimd
            def _(gpsimd, ks=ks):
                ctx.ckpt(gpsimd, st, stc, ("kvst", 7, 7))

    @block.gpsimd
    def _(gpsimd):
        gpsimd.wait_ge(ctx.h[stc], ctx.n[stc])
        i = gpsimd.collective_compute(
            "AllGather", ALU.bypass,
            replica_groups=[list(range(8))],
            ins=[ap["kv_tm_loc"].opt()],
            outs=[ap["kv_all"].opt()],
        )
        ctx.inc_cc(i, "s10_cc")


def emit_stage10qre(ctx, ap, stack, deps, tag="s10qr"):
    ctx.scope = tag
    nc, block = ctx.nc, ctx.block
    sb = lambda n, s, d: alloc_sbuf(nc, stack, f"{tag}_{n}", s, d)

    qkv = ap["qkv1L_fm"]
    ib = [sb(f"ib{j}", [128, T], BF16) for j in range(2)]
    ob = [sb(f"ob{j}", [128, T], BF16) for j in range(2)]

    ld, c, a, trn, st = ("ld", "c", "a", "trn", "st")
    ldc, stc = "ldc", "stc"
    M = ctx.mark

    @block.sync
    def _(sync):
        for s_, v_ in deps:
            sync.wait_ge(ctx.h[s_], v_)
        ctx.inc(sync.dma_start(out=ib[0][:], in_=qkv[1024:1152, :]), ld, 16)
        ctx.ckpt(sync, ld, ldc, ("qld", 0))

    for eng in ("tensor", "vector", "scalar", "gpsimd"):
        @getattr(block, eng)
        def _(e):
            for s_, v_ in deps:
                e.wait_ge(ctx.h[s_], v_)

    for qh in range(16):
        if qh + 1 < 16:
            @block.sync
            def _(sync, qh=qh):
                if qh >= 1:
                    ctx.wait(sync, ("qre", qh - 1))
                r0 = 1024 + 128 * (qh + 1)
                ctx.inc(sync.dma_start(out=ib[(qh + 1) % 2][:], in_=qkv[r0:r0 + 128, :]), ld, 16)
                ctx.ckpt(sync, ld, ldc, ("qld", qh + 1))

        @block.vector
        def _(vector, qh=qh):
            ctx.wait(vector, ("qld", qh))
            if qh >= 2:
                ctx.wait(vector, ("qst", qh - 2))
            iv = ib[qh % 2][:].rearrange("p (s b) -> p s b", s=2).transpose([0, 2, 1])
            ov = ob[qh % 2][:].rearrange("p (b s) -> p b s", s=2)
            i = vector.tensor_copy(ov, iv)
            v = ctx.inc(i, c, 1)
            M(("qre", qh), c, v)

        @block.gpsimd
        def _(gpsimd, qh=qh):
            if qh >= 1:
                ctx.ckpt(gpsimd, st, stc, ("qst", qh - 1))
            ctx.wait(gpsimd, ("qre", qh))
            ctx.inc(gpsimd.dma_start(out=ap["q_bmaj"][128 * qh:128 * (qh + 1), :], in_=ob[qh % 2][:]), st, 16)
    @block.gpsimd
    def _(gpsimd):
        ctx.ckpt(gpsimd, st, stc, ("qst", 15))


def stage10b_sems(tag="s10b"):
    return ["ld", "wld", "mm", "ev", "c", "a", "trn", "g", "evx", "evy", "st", "ldc", "wldc", "stc", "s10_cc"]




def emit_stage11(ctx, ap, stack, deps, tag="s11"):
    ctx.scope = tag
    nc, block = ctx.nc, ctx.block
    sb = lambda n, s, d: alloc_sbuf(nc, stack, f"{tag}_{n}", s, d)
    psa = lambda n, s, d: alloc_psum(nc, stack, f"{tag}_{n}", s, d)

    kv = [[sb(f"kv{j}_{x}", [128, 1024], BF16) for j in range(8)] for x in range(2)]
    qg = [sb(f"qg{x}", [128, 2048], BF16) for x in range(2)]
    kp = [sb(f"kp{x}", [128, 1024], BF16) for x in range(2)]
    stb = [sb(f"stb{x}", [128, 512], F32) for x in range(2)]
    pb = [sb(f"pb{x}", [128, 512], BF16) for x in range(2)]
    ptb = [sb(f"ptb{x}", [128, 512], BF16) for x in range(2)]
    osb = [sb(f"osb{x}", [128, 512], BF16) for x in range(2)]
    orb = [sb(f"orb{x}", [128, 512], BF16) for x in range(2)]
    nm4 = [sb(f"nm4{x}", [128, 4], F32) for x in range(2)]
    l4 = [sb(f"l4{x}", [128, 4], F32) for x in range(2)]
    li4 = [sb(f"li4{x}", [128, 4], F32) for x in range(2)]
    idb = sb("idb", [128, 128], BF16)
    idf = sb("idf", [128, 128], F32)
    msk = sb("msk", [128, 64], BF16)
    dmy = [sb(f"dmy{j}", [128, 32], F32) for j in range(2)]

    pkt = [psa(f"pkt{x}", [128, 512], BF16) for x in range(2)]
    pst_ = psa("pst", [128, 512], F32)
    ps_ = psa("ps", [128, 512], F32)
    ppt = psa("ppt", [128, 512], BF16)
    po = psa("po", [128, 512], F32)

    ld, c, a, trn, st = ("ld", "c", "a", "trn", "st")
    ldc, stc = "ldc", "stc"
    M = ctx.mark
    sepv = lambda e: e.tensor_copy(dmy[1][:], dmy[0][:])
    kva = ap["kv_all"].rearrange("(s n) f -> s n f", s=16)

    @block.sync
    def _(sync):
        for s_, v_ in deps:
            sync.wait_ge(ctx.h[s_], v_)
        ctx.inc(sync.dma_start(out=idb[:], in_=ap["identb"]), ld, 16)
        ctx.inc(sync.dma_start(out=idf[:], in_=ap["identf"]), ld, 16)
        ctx.inc(sync.dma_start(out=msk[:], in_=ap["mask_l1b"]), ld, 16)
        qr = ap["q_bmaj"].rearrange("(h p) t -> p h t", p=128)
        ctx.inc(sync.dma_start(out=qg[0][:].rearrange("p (h t) -> p h t", t=128),
                               in_=qr[:, :, 0:128]), ld, 16)
        for j in range(8):
            ctx.inc(sync.dma_start(out=kv[0][j][:],
                                   in_=kva[:, 8 * j:8 * (j + 1), :].transpose([1, 0, 2])), ld, 16)
        ctx.ckpt(sync, ld, ldc, ("gld", 0))

    @block.vector
    def _(vector):
        for s_, v_ in deps:
            vector.wait_ge(ctx.h[s_], v_)
        vector.memset(dmy[0][:], 0.0)

    for eng in ("tensor", "scalar", "gpsimd"):
        @getattr(block, eng)
        def _(e):
            for s_, v_ in deps:
                e.wait_ge(ctx.h[s_], v_)

    NIT = 32 * 4
    for it in range(NIT):
        G, kvh = it // 4, it % 4
        bb = (it // 4) % 2
        p2 = it % 2

        if kvh == 0 and G + 1 < 32:
            @block.sync
            def _(sync, G=G):
                if G >= 1:
                    ctx.wait(sync, ("pv", 4 * G - 1))
                qr = ap["q_bmaj"].rearrange("(h p) t -> p h t", p=128)
                ctx.inc(sync.dma_start(out=qg[(G + 1) % 2][:].rearrange("p (h t) -> p h t", t=128),
                                       in_=qr[:, :, 128 * (G + 1):128 * (G + 2)]), ld, 16)
                for j in range(8):
                    ctx.inc(sync.dma_start(out=kv[(G + 1) % 2][j][:],
                                           in_=kva[:, 64 * (G + 1) + 8 * j:64 * (G + 1) + 8 * (j + 1), :].transpose([1, 0, 2])), ld, 16)
                ctx.ckpt(sync, ld, ldc, ("gld", G + 1))

        for g4 in range(2):
            @block.tensor
            def _(tensor, it=it, G=G, kvh=kvh, g4=g4, bb=bb):
                if g4 == 0 and kvh == 0:
                    ctx.wait(tensor, ("gld", G))
                key = ("kpe", it - 1, g4) if it >= 1 else None
                if key is not None:
                    ctx.wait(tensor, key)
                i = None
                for jj in range(4):
                    j = 4 * g4 + jj
                    i = nc.tensor.transpose(pkt[g4][:, 128 * jj:128 * (jj + 1)],
                                            kv[bb][j][:, 128 * kvh:128 * (kvh + 1)], idb[:])
                v = ctx.inc(i, trn, 1)
                M(("ktr", it, g4), trn, v)

            @block.vector
            def _(vector, it=it, g4=g4, p2=p2):
                ctx.wait(vector, ("ktr", it, g4))
                if it >= 2:
                    ctx.wait(vector, ("stmm", it - 2))
                i = vector.tensor_copy(kp[p2][:, 512 * g4:512 * (g4 + 1)], pkt[g4][:])
                v = ctx.inc(i, c, 1)
                M(("kpe", it, g4), c, v)

        @block.tensor
        def _(tensor, it=it, G=G, kvh=kvh, bb=bb, p2=p2):
            ctx.wait(tensor, ("kpe", it, 1))
            if it >= 1:
                ctx.wait(tensor, ("stev", it - 1))
            i = None
            for j in range(8):
                nc.tensor.matmul(pst_[:, 64 * j:64 * (j + 1)],
                                 kp[p2][:, 128 * j:128 * (j + 1)],
                                 qg[bb][:].rearrange("p (h t) -> p h t", t=128)[:, 4 * kvh:4 * kvh + 4, 16 * j:16 * (j + 1)],
                                 start=True, stop=False)
                i = nc.tensor.matmul(pst_[:, 64 * j:64 * (j + 1)], idb[:], msk[:],
                                     start=False, stop=True)
            v = ctx.inc(i, trn, 1)
            M(("stmm", it), trn, v)

        @block.vector
        def _(vector, it=it, p2=p2):
            ctx.wait(vector, ("stmm", it))
            if it >= 2:
                ctx.wait(vector, ("str", it - 2))
            i = vector.tensor_copy(stb[p2][:], pst_[:])
            v = ctx.inc(i, c, 1)
            M(("stev", it), c, v)

        @block.tensor
        def _(tensor, it=it, p2=p2):
            ctx.wait(tensor, ("stev", it))
            if it >= 1:
                ctx.wait(tensor, ("exp", it - 1, 3))
            i = None
            for cc in range(4):
                i = nc.tensor.transpose(ps_[:, 128 * cc:128 * (cc + 1)],
                                        stb[p2][:, 128 * cc:128 * (cc + 1)], idf[:])
            v = ctx.inc(i, trn, 1)
            M(("str", it), trn, v)

        @block.vector
        def _(vector, it=it, p2=p2):
            ctx.wait(vector, ("str", it))
            if it >= 2:
                ctx.wait(vector, ("exp", it - 2, 3))
            i = vector.tensor_reduce(nm4[p2][:], ps_[:].rearrange("p (c k) -> p c k", k=128),
                                     axis=AX.X, op=ALU.max, negate=True)
            v = ctx.inc(i, c, 1)
            M(("nm", it), c, v)

        @block.scalar
        def _(scalar, it=it, p2=p2):
            ctx.wait(scalar, ("nm", it))
            if it >= 2:
                ctx.wait(scalar, ("psc", it - 2, 3))
            for cc in range(4):
                scalar.activation(out=pb[p2][:, 128 * cc:128 * (cc + 1)],
                                  in_=ps_[:, 128 * cc:128 * (cc + 1)], func=AF.Exp,
                                  bias=nm4[p2][:, cc:cc + 1], accum_out=l4[p2][:, cc:cc + 1])
            i = scalar.activation(out=dmy[1][:], in_=dmy[0][:], func=AF.Copy)
            v = ctx.inc(i, a, 1)
            M(("exp", it, 3), a, v)

        @block.vector
        def _(vector, it=it, p2=p2):
            ctx.wait(vector, ("exp", it, 3))
            vector.reciprocal(out=li4[p2][:], in_=l4[p2][:])
            sepv(vector)
            i = None
            for cc in range(4):
                i = vector.tensor_scalar(out=pb[p2][:, 128 * cc:128 * (cc + 1)],
                                         in0=pb[p2][:, 128 * cc:128 * (cc + 1)],
                                         scalar1=li4[p2][:, cc:cc + 1], scalar2=None, op0=ALU.mult)
                if cc == 3:
                    v = ctx.inc(i, c, 1)
                    M(("psc", it, 3), c, v)

        @block.tensor
        def _(tensor, it=it, p2=p2):
            ctx.wait(tensor, ("psc", it, 3))
            if it >= 1:
                ctx.wait(tensor, ("pte", it - 1))
            i = None
            for cc in range(4):
                i = nc.tensor.transpose(ppt[:, 128 * cc:128 * (cc + 1)],
                                        pb[p2][:, 128 * cc:128 * (cc + 1)], idb[:])
            v = ctx.inc(i, trn, 1)
            M(("ptr", it), trn, v)

        @block.vector
        def _(vector, it=it, p2=p2):
            ctx.wait(vector, ("ptr", it))
            if it >= 2:
                ctx.wait(vector, ("pv", it - 2))
            i = vector.tensor_copy(ptb[p2][:], ppt[:])
            v = ctx.inc(i, c, 1)
            M(("pte", it), c, v)

        @block.tensor
        def _(tensor, it=it, kvh=kvh, bb=bb, p2=p2):
            ctx.wait(tensor, ("pte", it))
            if it >= 1:
                ctx.wait(tensor, ("oev", it - 1))
            i = None
            for j in range(8):
                i = nc.tensor.matmul(po[:, 64 * j:64 * (j + 1)],
                                     kv[bb][j][:, 512 + 128 * kvh:512 + 128 * (kvh + 1)],
                                     ptb[p2][:, 128 * (j // 2) + 64 * (j % 2):128 * (j // 2) + 64 * (j % 2) + 64],
                                     start=True, stop=True)
            v = ctx.inc(i, trn, 1)
            M(("pv", it), trn, v)

        @block.scalar
        def _(scalar, it=it, p2=p2):
            ctx.wait(scalar, ("pv", it))
            if it >= 2:
                ctx.wait(scalar, ("orr", it - 2))
            i = scalar.activation(out=osb[p2][:], in_=po[:], func=AF.Copy)
            v = ctx.inc(i, a, 1)
            M(("oev", it), a, v)

        @block.vector
        def _(vector, it=it, p2=p2):
            ctx.wait(vector, ("oev", it))
            if it >= 2:
                ctx.wait(vector, ("ost", it - 2))
            iv = osb[p2][:].rearrange("p (j h t) -> p j h t", h=4, t=16).transpose([0, 2, 1, 3])
            ov = orb[p2][:].rearrange("p (h j t) -> p h j t", h=4, t=16)
            i = vector.tensor_copy(ov, iv)
            v = ctx.inc(i, c, 1)
            M(("orr", it), c, v)

        @block.gpsimd
        def _(gpsimd, it=it, G=G, kvh=kvh, p2=p2):
            if it >= 1:
                ctx.ckpt(gpsimd, st, stc, ("ost", it - 1))
            ctx.wait(gpsimd, ("orr", it))
            i = None
            for ql in range(4):
                qh = 4 * kvh + ql
                i = gpsimd.dma_start(out=ap["attn1_fm"][128 * qh:128 * (qh + 1), 128 * G:128 * (G + 1)],
                                     in_=orb[p2][:, 128 * ql:128 * (ql + 1)])
                ctx.inc(i, st, 16)


def stage11_sems(tag="s11"):
    return ["ld", "wld", "mm", "ev", "c", "a", "trn", "g", "evx", "evy", "st", "ldc", "wldc", "stc", "s10_cc"]




def emit_stage13a(ctx, ap, stack, deps, mun_sb, tag="s13a"):
    ctx.scope = tag
    nc, block = ctx.nc, ctx.block
    sb = lambda n, s, d: alloc_sbuf(nc, stack, f"{tag}_{n}", s, d)
    psa = lambda n, s, d: alloc_psum(nc, stack, f"{tag}_{n}", s, d)

    mus = [sb(f"mus{j}", [128, 16 * 512], BF16) for j in range(2)]
    sqs = [sb(f"sqs{j}", [128, 16 * 512], BF16) for j in range(2)]
    rt = sb("rt", [128, 512], F32)
    rn = sb("rn", [128, 512], F32)
    ones128 = sb("ones128", [128, 128], BF16)
    dmy = [sb(f"dmy{j}", [128, 32], F32) for j in range(2)]
    pn = psa("pn", [128, 512], F32)

    ld, wld, c, a, trn, st = ("ld", "wld", "c", "a", "trn", "st")
    ldc, wldc, stc = ("ldc", "wldc", "stc")
    M = ctx.mark
    mur = ap["mu_in"].rearrange("(c p) e -> p c e", p=128)

    @block.sync
    def _(sync):
        for s_, v_ in deps:
            sync.wait_ge(ctx.h[s_], v_)
        ctx.inc(sync.dma_start(out=ones128[:], in_=ap["ones128b"]), ld, 16)
        ctx.ckpt(sync, ld, ldc, ("aux",))

    @block.vector
    def _(vector):
        for s_, v_ in deps:
            vector.wait_ge(ctx.h[s_], v_)
        vector.memset(dmy[0][:], 0.0)

    for eng in ("tensor", "scalar", "gpsimd"):
        @getattr(block, eng)
        def _(e):
            for s_, v_ in deps:
                e.wait_ge(ctx.h[s_], v_)

    @block.gpsimd
    def _(gpsimd):
        ctx.inc(gpsimd.dma_start(out=mus[0][:].rearrange("p (c e) -> p c e", e=512),
                                 in_=mur[:, :, 0:512]), wld, 16)
        ctx.ckpt(gpsimd, wld, wldc, ("muld", 0))

    for esc in range(4):
        if esc + 1 < 4:
            @block.gpsimd
            def _(gpsimd, esc=esc):
                if esc >= 1:
                    ctx.wait(gpsimd, ("mne", esc - 1, 15))
                    ctx.wait(gpsimd, ("sqA", esc - 1))
                ctx.inc(gpsimd.dma_start(out=mus[(esc + 1) % 2][:].rearrange("p (c e) -> p c e", e=512),
                                         in_=mur[:, :, 512 * (esc + 1):512 * (esc + 2)]), wld, 16)
                ctx.ckpt(gpsimd, wld, wldc, ("muld", esc + 1))

        @block.scalar
        def _(scalar, esc=esc):
            ctx.wait(scalar, ("muld", esc))
            if esc == 0:
                ctx.wait(scalar, ("aux",))
            if esc >= 2:
                ctx.wait(scalar, ("nmm", esc - 2, 15))
            i = scalar.activation(out=sqs[esc % 2][:], in_=mus[esc % 2][:], func=AF.Square)
            v = ctx.inc(i, a, 1)
            M(("sqA", esc), a, v)

        for dc in range(16):
            @block.tensor
            def _(tensor, esc=esc, dc=dc):
                if dc == 0:
                    ctx.wait(tensor, ("sqA", esc))
                    if esc >= 1:
                        ctx.wait(tensor, ("rev", esc - 1))
                i = nc.tensor.matmul(pn[:], ones128[:], sqs[esc % 2][:, 512 * dc:512 * (dc + 1)],
                                     start=(dc == 0), stop=(dc == 15))
                v = ctx.inc(i, trn, 1)
                M(("nmm", esc, dc), trn, v)

        @block.scalar
        def _(scalar, esc=esc):
            ctx.wait(scalar, ("nmm", esc, 15))
            i = scalar.activation(out=rt[:], in_=pn[:], func=AF.Sqrt)
            v = ctx.inc(i, a, 1)
            M(("rts", esc), a, v)

        @block.vector
        def _(vector, esc=esc):
            ctx.wait(vector, ("rts", esc))
            vector.tensor_scalar(out=rt[:], in0=rt[:], scalar1=1e-12, scalar2=None, op0=ALU.max)
            vector.tensor_copy(dmy[1][:], dmy[0][:])
            i = vector.reciprocal(out=rn[:], in_=rt[:])
            v = ctx.inc(i, c, 1)
            M(("rev", esc), c, v)

        for dc in range(16):
            @block.vector
            def _(vector, esc=esc, dc=dc):
                if dc == 0:
                    ctx.wait(vector, ("rev", esc))
                i = vector.tensor_tensor(out=mun_sb[:, 2048 * dc + 512 * esc:2048 * dc + 512 * (esc + 1)],
                                         in0=mus[esc % 2][:, 512 * dc:512 * (dc + 1)],
                                         in1=rn[:], op=ALU.mult)
                v = ctx.inc(i, c, 1)
                M(("mne", esc, dc), c, v)


def stage13a_sems(tag="s13a"):
    return ["ld", "wld", "mm", "ev", "c", "a", "trn", "g", "evx", "evy", "st", "ldc", "wldc", "stc", "s10_cc"]


def emit_stage13b(ctx, ap, stack, deps, b, mun_sb, tag=None):
    tag = tag or f"s13b{b}"
    ctx.scope = tag
    nc, block = ctx.nc, ctx.block
    sb = lambda n, s, d: alloc_sbuf(nc, stack, f"{tag}_{n}", s, d)
    psa = lambda n, s, d: alloc_psum(nc, stack, f"{tag}_{n}", s, d)

    xns = [sb(f"xns{j}", [128, 2048], BF16) for j in range(16)]
    lg = [sb(f"lg{j}", [128, 2048], F32) for j in range(2)]
    dspb = [sb(f"dspb{j}", [128, 2048], BF16) for j in range(2)]
    dspc = [sb(f"dspc{j}", [128, 2048], BF16) for j in range(2)]
    lsum = [sb(f"ls{j}", [128, 1], F32) for j in range(2)]
    linv = sb("linv", [128, 1], F32)
    idb = sb("idb", [128, 128], BF16)
    ones_c = sb("ones_c", [128, 1], BF16)
    rdt = sb("rdt", [128, 16], F32)
    dtmp = sb("dtmp", [128, 16], F32)
    evd = [sb(f"evd{j}", [128, 512], BF16) for j in range(2)]
    dmy = [sb(f"dmy{j}", [128, 32], F32) for j in range(2)]
    pl = psa("pl", [128, 2048], F32)
    ptd = [psa(f"ptd{j}", [128, 512], BF16) for j in range(2)]
    pdn = psa("pdn", [128, 16], F32)

    ld, wld, c, a, trn, st = ("ld", "wld", "c", "a", "trn", "st")
    ldc, wldc, stc = ("ldc", "wldc", "stc")
    M = ctx.mark
    sepv = lambda e: e.tensor_copy(dmy[1][:], dmy[0][:])

    @block.sync
    def _(sync):
        for s_, v_ in deps:
            sync.wait_ge(ctx.h[s_], v_)
        ctx.inc(sync.dma_start(out=idb[:], in_=ap["identb"]), ld, 16)
        for dc in range(16):
            ctx.inc(sync.dma_start(out=xns[dc][:],
                                   in_=ap["x_n_fm"][128 * dc:128 * (dc + 1), 2048 * b:2048 * (b + 1)]), ld, 16)
        ctx.ckpt(sync, ld, ldc, ("xld",))

    @block.vector
    def _(vector):
        for s_, v_ in deps:
            vector.wait_ge(ctx.h[s_], v_)
        vector.memset(dmy[0][:], 0.0)
        vector.memset(ones_c[:], 1.0)

    for eng in ("tensor", "scalar", "gpsimd"):
        @getattr(block, eng)
        def _(e):
            for s_, v_ in deps:
                e.wait_ge(ctx.h[s_], v_)

    def emit_dn_dtr(esc):
        @block.tensor
        def _(tensor, esc=esc):
            ctx.wait(tensor, ("dexp", esc))
            i = None
            for nt in range(16):
                i = nc.tensor.matmul(pdn[:, nt:nt + 1],
                                     dspb[esc % 2][:, 128 * nt:128 * (nt + 1)],
                                     ones_c[:], start=(esc == 0), stop=(esc == 15))
            v = ctx.inc(i, trn, 1)
            M(("dnmm", esc), trn, v)

        for g4 in range(4):
            @block.tensor
            def _(tensor, esc=esc, g4=g4):
                if g4 == 0:
                    ctx.wait(tensor, ("dsc", esc))
                key = ("dte", esc, g4 - 2) if g4 >= 2 else (("dte", esc - 1, 2 + g4) if esc >= 1 else None)
                if key is not None:
                    ctx.wait(tensor, key)
                i = None
                for jj in range(4):
                    j = 4 * g4 + jj
                    i = nc.tensor.transpose(ptd[g4 % 2][:, 128 * jj:128 * (jj + 1)],
                                            dspc[esc % 2][:, 128 * j:128 * (j + 1)], idb[:])
                v = ctx.inc(i, trn, 1)
                M(("dtr", esc, g4), trn, v)

            @block.vector
            def _(vector, esc=esc, g4=g4):
                ctx.wait(vector, ("dtr", esc, g4))
                key = ("dst_", esc, g4 - 2) if g4 >= 2 else (("dst_", esc - 1, 2 + g4) if esc >= 1 else None)
                if key is not None:
                    ctx.wait(vector, key)
                i = vector.tensor_copy(evd[g4 % 2][:], ptd[g4 % 2][:])
                v = ctx.inc(i, c, 1)
                M(("dte", esc, g4), c, v)

            @block.gpsimd
            def _(gpsimd, esc=esc, g4=g4):
                if not (esc == 0 and g4 == 0):
                    pr = (esc, g4 - 1) if g4 >= 1 else (esc - 1, 3)
                    ctx.ckpt(gpsimd, st, stc, ("dst_", pr[0], pr[1]))
                ctx.wait(gpsimd, ("dte", esc, g4))
                dd = ap["dispatch_tm"].rearrange("(B J P) e -> B P J e", B=2, P=128)[b]
                ctx.inc(gpsimd.dma_start(out=dd[:, 4 * g4:4 * (g4 + 1), 128 * esc:128 * (esc + 1)],
                                         in_=evd[g4 % 2][:].rearrange("p (j n) -> p j n", n=128)), st, 16)

    for esc in range(16):
        @block.tensor
        def _(tensor, esc=esc):
            if esc == 0:
                ctx.wait(tensor, ("xld",))
            if esc >= 1:
                ctx.wait(tensor, ("lev", esc - 1))
            i = None
            for nb in range(4):
                for dc in range(16):
                    i = nc.tensor.matmul(pl[:, 512 * nb:512 * (nb + 1)],
                                         mun_sb[:, 2048 * dc + 128 * esc:2048 * dc + 128 * (esc + 1)],
                                         xns[dc][:, 512 * nb:512 * (nb + 1)],
                                         start=(dc == 0), stop=(dc == 15))
            v = ctx.inc(i, trn, 1)
            M(("lmm", esc), trn, v)

        @block.vector
        def _(vector, esc=esc):
            ctx.wait(vector, ("lmm", esc))
            if esc >= 2:
                ctx.wait(vector, ("dexp", esc - 2))
            i = vector.tensor_copy(lg[esc % 2][:], pl[:])
            v = ctx.inc(i, c, 1)
            M(("lev", esc), c, v)

        @block.scalar
        def _(scalar, esc=esc):
            ctx.wait(scalar, ("lev", esc))
            if esc >= 2:
                ctx.wait(scalar, ("cmst", esc - 2))
                ctx.wait(scalar, ("dnmm", esc - 2))
                ctx.wait(scalar, ("dsc", esc - 2))
            scalar.activation(out=dspb[esc % 2][:], in_=lg[esc % 2][:], func=AF.Exp,
                              accum_out=lsum[esc % 2][:])
            i = scalar.activation(out=dmy[1][:], in_=dmy[0][:], func=AF.Copy)
            v = ctx.inc(i, a, 1)
            M(("dexp", esc), a, v)

        @block.vector
        def _(vector, esc=esc):
            ctx.wait(vector, ("dexp", esc))
            vector.reciprocal(out=linv[:], in_=lsum[esc % 2][:])
            sepv(vector)
            if esc >= 2:
                ctx.wait(vector, ("dtr", esc - 2, 3))
            i = vector.tensor_scalar(out=dspc[esc % 2][:], in0=dspb[esc % 2][:],
                                     scalar1=linv[:], scalar2=None, op0=ALU.mult)
            v = ctx.inc(i, c, 1)
            M(("dsc", esc), c, v)

        @block.gpsimd
        def _(gpsimd, esc=esc):
            if esc >= 1:
                ctx.ckpt(gpsimd, st, stc, ("cmst", esc - 1))
            ctx.wait(gpsimd, ("dexp", esc))
            ctx.inc(gpsimd.dma_start(
                out=ap["combine_fm"][2048 * b + 128 * esc:2048 * b + 128 * (esc + 1), :],
                in_=dspb[esc % 2][:]), st, 16)

        if esc >= 1:
            emit_dn_dtr(esc - 1)

    emit_dn_dtr(15)

    @block.vector
    def _(vector):
        ctx.wait(vector, ("dnmm", 15))
        vector.tensor_copy(dtmp[:], pdn[:])
        sepv(vector)
        i = vector.reciprocal(out=rdt[:], in_=dtmp[:])
        v = ctx.inc(i, c, 1)
        M(("rdtv",), c, v)

    @block.gpsimd
    def _(gpsimd):
        ctx.ckpt(gpsimd, st, stc, ("cmst", 15))
        ctx.wait(gpsimd, ("rdtv",))
        ctx.inc(gpsimd.dma_start(out=ap["rden_d"][128 * b:128 * (b + 1), :], in_=rdt[:]), st, 16)
        ctx.ckpt(gpsimd, st, stc, ("dst_", 15, 3))


def stage13b_sems(tag):
    return ["ld", "wld", "mm", "ev", "c", "a", "trn", "g", "evx", "evy", "st", "ldc", "wldc", "stc", "s10_cc"]


def emit_stage13c(ctx, ap, stack, deps, b, tag=None):
    tag = tag or f"s13c{b}"
    ctx.scope = tag
    nc, block = ctx.nc, ctx.block
    sb = lambda n, s, d: alloc_sbuf(nc, stack, f"{tag}_{n}", s, d)
    psa = lambda n, s, d: alloc_psum(nc, stack, f"{tag}_{n}", s, d)

    lgt = [sb(f"lgt{j}", [128, 2048], F32) for j in range(2)]
    cmb = [sb(f"cmb{j}", [128, 2048], BF16) for j in range(2)]
    nmx = [sb(f"nmx{j}", [128, 1], F32) for j in range(2)]
    lsum = [sb(f"ls{j}", [128, 1], F32) for j in range(2)]
    linv = sb("linv", [128, 1], F32)
    idb = sb("idb", [128, 128], BF16)
    evc = [sb(f"evc{j}", [128, 512], BF16) for j in range(2)]
    dmy = [sb(f"dmy{j}", [128, 32], F32) for j in range(2)]
    ptc = [psa(f"ptc{j}", [128, 512], BF16) for j in range(2)]

    ld, c, a, trn, st = ("ld", "c", "a", "trn", "st")
    ldc, stc = "ldc", "stc"
    M = ctx.mark
    sepv = lambda e: e.tensor_copy(dmy[1][:], dmy[0][:])

    @block.sync
    def _(sync):
        for s_, v_ in deps:
            sync.wait_ge(ctx.h[s_], v_)
        ctx.inc(sync.dma_start(out=idb[:], in_=ap["identb"]), ld, 16)
        ctx.inc(sync.dma_start(out=lgt[0][:], in_=ap["logT"][0:128, :]), ld, 16)
        ctx.ckpt(sync, ld, ldc, ("lld", 0))

    @block.vector
    def _(vector):
        for s_, v_ in deps:
            vector.wait_ge(ctx.h[s_], v_)
        vector.memset(dmy[0][:], 0.0)

    for eng in ("tensor", "scalar", "gpsimd"):
        @getattr(block, eng)
        def _(e):
            for s_, v_ in deps:
                e.wait_ge(ctx.h[s_], v_)

    for nt in range(16):
        if nt + 1 < 16:
            @block.sync
            def _(sync, nt=nt):
                if nt >= 1:
                    ctx.wait(sync, ("cexp", nt - 1))
                ctx.inc(sync.dma_start(out=lgt[(nt + 1) % 2][:],
                                       in_=ap["logT"][128 * (nt + 1):128 * (nt + 2), :]), ld, 16)
                ctx.ckpt(sync, ld, ldc, ("lld", nt + 1))

        @block.vector
        def _(vector, nt=nt):
            ctx.wait(vector, ("lld", nt))
            i = vector.tensor_reduce(nmx[nt % 2][:], lgt[nt % 2][:], axis=AX.X, op=ALU.max, negate=True)
            v = ctx.inc(i, c, 1)
            M(("cmax", nt), c, v)

        @block.scalar
        def _(scalar, nt=nt):
            ctx.wait(scalar, ("cmax", nt))
            if nt >= 2:
                ctx.wait(scalar, ("ctr", nt - 2, 3))
            scalar.activation(out=cmb[nt % 2][:], in_=lgt[nt % 2][:], func=AF.Exp,
                              bias=nmx[nt % 2][:], accum_out=lsum[nt % 2][:])
            i = scalar.activation(out=dmy[1][:], in_=dmy[0][:], func=AF.Copy)
            v = ctx.inc(i, a, 1)
            M(("cexp", nt), a, v)

        @block.vector
        def _(vector, nt=nt):
            ctx.wait(vector, ("cexp", nt))
            vector.reciprocal(out=linv[:], in_=lsum[nt % 2][:])
            sepv(vector)
            i = vector.tensor_scalar(out=cmb[nt % 2][:], in0=cmb[nt % 2][:],
                                     scalar1=linv[:], scalar2=None, op0=ALU.mult)
            v = ctx.inc(i, c, 1)
            M(("csc", nt), c, v)

        for g4 in range(4):
            @block.tensor
            def _(tensor, nt=nt, g4=g4):
                if g4 == 0:
                    ctx.wait(tensor, ("csc", nt))
                key = ("cte", nt, g4 - 2) if g4 >= 2 else (("cte", nt - 1, 2 + g4) if nt >= 1 else None)
                if key is not None:
                    ctx.wait(tensor, key)
                i = None
                for jj in range(4):
                    j = 4 * g4 + jj
                    i = nc.tensor.transpose(ptc[g4 % 2][:, 128 * jj:128 * (jj + 1)],
                                            cmb[nt % 2][:, 128 * j:128 * (j + 1)], idb[:])
                v = ctx.inc(i, trn, 1)
                M(("ctr", nt, g4), trn, v)

            @block.vector
            def _(vector, nt=nt, g4=g4):
                ctx.wait(vector, ("ctr", nt, g4))
                key = ("cst", nt, g4 - 2) if g4 >= 2 else (("cst", nt - 1, 2 + g4) if nt >= 1 else None)
                if key is not None:
                    ctx.wait(vector, key)
                i = vector.tensor_copy(evc[g4 % 2][:], ptc[g4 % 2][:])
                v = ctx.inc(i, c, 1)
                M(("cte", nt, g4), c, v)

            @block.gpsimd
            def _(gpsimd, nt=nt, g4=g4):
                if not (nt == 0 and g4 == 0):
                    pr = (nt, g4 - 1) if g4 >= 1 else (nt - 1, 3)
                    ctx.ckpt(gpsimd, st, stc, ("cst", pr[0], pr[1]))
                ctx.wait(gpsimd, ("cte", nt, g4))
                dd = ap["combine_fm"].rearrange("(J P) n -> P J n", P=128)
                ctx.inc(gpsimd.dma_start(out=dd[:, 4 * g4:4 * (g4 + 1), 128 * nt:128 * (nt + 1)],
                                         in_=evc[g4 % 2][:].rearrange("p (j n) -> p j n", n=128)), st, 16)
    @block.gpsimd
    def _(gpsimd):
        ctx.ckpt(gpsimd, st, stc, ("cst", 15, 3))


def stage13c_sems(tag):
    return ["ld", "wld", "mm", "ev", "c", "a", "trn", "g", "evx", "evy", "st", "ldc", "wldc", "stc", "s10_cc"]


def emit_stage13def(ctx, ap, stack, deps, b, tag=None):
    tag = tag or f"s13d{b}"
    ctx.scope = tag
    nc, block = ctx.nc, ctx.block
    sb = lambda n, s, d: alloc_sbuf(nc, stack, f"{tag}_{n}", s, d)
    psa = lambda n, s, d: alloc_psum(nc, stack, f"{tag}_{n}", s, d)

    SI = [sb(f"si{dc}", [128, 2048], BF16) for dc in range(16)]
    xls = [sb(f"xls{j}", [128, 2048], BF16) for j in range(16)]
    SO = xls
    w1s = [sb(f"w1s{j}", [128, 16 * 512], BF16) for j in range(2)]
    dsp = [w1s[j // 8][:, 512 * (j % 8):512 * (j % 8 + 1)] for j in range(16)]
    w2s = sb("w2s", [128, 4 * 2048], BF16)
    b1s = sb("b1s", [128, 64], F32)
    rdt = sb("rdt", [128, 16], F32)
    b2t = [sb(f"b2t{j}", [128, 2048], BF16) for j in range(2)]
    hts = [sb(f"hts{j}", [128, 512], BF16) for j in range(2)]
    ot = [w1s[j][:].bitcast(F32)[:, 0:2048] for j in range(2)]
    ps4 = [psa(f"ps4{j}", [128, 512], F32) for j in range(8)]

    ld, wld, c, a, trn, st = ("ld", "wld", "c", "a", "trn", "st")
    ldc, wldc, stc = ("ldc", "wldc", "stc")
    M = ctx.mark

    @block.sync
    def _(sync):
        for s_, v_ in deps:
            sync.wait_ge(ctx.h[s_], v_)
        for nc_ in range(16):
            ctx.inc(sync.dma_start(out=xls[nc_][:],
                                   in_=ap["x_ln_tm"][2048 * b + 128 * nc_:2048 * b + 128 * (nc_ + 1), :]), ld, 16)
        ctx.inc(sync.dma_start(out=rdt[:], in_=ap["rden_d"][128 * b:128 * (b + 1), :]), ld, 16)
        ctx.ckpt(sync, ld, ldc, ("xlld",))

    for eng in ("tensor", "vector", "scalar", "gpsimd"):
        @getattr(block, eng)
        def _(e):
            for s_, v_ in deps:
                e.wait_ge(ctx.h[s_], v_)

    for es4 in range(4):
        @block.sync
        def _(sync, es4=es4):
            dd = ap["dispatch_tm"].rearrange("(B J P) e -> B P J e", B=2, P=128)[b]
            for nc_ in range(16):
                ctx.inc(sync.dma_start(out=dsp[nc_],
                                       in_=dd[:, nc_, 512 * es4:512 * (es4 + 1)]), ld, 16)
            ctx.ckpt(sync, ld, ldc, ("dld", es4))

        for dc in range(16):
            @block.tensor
            def _(tensor, es4=es4, dc=dc):
                if dc == 0:
                    ctx.wait(tensor, ("dld", es4))
                    if es4 == 0:
                        ctx.wait(tensor, ("xlld",))
                pidx = dc % 2
                if 16 * es4 + dc >= 2:
                    pr = 16 * es4 + dc - 2
                    ctx.wait(tensor, ("siev", pr))
                i = None
                for nc_ in range(16):
                    i = nc.tensor.matmul(ps4[pidx][:], xls[nc_][:, 128 * dc:128 * (dc + 1)],
                                         dsp[nc_], start=(nc_ == 0), stop=(nc_ == 15))
                v = ctx.inc(i, trn, 1)
                M(("simm", es4, dc), trn, v)

            @block.vector
            def _(vector, es4=es4, dc=dc):
                ctx.wait(vector, ("simm", es4, dc))
                i = vector.tensor_copy(SI[dc][:, 512 * es4:512 * (es4 + 1)], ps4[dc % 2][:])
                v = ctx.inc(i, c, 1)
                M(("siev", 16 * es4 + dc), c, v)

    @block.gpsimd
    def _(gpsimd):
        ctx.wait(gpsimd, ("simm", 3, 15))
        w1r = ap["W1_in"].rearrange("E (c p) h -> E p c h", p=128)
        ctx.inc(gpsimd.dma_start(out=w1s[0][:].rearrange("p (c h) -> p c h", h=512),
                                 in_=w1r[0]), wld, 16)
        ctx.inc(gpsimd.dma_start(out=b1s[:].rearrange("p (hc e) -> p hc e", e=16),
                                 in_=ap["b1T_in"].rearrange("(hc p) e -> p hc e", p=128)), wld, 16)
        w2r = ap["W2_in"].rearrange("E (hc p) d -> E p hc d", p=128)
        ctx.inc(gpsimd.dma_start(out=w2s[:].rearrange("p (hc d) -> p hc d", d=2048),
                                 in_=w2r[0]), wld, 16)
        ctx.inc(gpsimd.dma_start(out=b2t[0][:], in_=ap["b2B"][0]), wld, 16)
        ctx.ckpt(gpsimd, wld, wldc, ("w1ld", 0))
        M(("w2ld", 0), wldc, ctx.n[wldc])

    for e in range(16):
        if e + 1 < 16:
            @block.gpsimd
            def _(gpsimd, e=e):
                if e >= 1:
                    ctx.wait(gpsimd, ("h1mm", e - 1, 3))
                w1r = ap["W1_in"].rearrange("E (c p) h -> E p c h", p=128)
                ctx.inc(gpsimd.dma_start(out=w1s[(e + 1) % 2][:].rearrange("p (c h) -> p c h", h=512),
                                         in_=w1r[e + 1]), wld, 16)
                ctx.inc(gpsimd.dma_start(out=b2t[(e + 1) % 2][:], in_=ap["b2B"][e + 1]), wld, 16)
                ctx.ckpt(gpsimd, wld, wldc, ("w1ld", e + 1))

        for hc in range(4):
            @block.tensor
            def _(tensor, e=e, hc=hc):
                if hc == 0:
                    ctx.wait(tensor, ("w1ld", e))
                    if e == 0:
                        ctx.wait(tensor, ("siev", 63))
                    if e >= 2:
                        ctx.wait(tensor, ("gel", e - 2, 3))
                i = None
                for dc in range(16):
                    i = nc.tensor.matmul(ps4[4 + e % 2][:, 128 * hc:128 * (hc + 1)],
                                         w1s[e % 2][:, 512 * dc + 128 * hc:512 * dc + 128 * (hc + 1)],
                                         SI[dc][:, 128 * e:128 * (e + 1)],
                                         start=(dc == 0), stop=(dc == 15))
                v = ctx.inc(i, trn, 1)
                M(("h1mm", e, hc), trn, v)

        @block.scalar
        def _(scalar, e=e):
            ctx.wait(scalar, ("h1mm", e, 3))
            if e >= 2:
                ctx.wait(scalar, ("so4", e - 2))
            for hc in range(4):
                i = scalar.activation(out=hts[e % 2][:, 128 * hc:128 * (hc + 1)],
                                      in_=ps4[4 + e % 2][:, 128 * hc:128 * (hc + 1)], func=AF.Gelu,
                                      bias=b1s[:, 16 * hc + e:16 * hc + e + 1])
                v = ctx.inc(i, a, 1)
                M(("gel", e, hc), a, v)

        @block.tensor
        def _(tensor, e=e):
            ctx.wait(tensor, ("gel", e, 3))
            ctx.wait(tensor, ("w2ld", e))
            if e == 0:
                ctx.wait(tensor, ("siev", 63))
            if e >= 1:
                ctx.wait(tensor, ("soev", e - 1, 3))
            i = None
            for nb in range(4):
                for hc in range(4):
                    i = nc.tensor.matmul(ps4[nb][:],
                                         hts[e % 2][:, 128 * hc:128 * (hc + 1)],
                                         w2s[:, 2048 * hc + 512 * nb:2048 * hc + 512 * (nb + 1)],
                                         start=(hc == 0), stop=(hc == 3))
            v = ctx.inc(i, trn, 1)
            M(("so4", e), trn, v)

        @block.vector
        def _(vector, e=e):
            ctx.wait(vector, ("so4", e))
            i = None
            for nb in range(4):
                i = vector.tensor_tensor(out=SO[e][:, 512 * nb:512 * (nb + 1)],
                                         in0=ps4[nb][:], in1=b2t[e % 2][:, 512 * nb:512 * (nb + 1)],
                                         op=ALU.add)
                v = ctx.inc(i, c, 1)
                M(("soev", e, nb), c, v)

        if e == 15:
            @block.sync
            def _(sync):
                ctx.wait(sync, ("h1mm", 15, 3))
                for esc in range(16):
                    ctx.inc(sync.dma_start(
                        out=SI[esc][:],
                        in_=ap["combine_fm"][2048 * b + 128 * esc:2048 * b + 128 * (esc + 1), :]), ld, 16)
                ctx.ckpt(sync, ld, ldc, ("cmbld",))

        if e + 1 < 16:
            @block.gpsimd
            def _(gpsimd, e=e):
                ctx.wait(gpsimd, ("so4", e))
                w2r = ap["W2_in"].rearrange("E (hc p) d -> E p hc d", p=128)
                ctx.inc(gpsimd.dma_start(out=w2s[:].rearrange("p (hc d) -> p hc d", d=2048),
                                         in_=w2r[e + 1]), wld, 16)
                ctx.ckpt(gpsimd, wld, wldc, ("w2ld", e + 1))

    for nt in range(16):
        @block.tensor
        def _(tensor, nt=nt):
            if nt == 0:
                ctx.wait(tensor, ("cmbld",))
                ctx.wait(tensor, ("soev", 15, 3))
            i = None
            for nb in range(4):
                if nt >= 2:
                    ctx.wait(tensor, ("fevb", nt - 2, nb))
                for esc in range(16):
                    i = nc.tensor.matmul(ps4[4 * (nt % 2) + nb][:],
                                         SI[esc][:, 128 * nt:128 * (nt + 1)],
                                         SO[esc][:, 512 * nb:512 * (nb + 1)],
                                         start=(esc == 0), stop=(esc == 15))
            v = ctx.inc(i, trn, 1)
            M(("fmm", nt), trn, v)

        @block.vector
        def _(vector, nt=nt):
            ctx.wait(vector, ("fmm", nt))
            if nt >= 2:
                ctx.wait(vector, ("fst", nt - 2))
            i = None
            for nb in range(4):
                i = vector.tensor_scalar(out=ot[nt % 2][:, 512 * nb:512 * (nb + 1)],
                                         in0=ps4[4 * (nt % 2) + nb][:],
                                         scalar1=rdt[:, nt:nt + 1], scalar2=None, op0=ALU.mult)
                v = ctx.inc(i, c, 1)
                M(("fevb", nt, nb), c, v)
            M(("fev", nt), c, ctx.n[c])

        @block.gpsimd
        def _(gpsimd, nt=nt):
            if nt >= 1:
                ctx.ckpt(gpsimd, st, stc, ("fst", nt - 1))
            ctx.wait(gpsimd, ("fev", nt))
            ctx.inc(gpsimd.dma_start(out=ap["out_tm"][2048 * b + 128 * nt:2048 * b + 128 * (nt + 1), :],
                                     in_=ot[nt % 2], accum_op=ALU.add), st, 16)
    @block.gpsimd
    def _(gpsimd):
        ctx.ckpt(gpsimd, st, stc, ("fst", 15))


def stage13def_sems(tag):
    return ["ld", "wld", "mm", "ev", "c", "a", "trn", "g", "evx", "evy", "st", "ldc", "wldc", "stc", "s10_cc"]

GLOBAL_SEMS = ["ld", "wld", "mm", "ev", "c", "a", "trn", "g", "evx", "evy", "st",
               "ldc", "wldc", "stc", "s10_cc"]

SPECS = {
    "x_tm": ((T, D), F32, "ExternalInput"),
    "lnw": ((128, 32), F32, "ExternalInput"),
    "identb": ((128, 128), BF16, "ExternalInput"),
    "identf": ((128, 128), F32, "ExternalInput"),
    "ones1b": ((1, 128), BF16, "ExternalInput"),
    "ones128b": ((128, 128), BF16, "ExternalInput"),
    "mask128": ((128, 128), BF16, "ExternalInput"),
    "mask4": ((128, 4), F32, "ExternalInput"),
    "sel4": ((4, 128), BF16, "ExternalInput"),
    "tri_maskb": ((128, 128), BF16, "ExternalInput"),
    "mask_l1b": ((128, 64), BF16, "ExternalInput"),
    "w_qkv0T": ((D, 6144), F32, "ExternalInput"),
    "w_proj0T": ((D, D), F32, "ExternalInput"),
    "b_proj0": ((1, D), F32, "ExternalInput"),
    "WqT": ((D, D), F32, "ExternalInput"),
    "WkT": ((D, D), F32, "ExternalInput"),
    "WvT": ((D, D), F32, "ExternalInput"),
    "Wo_in": ((D, D), F32, "ExternalInput"),
    "A_in": ((D, 16), F32, "ExternalInput"),
    "B_in": ((16, 8192), F32, "ExternalInput"),
    "W_inT": ((D, 8192), F32, "ExternalInput"),
    "W_upT": ((D, 8192), F32, "ExternalInput"),
    "W_downT": ((8192, D), F32, "ExternalInput"),
    "gammaB": ((128, D), BF16, "ExternalInput"),
    "betaB": ((128, D), BF16, "ExternalInput"),
    "scaleB": ((128, 1), F32, "ExternalInput"),
    "w_qkv1T": ((D, 3072), F32, "ExternalInput"),
    "w_proj1T": ((D, D), F32, "ExternalInput"),
    "mu_in": ((D, D), F32, "ExternalInput"),
    "W1_in": ((16, D, 512), F32, "ExternalInput"),
    "W2_in": ((16, 512, D), F32, "ExternalInput"),
    "b1T_in": ((512, 16), F32, "ExternalInput"),
    "b2B": ((16, 128, D), BF16, "ExternalInput"),
    "xn_tm": ((T, D), F32, "Internal"),
    "xn_fm": ((D, T), BF16, "Internal"),
    "h1_fm": ((D, T), BF16, "Internal"),
    "h2_fm": ((D, T), BF16, "Internal"),
    "qkv0_fm": ((6144, T), BF16, "Internal"),
    "attn0_fm": ((D, T), BF16, "Internal"),
    "x1_tm": ((T, D), F32, "Internal"),
    "qkv1_fm": ((6144, T), BF16, "Internal"),
    "attnPV_fm": ((D, T), BF16, "Internal"),
    "mm_fm": ((16, T), BF16, "Internal"),
    "win_b": ((64, 128, D), BF16, "Internal"),
    "wup_b": ((64, 128, D), BF16, "Internal"),
    "gu_fm": ((8192, T), BF16, "Internal"),
    "xn1_fm": ((D, T), BF16, "Internal"),
    "x_ln_tm": ((T, D), BF16, "Internal"),
    "x_n_fm": ((D, T), BF16, "Internal"),
    "qkv1L_fm": ((1024, T), BF16, "Internal"),
    "q_bmaj": ((D, T), BF16, "Internal"),
    "kv_tm_loc": ((T, 1024), BF16, "Internal"),
    "kv_all": ((8 * T, 1024), BF16, "Internal"),
    "attn1_fm": ((D, T), BF16, "Internal"),
    "mun_fm": ((D, D), BF16, "Internal"),
    "dispatch_tm": ((2 * D, D), BF16, "Internal"),
    "combine_fm": ((2 * D, D), BF16, "Internal"),
    "rden_d": ((256, 16), F32, "Internal"),
    "out_tm": ((T, D), F32, "ExternalOutput"),
}


def host_prep(inp):
    bfc = lambda x: np.ascontiguousarray(x).astype(ml_dtypes.bfloat16)
    ii = np.arange(128)
    jj = np.arange(64)
    d = {}
    d["lnw"] = np.concatenate([
        inp["L0_ffn_ln1_w"].reshape(16, 128).T,
        inp["L0_ffn_ln2_w"].reshape(16, 128).T,
    ], axis=1).astype(np.float32).copy()
    d["identb"] = np.eye(128, dtype=ml_dtypes.bfloat16)
    d["identf"] = np.eye(128, dtype=np.float32)
    d["ones1b"] = np.ones((1, 128), dtype=ml_dtypes.bfloat16)
    d["ones128b"] = np.ones((128, 128), dtype=ml_dtypes.bfloat16)
    d["mask128"] = (ii[:, None] // 32 == ii[None, :] // 32).astype(ml_dtypes.bfloat16)
    d["mask4"] = (ii[:, None] // 32 == np.arange(4)[None, :]).astype(np.float32)
    d["sel4"] = (np.arange(4)[:, None] == ii[None, :] // 32).astype(ml_dtypes.bfloat16)
    d["tri_maskb"] = np.where(ii[None, :] <= ii[:, None], 0.0, -1e30).astype(ml_dtypes.bfloat16)
    d["mask_l1b"] = np.where((ii[:, None] // 16) == ((jj[None, :] % 16) // 2), 0.0, -1e30).astype(ml_dtypes.bfloat16)
    d["w_qkv0T"] = np.ascontiguousarray(inp["L0_attn_w_qkv"].T.astype(np.float32))
    d["w_proj0T"] = np.ascontiguousarray(inp["L0_attn_w_proj"].T.astype(np.float32))
    d["b_proj0"] = np.asarray(inp["L0_attn_b_proj"], np.float32).reshape(1, D).copy()
    d["WqT"] = np.ascontiguousarray(inp["L0_ffn_Wq"].T.astype(np.float32))
    d["WkT"] = np.ascontiguousarray(inp["L0_ffn_Wk"].T.astype(np.float32))
    d["WvT"] = np.ascontiguousarray(inp["L0_ffn_Wv"].T.astype(np.float32))
    d["Wo_in"] = np.ascontiguousarray(inp["L0_ffn_Wo"].astype(np.float32))
    d["A_in"] = np.ascontiguousarray(inp["L0_ffn_A"].astype(np.float32))
    d["B_in"] = np.ascontiguousarray(inp["L0_ffn_B"].astype(np.float32))
    d["W_inT"] = np.ascontiguousarray(inp["L0_ffn_W_in_base"].T.astype(np.float32))
    d["W_upT"] = np.ascontiguousarray(inp["L0_ffn_W_up_base"].T.astype(np.float32))
    d["W_downT"] = np.ascontiguousarray(inp["L0_ffn_W_down"].T.astype(np.float32))
    d["gammaB"] = np.broadcast_to(bfc(inp["L1_ffn_gamma"]).reshape(1, D), (128, D)).copy()
    d["betaB"] = np.broadcast_to(bfc(inp["L1_ffn_beta"]).reshape(1, D), (128, D)).copy()
    d["scaleB"] = np.broadcast_to(np.asarray(inp["L1_ffn_scale"], np.float32).reshape(1, 1), (128, 1)).copy()
    d["w_qkv1T"] = np.ascontiguousarray(inp["L1_attn_w_qkv"].T.astype(np.float32))
    d["w_proj1T"] = np.ascontiguousarray(inp["L1_attn_w_proj"].T.astype(np.float32))
    d["mu_in"] = np.ascontiguousarray(np.asarray(inp["L1_ffn_mu"], np.float32).reshape(D, D))
    d["W1_in"] = np.ascontiguousarray(inp["L1_ffn_W1"].astype(np.float32))
    d["W2_in"] = np.ascontiguousarray(inp["L1_ffn_W2"].astype(np.float32))
    d["b1T_in"] = np.ascontiguousarray(inp["L1_ffn_b1"].T.astype(np.float32))
    d["b2B"] = np.broadcast_to(bfc(inp["L1_ffn_b2"])[:, None, :], (16, 128, D)).copy()
    return d


def free_stage(stack):
    while stack:
        stack.pop().__exit__(None, None, None)


def build_module():
    nc = bass.Bass("TRN2", target_bir_lowering=False, debug=False)
    ap = {}
    for name, (shape, dtype, kind) in SPECS.items():
        if name == "kv_all":
            ap[name] = nc.dram_tensor(name, list(shape), dtype, addr_space="Shared").ap()
        elif kind == "Internal":
            ap[name] = nc.dram_tensor(name, list(shape), dtype).ap()
        else:
            ap[name] = nc.dram_tensor(name, list(shape), dtype, kind=kind).ap()
    stack = []
    with nc.Block() as block:
        ctx = Ctx(nc, block, GLOBAL_SEMS)

        def dd():
            return [(s_, ctx.n[s_]) for s_ in GLOBAL_SEMS
                    if s_ != "s10_cc" and ctx.n.get(s_, 0) > 0]

        emit_stage1(ctx, ap, stack)
        dX = dd(); free_stage(stack)
        emit_fm_matmul(ctx, ap, stack, "s2", dX, "xn_fm", "qkv0_fm",
                       [("w_qkv0T", 0, 4096, AF.Relu, None),
                        ("w_qkv0T", 4096, 6144, AF.Copy, None)])
        dX = dd(); free_stage(stack)
        emit_stage3(ctx, ap, stack, dX)
        dX = dd(); free_stage(stack)
        emit_tm_matmul(ctx, ap, stack, "s4", dX, "attn0_fm", "w_proj0T", "x1_tm",
                       bias_name="b_proj0", addins=("x_tm", "xn_tm"))
        dX = dd(); free_stage(stack)
        emit_fm_matmul(ctx, ap, stack, "s5", dX, "h1_fm", "qkv1_fm",
                       [("WqT", 0, 2048, AF.Copy, 1.0 / math.sqrt(128.0)),
                        ("WkT", 0, 2048, AF.Copy, None),
                        ("WvT", 0, 2048, AF.Copy, None)])
        dX = dd(); free_stage(stack)
        emit_stage6(ctx, ap, stack, dX)
        dX = dd(); free_stage(stack)
        emit_stage7(ctx, ap, stack, dX)
        dX = dd(); free_stage(stack)
        emit_stage8a(ctx, ap, stack, dX)
        dX = dd(); free_stage(stack)
        emit_stage8c(ctx, ap, stack, dX)
        dX = dd(); free_stage(stack)
        emit_stage9(ctx, ap, stack, dX)
        dX = dd(); free_stage(stack)
        def qre_view(obt, r):
            return obt[:].rearrange("p (b2 s) -> p s b2", s=2)[:, r // 4, 512 * (r % 4):512 * (r % 4) + 512]

        segs10 = []
        for g_ in range(4):
            segs10 += [("w_qkv1T", 768 * g_ + 512, 768 * g_ + 640, AF.Copy, None),
                       ("w_qkv1T", 768 * g_ + 640, 768 * g_ + 768, AF.Copy, None)]
        for g_ in range(4):
            segs10.append(("w_qkv1T", 768 * g_, 768 * g_ + 512, AF.Copy, 1.0 / math.sqrt(128.0),
                           "q_bmaj", qre_view))
        emit_fm_matmul(ctx, ap, stack, "s10", dX, "xn1_fm", "qkv1L_fm", segs10)
        dX = dd(); free_stage(stack)
        emit_stage10kvT(ctx, ap, stack, dX)
        dX = dd(); free_stage(stack)
        mun_cm = nc.sbuf_tensor("mun_sb", [128, 16 * 2048], BF16)
        mun_sb = mun_cm.__enter__()
        emit_stage13a(ctx, ap, stack, dX, mun_sb)
        dX = dd(); free_stage(stack)
        emit_stage13b(ctx, ap, stack, dX, 0, mun_sb)
        dX = dd(); free_stage(stack)
        emit_stage13b(ctx, ap, stack, dX, 1, mun_sb)
        dX = dd(); free_stage(stack)
        mun_cm.__exit__(None, None, None)
        emit_stage11(ctx, ap, stack, dX + [("s10_cc", 1)])
        dX = dd(); free_stage(stack)

        def a1_view(out, tt):
            vv = out.rearrange("(s n) d -> s n d", s=2)
            return vv[:, 64 * tt:64 * (tt + 1), :].transpose([1, 0, 2])

        emit_tm_matmul(ctx, ap, stack, "s12", dX, "attn1_fm", "w_proj1T", "out_tm",
                       addins=("x1_tm",), out_view=a1_view)
        for b_ in range(2):
            dX = dd(); free_stage(stack)
            emit_stage13def(ctx, ap, stack, dX, b_)

        @block.vector
        def _(vector):
            for nm in GLOBAL_SEMS:
                if ctx.n[nm] > 0:
                    vector.wait_ge(ctx.h[nm], ctx.n[nm])

    free_stage(stack)
    for cm in reversed(ctx._cm):
        cm.__exit__(None, None, None)
    return nc


_NC_CACHE = None


def kernel(**inputs):
    global _NC_CACHE
    prep = host_prep(inputs)
    x = np.asarray(inputs["L0_attn_x"], np.float32)
    if _NC_CACHE is None:
        _NC_CACHE = build_module()
    nc = _NC_CACHE
    in_maps = []
    for cidx in range(8):
        m = {"x_tm": np.ascontiguousarray(x[2 * cidx:2 * cidx + 2].reshape(T, D))}
        m.update(prep)
        in_maps.append(m)
    res = run_bass_kernel_spmd(nc, in_maps, core_ids=list(range(8)))
    out = np.concatenate([res.results[cidx]["out_tm"].reshape(2, 2048, 2048)
                          for cidx in range(8)], axis=0)
    return out.astype(np.float32)
